# version 24
# baseline (speedup 1.0000x reference)
import contextlib
from collections import defaultdict

import numpy as np

import concourse.bass as bass
import concourse.mybir as mybir
from concourse.bass_utils import run_bass_kernel_spmd

F32 = mybir.dt.float32
BF16 = mybir.dt.bfloat16
AF = mybir.ActivationFunctionType
ALU = mybir.AluOpType
AX = mybir.AxisListType.X

B = 4
SEQ = 2048
D = 2048
T = 1024
NCH = 16
H = 16
FF = 5632
FCH = 44
LN_EPS = 1e-5
QEPS = 1e-8
LFULL = FF * D
L8 = LFULL // 8
GW = L8 // 128
GTS = 8
GTW = GW // GTS


class MK:
    def __init__(self, nc):
        self.nc = nc
        self.ops = {k: [] for k in ("sync", "scalar", "vector", "tensor", "gpsimd")}
        self.meta = {k: [] for k in ("sync", "scalar", "vector", "tensor", "gpsimd")}
        self.c = defaultdict(int)
        self.sems = {}
        self.es = contextlib.ExitStack()
        self.paw = 0
        self.pbw = 0
        self.pkaw = 0
        self.pkbw = 0

    def snapshot(self):
        return {k: len(v) for k, v in self.ops.items()}

    def truncate(self, snap):
        for k in self.ops:
            self.ops[k] = self.ops[k][:snap[k]]
            self.meta[k] = self.meta[k][:snap[k]]

    def sem(self, name):
        self.sems[name] = self.es.enter_context(self.nc.semaphore(name))

    def emit(self, eng, build, waits=(), incs=(), raw=False):
        w = tuple((s, v) for (s, v) in waits if v > 0)
        for s, a in incs:
            self.c[s] += a
        i = tuple(incs)

        def fn(e, _b=build, _w=w, _i=i, _raw=raw):
            for s, v in _w:
                e.wait_ge(self.sems[s], v)
            inst = _b(e)
            if _i:
                if _raw:
                    inst.then_inc(self.sems[_i[0][0]], _i[0][1])
                    rest = _i[1:]
                else:
                    e.drain().then_inc(self.sems[_i[0][0]], _i[0][1])
                    rest = _i[1:]
                for s, a in rest:
                    e.nop().then_inc(self.sems[s], a)
        self.ops[eng].append(fn)
        self.meta[eng].append((w, i, getattr(build, "__qualname__", "?")))

    def dma(self, eng, out, in_, sem, waits=()):
        self.emit(eng, lambda e, o=out, x=in_: e.dma_start(out=o, in_=x),
                  waits=waits, incs=((sem, 16),), raw=True)

    def mm(self, out, lhsT, rhs, start, stop, waits=(), incs=()):
        self.emit("tensor",
                  lambda e, o=out, l=lhsT, r=rhs, s0=start, s1=stop:
                      e.matmul(o, l, r, start=s0, stop=s1, skip_group_check=True),
                  waits=waits, incs=incs)


def build_program(phases=99, debug=False):
    nc = bass.Bass("TRN2", target_bir_lowering=False, detect_race_conditions=False)
    m = MK(nc)
    es = m.es

    x_t = nc.dram_tensor("x_t", [D, T], F32, kind="ExternalInput").ap()
    wqk_d = nc.dram_tensor("wqk", [NCH, 128, D], F32, kind="ExternalInput").ap()
    wv_d = nc.dram_tensor("wv", [NCH, 128, D], F32, kind="ExternalInput").ap()
    wu_d = nc.dram_tensor("wu", [NCH, 128, D], F32, kind="ExternalInput").ap()
    wo_d = nc.dram_tensor("wo", [NCH, 128, D], F32, kind="ExternalInput").ap()
    bqk_d = nc.dram_tensor("bqk", [128, NCH], F32, kind="ExternalInput").ap()
    bv_d = nc.dram_tensor("bv", [128, NCH], F32, kind="ExternalInput").ap()
    bu_d = nc.dram_tensor("bu", [128, NCH], F32, kind="ExternalInput").ap()
    bo_d = nc.dram_tensor("bo", [128, NCH], F32, kind="ExternalInput").ap()
    lng_d = nc.dram_tensor("lng", [128, NCH], F32, kind="ExternalInput").ap()
    lnb_d = nc.dram_tensor("lnb", [128, NCH], F32, kind="ExternalInput").ap()
    w1t_d = nc.dram_tensor("w1t", [FCH, 128, D], F32, kind="ExternalInput").ap()
    w2t_d = nc.dram_tensor("w2t", [FCH, 128, D], F32, kind="ExternalInput").ap()
    w3t_d = nc.dram_tensor("w3t", [NCH, 128, FF], F32, kind="ExternalInput").ap()
    gs1_d = nc.dram_tensor("gs1", [128, GW], F32, kind="ExternalInput").ap()
    gs2_d = nc.dram_tensor("gs2", [128, GW], F32, kind="ExternalInput").ap()
    gs3_d = nc.dram_tensor("gs3", [128, GW], F32, kind="ExternalInput").ap()
    out_t = nc.dram_tensor("out_t", [D, T], F32, kind="ExternalOutput").ap()

    gcc_in = nc.dram_tensor("gcc_in", [128, 24], F32, kind="Internal").ap()
    gcc_out = nc.dram_tensor("gcc_out", [128, 24], F32, kind="Internal").ap()
    kv_in = nc.dram_tensor("kv_in", [128, D], F32, kind="Internal").ap()
    kv_out = nc.dram_tensor("kv_out", [128, D], F32, kind="Internal").ap()
    x2_d = nc.dram_tensor("x2_d", [D, T], F32, kind="Internal").ap()
    o3_d = nc.dram_tensor("o3_d", [D, T], BF16, kind="Internal").ap()

    for s in ("x0", "x1", "x2", "x3", "wa0", "wa1", "wa2", "wa3", "wf0", "wf1", "wf2",
              "w3", "g0", "g1", "gb", "sp", "rlx0", "rlx1", "rla0", "rla1",
              "rlk", "cc", "pe", "peW", "act", "dve", "pl", "st",
              "fa", "fb", "fka", "fkb", "dbg"):
        m.sem(s)
    if debug:
        dbg_kvp = nc.dram_tensor("dbg_kvp", [128, D], F32, kind="ExternalOutput").ap()
        dbg_kvf = nc.dram_tensor("dbg_kvf", [128, D], F32, kind="ExternalOutput").ap()
        dbg_thr = nc.dram_tensor("dbg_thr", [128, 4], F32, kind="ExternalOutput").ap()
        dbg_gc = nc.dram_tensor("dbg_gc", [128, 24], F32, kind="ExternalOutput").ap()
        dbg_qfm = nc.dram_tensor("dbg_qfm", [128, 16384], BF16, kind="ExternalOutput").ap()
        dbg_xln = nc.dram_tensor("dbg_xln", [128, 16384], BF16, kind="ExternalOutput").ap()
        dbg_trow = nc.dram_tensor("dbg_trow", [1, 32], F32, kind="ExternalOutput").ap()
        dbg_ao = nc.dram_tensor("dbg_ao", [128, 16384], BF16, kind="ExternalOutput").ap()

    L = dict(side="left")
    ident = es.enter_context(nc.sbuf_tensor("ident", [128, 128], BF16, **L))
    onest = es.enter_context(nc.sbuf_tensor("onest", [128, 128], BF16, **L))
    colones = es.enter_context(nc.sbuf_tensor("colones", [128, 128], BF16, **L))
    invd = es.enter_context(nc.sbuf_tensor("invd", [128, 128], BF16, **L))
    ones128b = es.enter_context(nc.sbuf_tensor("ones128b", [1, 128], BF16, **L))
    m_rb = es.enter_context(nc.sbuf_tensor("m_rb", [1, T], BF16, **L))
    r_rb = es.enter_context(nc.sbuf_tensor("r_rb", [1, T], BF16, **L))
    ones128f = es.enter_context(nc.sbuf_tensor("ones128f", [1, 128], F32, **L))
    onescol_b = es.enter_context(nc.sbuf_tensor("onescol_b", [128, 1], BF16, **L))
    onescol_f = es.enter_context(nc.sbuf_tensor("onescol_f", [128, 1], F32, **L))
    eps_ap = es.enter_context(nc.sbuf_tensor("eps_ap", [1, 1], F32, **L))
    zcol = es.enter_context(nc.sbuf_tensor("zcol", [128, 1], F32, **L))
    bqk_c = es.enter_context(nc.sbuf_tensor("bqk_c", [128, NCH], F32, **L))
    bv_c = es.enter_context(nc.sbuf_tensor("bv_c", [128, NCH], F32, **L))
    bu_c = es.enter_context(nc.sbuf_tensor("bu_c", [128, NCH], F32, **L))
    bo_c = es.enter_context(nc.sbuf_tensor("bo_c", [128, NCH], F32, **L))
    g_c = es.enter_context(nc.sbuf_tensor("g_c", [128, NCH], F32, **L))
    bln_c = es.enter_context(nc.sbuf_tensor("bln_c", [128, NCH], F32, **L))
    thrc = es.enter_context(nc.sbuf_tensor("thrc", [128, 4], F32, **L))
    nthrc = es.enter_context(nc.sbuf_tensor("nthrc", [128, 4], F32, **L))
    gcols = es.enter_context(nc.sbuf_tensor("gcols", [128, 24], F32, **L))
    m_r = es.enter_context(nc.sbuf_tensor("m_r", [1, T], F32, **L))
    r_r = es.enter_context(nc.sbuf_tensor("r_r", [1, T], F32, **L))
    trow = es.enter_context(nc.sbuf_tensor("trow", [1, 32], F32, **L))
    mbb = es.enter_context(nc.sbuf_tensor("mbb", [128, T], BF16, **L))
    rbb = es.enter_context(nc.sbuf_tensor("rbb", [128, T], BF16, **L))
    arb2 = [es.enter_context(nc.sbuf_tensor(f"arb2{i}", [128, T], BF16, **L))
            for i in range(2)]
    x2r = [es.enter_context(nc.sbuf_tensor(f"x2r{i}", [128, T], F32, **L))
           for i in range(2)]
    scr0 = es.enter_context(nc.sbuf_tensor("scr0", [128, T], F32, **L))
    scr1 = es.enter_context(nc.sbuf_tensor("scr1", [128, T], F32, **L))
    sqb0 = es.enter_context(nc.sbuf_tensor("sqb0", [128, T], BF16, **L))
    sqb1 = es.enter_context(nc.sbuf_tensor("sqb1", [128, T], BF16, **L))
    sqb = [sqb0, sqb1]
    t1_r = scr0[0:1, :]
    t2_r = scr1[0:1, :]
    s0b = scr0[:].bitcast(BF16)[:, 0:T]
    s1b = scr1[:].bitcast(BF16)[:, 0:T]

    es_g = contextlib.ExitStack()
    gbuf = [es_g.enter_context(nc.sbuf_tensor(f"gbuf{i}", [128, GTW], F32, **L))
            for i in range(2)]

    R = dict(side="right")
    es_a = contextlib.ExitStack()
    xin = [es_a.enter_context(nc.sbuf_tensor(f"xin{i}", [128, T], F32, **R))
           for i in range(2)]
    es_u = contextlib.ExitStack()
    u_fm = es_u.enter_context(nc.sbuf_tensor("u_fm", [128, NCH * T], BF16, **R))
    es_a2 = contextlib.ExitStack()
    kv_f = es_a2.enter_context(nc.sbuf_tensor("kv_f", [128, D], F32, **R))
    kv_b = es_a2.enter_context(nc.sbuf_tensor("kv_b", [128, D], BF16, **R))
    wa = [es_a2.enter_context(nc.sbuf_tensor(f"wa{i}", [128, D], BF16, **R))
          for i in range(4)]
    scr2 = es_a2.enter_context(nc.sbuf_tensor("scr2", [128, T], F32, **R))
    scr3 = es_a2.enter_context(nc.sbuf_tensor("scr3", [128, T], F32, **R))
    q_fm = es_a2.enter_context(nc.sbuf_tensor("q_fm", [128, NCH * T], BF16, **R))
    es_xln = contextlib.ExitStack()
    xln = es_xln.enter_context(nc.sbuf_tensor("xln", [128, NCH * T], BF16, **R))
    vpan = es_xln.enter_context(nc.sbuf_tensor("vpan", [128, T], BF16, **R))
    qtok = es_xln.enter_context(nc.sbuf_tensor("qtok", [128, T], BF16, **R))
    vtok = es_xln.enter_context(nc.sbuf_tensor("vtok", [128, T], BF16, **R))

    pk = es.enter_context(nc.psum_tensor("pk", [128, 2048], F32))
    pa = es.enter_context(nc.psum_tensor("pa", [128, 1024], F32))
    pb = es.enter_context(nc.psum_tensor("pb", [128, 1024], F32))

    xln_ch = lambda c: xln[:, c * T:(c + 1) * T]
    qfm_ch = lambda c: q_fm[:, c * T:(c + 1) * T]
    ufm_ch = lambda c: u_fm[:, c * T:(c + 1) * T]

    m.emit("vector", lambda e: e.memset(onest[:], 1.0))
    m.emit("vector", lambda e: e.memset(invd[:], 1.0 / D))
    m.emit("vector", lambda e: e.memset(ones128f[:], 1.0))
    m.emit("vector", lambda e: e.memset(ones128b[:], 1.0))
    m.emit("vector", lambda e: e.memset(onescol_b[:], 1.0))
    m.emit("vector", lambda e: e.memset(onescol_f[:], 1.0))
    m.emit("vector", lambda e: e.memset(zcol[:], 0.0))
    m.emit("vector", lambda e: e.memset(eps_ap[:], LN_EPS), incs=(("dve", 1),))
    consts_dve = m.c["dve"]
    m.emit("gpsimd",
           lambda e: e.affine_select(ident[:], onest[:], [[-1, 128]],
                                     ALU.is_equal, 0.0, base=0,
                                     channel_multiplier=1),
           waits=(("dve", consts_dve),))
    m.emit("gpsimd",
           lambda e: e.affine_select(colones[:], invd[:], [[1, 128]],
                                     ALU.is_equal, 0.0, base=0,
                                     channel_multiplier=0),
           incs=(("pl", 1),))
    ident_pl = m.c["pl"]
    for dst, srcd in ((bqk_c, bqk_d), (bv_c, bv_d), (bu_c, bu_d), (bo_c, bo_d),
                     (g_c, lng_d), (bln_c, lnb_d)):
        m.dma("scalar", dst[:], srcd, "gb")
    bias_gb = m.c["gb"]
    ck = {}
    ck[0] = m.snapshot()

    gsrc = (gs1_d, gs2_d, gs3_d)
    gred_dve = []

    def emit_gamma_pair(idx):
        mtx, k = divmod(idx, GTS)
        gsem = f"g{idx % 2}"
        srcap = gsrc[mtx][:, k * GTW:(k + 1) * GTW]
        waits = [("dve", gred_dve[idx - 2])] if idx >= 2 else []
        m.dma("scalar", gbuf[idx % 2][:], srcap, gsem, waits=waits)
        m.emit("vector",
               lambda e, i=idx: e.tensor_reduce(
                   gcols[:, i:i + 1], gbuf[i % 2][:], AX, ALU.add,
                   apply_absolute_value=True),
               waits=((gsem, 16 * (idx // 2 + 1)),),
               incs=(("dve", 1),))
        gred_dve.append(m.c["dve"])

    def wa_panel_src(idx):
        if idx < 32:
            h, which = divmod(idx, 2)
            return (wqk_d if which == 0 else wv_d)[h, :, :]
        if idx < 48:
            return wu_d[idx - 32, :, :]
        return wo_d[idx - 48, :, :]

    def emit_wa(idx, extra_waits=()):
        w = list(extra_waits)
        if idx >= 4:
            w.append(("peW", idx - 3))
        m.emit("gpsimd",
               lambda e, i=idx: e.dma_start(out=wa[i % 4][:],
                                            in_=wa_panel_src(i)),
               waits=w, incs=((f"wa{idx % 4}", 16),), raw=True)

    def wa_wait(idx):
        return (f"wa{idx % 4}", 16 * (idx // 4 + 1))

    sq_act = []
    sq_pe = []
    for c in range(NCH):
        w = [("act", sq_act[c - 4])] if c >= 4 else []
        if c == 0:
            w.append(("pl", ident_pl))
        m.emit("gpsimd",
               lambda e, c=c: e.dma_start(out=xln_ch(c),
                                          in_=x_t[c * 128:(c + 1) * 128, :]),
               waits=w, incs=((f"x{c % 4}", 16),), raw=True)
        aw = [(f"x{c % 4}", 16 * (c // 4 + 1))]
        if c >= 2:
            aw.append(("pe", sq_pe[c - 2]))
        if c % 2 == 1 and c < 9:
            emit_wa((c - 1) // 2)
        m.emit("scalar",
               lambda e, c=c: e.activation(sqb[c % 2][:], xln_ch(c), AF.Square),
               waits=aw, incs=(("act", 1),))
        sq_act.append(m.c["act"])
        stw = [("act", sq_act[c])]
        for half in range(2):
            m.mm(pk[:, half * 512:(half + 1) * 512], colones[:],
                 xln[:, c * T + half * 512: c * T + (half + 1) * 512],
                 start=(c == 0), stop=(c == NCH - 1),
                 waits=stw if half == 0 else ())
            m.mm(pk[:, 1024 + half * 512:1024 + (half + 1) * 512],
                 colones[:], sqb[c % 2][:, half * 512:(half + 1) * 512],
                 start=(c == 0), stop=(c == NCH - 1))
        m.emit("tensor", lambda e: e.nop(), incs=(("pe", 1),))
        sq_pe.append(m.c["pe"])
    stats0_pe = m.c["pe"]
    ck[8] = m.snapshot()

    def ln_rows(wait_pe):
        m.emit("vector", lambda e: e.tensor_copy(m_r[:], pk[0:1, 0:1024]),
               waits=(("pe", wait_pe),))
        m.emit("vector", lambda e: e.tensor_copy(t2_r, pk[0:1, 1024:2048]))
        m.emit("vector", lambda e: e.drain())
        m.emit("vector", lambda e: e.tensor_mul(t1_r, m_r[:], m_r[:]))
        m.emit("vector", lambda e: e.drain())
        m.emit("vector",
               lambda e: e.tensor_tensor(t2_r, t2_r, t1_r, ALU.subtract),
               incs=(("dve", 1),))
        rows_dve = m.c["dve"]
        m.emit("scalar",
               lambda e: e.activation(t1_r, t2_r, AF.Ln, bias=eps_ap[:]),
               waits=(("dve", rows_dve),))
        m.emit("scalar", lambda e: e.drain())
        m.emit("scalar",
               lambda e: e.activation(r_r[:], t1_r, AF.Exp, scale=-0.5),
               incs=(("act", 1),))
        rows_act2 = m.c["act"]
        m.emit("vector", lambda e: e.tensor_copy(m_rb[:], m_r[:]),
               waits=(("act", rows_act2),))
        m.emit("vector", lambda e: e.tensor_copy(r_rb[:], r_r[:]),
               incs=(("dve", 1),))
        return m.c["dve"]

    def ln_bcast(wait_dve):
        w = [("dve", wait_dve), ("fa", m.paw)]
        for half in range(2):
            m.mm(pa[:, half * 512:(half + 1) * 512], ones128b[:],
                 m_rb[0:1, half * 512:(half + 1) * 512], True, True,
                 waits=w if half == 0 else ())
        m.emit("tensor", lambda e: e.nop(), incs=(("pe", 1),))
        bpe_a = m.c["pe"]
        for half in range(2):
            m.mm(pb[:, half * 512:(half + 1) * 512], ones128b[:],
                 r_rb[0:1, half * 512:(half + 1) * 512], True, True,
                 waits=[("fb", m.pbw)] if half == 0 else ())
        m.emit("tensor", lambda e: e.nop(), incs=(("pe", 1),))
        bpe_b = m.c["pe"]
        m.emit("vector", lambda e: e.tensor_copy(mbb[:], pa[:]),
               waits=(("pe", bpe_a),), incs=(("fa", 1),))
        m.paw += 1
        m.emit("vector", lambda e: e.tensor_copy(rbb[:], pb[:]),
               waits=(("pe", bpe_b),), incs=(("fb", 1), ("dve", 1)))
        m.pbw += 1
        return m.c["dve"]

    rows1 = ln_rows(stats0_pe)
    ck[9] = m.snapshot()
    bc1 = ln_bcast(rows1)
    ck[10] = m.snapshot()
    ln1_dve = []
    for c in range(NCH):
        m.emit("vector",
               lambda e, c=c: e.tensor_tensor(s0b, xln_ch(c), mbb[:],
                                              ALU.subtract))
        m.emit("vector",
               lambda e, c=c: e.tensor_tensor(xln_ch(c), s0b, rbb[:],
                                              ALU.mult),
               incs=(("dve", 1),))
        ln1_dve.append(m.c["dve"])
    ln1_done = ln1_dve[-1]
    if debug:
        m.dma("sync", dbg_xln, xln[:], "dbg", waits=(("dve", ln1_done),))
    ck[1] = m.snapshot()


    for idx in range(4, 32):
        emit_wa(idx)

    pairs = [pa, pb, pk[:, 1024:2048]]
    pfree = ["fa", "fb", "fkb"]
    pattr = ["paw", "pbw", "pkbw"]
    pget = lambda pi: getattr(m, pattr[pi])
    def pinc(pi):
        setattr(m, pattr[pi], pget(pi) + 1)
    kvacc = lambda h: pk[:, (h % 2) * 512:(h % 2) * 512 + 128]

    elu_dve = [None] * H
    xb_acts = [None] * H
    silu_acts = [None] * H
    qt_pe = [None] * H
    vt_pe = [None] * H
    qtok_dve = [None] * H
    vtok_dve = [None] * H
    kv_pe = [None] * H
    kvev_dve = [None] * H

    def emit_qkproj(h):
        pi = (2 * h) % 3
        tgt = pairs[pi]
        w = [wa_wait(2 * h), (pfree[pi], pget(pi))]
        if h == 0:
            w += [("gb", bias_gb)]
        for cd in range(NCH):
            for half in range(2):
                cw = list(w) if (cd == 0 and half == 0) else []
                if h == 0 and half == 0:
                    cw.append(("dve", ln1_dve[cd]))
                m.mm(tgt[:, half * 512:(half + 1) * 512],
                     wa[(2 * h) % 4][:, cd * 128:(cd + 1) * 128],
                     xln[:, cd * T + half * 512: cd * T + (half + 1) * 512],
                     start=(cd == 0), stop=(cd == NCH - 1),
                     waits=cw)
        m.emit("tensor", lambda e: e.nop(), incs=(("pe", 1), ("peW", 1)))
        qk_pe = m.c["pe"]
        sA = scr0 if h % 2 == 0 else scr2
        sB = scr1 if h % 2 == 0 else scr3
        aw = [("pe", qk_pe)]
        if h > 1:
            aw.append(("dve", elu_dve[h - 2]))
        m.emit("scalar",
               lambda e, h=h, sA=sA, tg=tgt: e.activation(
                   sA[:], tg[:], AF.Exp, bias=bqk_c[:, h:h + 1]),
               waits=aw)
        m.emit("scalar",
               lambda e, h=h, sB=sB, tg=tgt: e.activation(
                   sB[:], tg[:], AF.Identity, bias=bqk_c[:, h:h + 1]),
               incs=(("act", 1), (pfree[pi], 1)))
        pinc(pi)
        xb_acts[h] = m.c["act"]

    def emit_elu_dve(h):
        sA = scr0 if h % 2 == 0 else scr2
        sB = scr1 if h % 2 == 0 else scr3
        m.emit("vector",
               lambda e, sA=sA: e.tensor_scalar(sA[:], sA[:], 1.0, -1.0,
                                                ALU.min, ALU.add),
               waits=(("act", xb_acts[h]),))
        m.emit("vector",
               lambda e, h=h, sA=sA, sB=sB: e.scalar_tensor_tensor(
                   qfm_ch(h), sB[:], 0.0, sA[:], ALU.max, ALU.add),
               incs=(("dve", 1),))
        elu_dve[h] = m.c["dve"]

    def emit_vproj(h):
        pi = (2 * h + 1) % 3
        tgt = pairs[pi]
        w = [wa_wait(2 * h + 1), (pfree[pi], pget(pi))]
        for cd in range(NCH):
            for half in range(2):
                m.mm(tgt[:, half * 512:(half + 1) * 512],
                     wa[(2 * h + 1) % 4][:, cd * 128:(cd + 1) * 128],
                     xln[:, cd * T + half * 512: cd * T + (half + 1) * 512],
                     start=(cd == 0), stop=(cd == NCH - 1),
                     waits=w if (cd == 0 and half == 0) else ())
        m.emit("tensor", lambda e: e.nop(), incs=(("pe", 1), ("peW", 1)))
        v_pe = m.c["pe"]
        m.emit("scalar",
               lambda e, h=h, tg=tgt: e.activation(vpan[:], tg[:], AF.Silu,
                                                   bias=bv_c[:, h:h + 1]),
               waits=(("pe", v_pe),), incs=(("act", 1), (pfree[pi], 1)))
        pinc(pi)
        silu_acts[h] = m.c["act"]

    def emit_qT(h):
        pi = (2 * h) % 3
        tgt = pairs[pi]
        w = [("dve", elu_dve[h]), (pfree[pi], pget(pi))]
        for s in range(8):
            m.mm(tgt[:, s * 128:(s + 1) * 128],
                 q_fm[:, h * T + s * 128: h * T + (s + 1) * 128],
                 ident[:], True, True, waits=w if s == 0 else ())
        m.emit("tensor", lambda e: e.nop(), incs=(("pe", 1),))
        qt_pe[h] = m.c["pe"]
        aw = [("pe", qt_pe[h])]
        if h > 0:
            aw.append(("pe", kv_pe[h - 1]))
        m.emit("vector", lambda e, tg=tgt: e.tensor_copy(qtok[:], tg[:]),
               waits=aw, incs=(("dve", 1), (pfree[pi], 1)))
        pinc(pi)
        qtok_dve[h] = m.c["dve"]

    def emit_vT(h):
        pi = (2 * h + 1) % 3
        tgt = pairs[pi]
        w = [("act", silu_acts[h]), (pfree[pi], pget(pi))]
        for s in range(8):
            m.mm(tgt[:, s * 128:(s + 1) * 128],
                 vpan[:, s * 128:(s + 1) * 128],
                 ident[:], True, True, waits=w if s == 0 else ())
        m.emit("tensor", lambda e: e.nop(), incs=(("pe", 1),))
        vt_pe[h] = m.c["pe"]
        aw = [("pe", vt_pe[h])]
        if h > 0:
            aw.append(("pe", kv_pe[h - 1]))
        m.emit("vector", lambda e, tg=tgt: e.tensor_copy(vtok[:], tg[:]),
               waits=aw, incs=(("dve", 1), (pfree[pi], 1)))
        pinc(pi)
        vtok_dve[h] = m.c["dve"]

    def emit_kv(h):
        w = [("dve", vtok_dve[h])]
        if h >= 2:
            w.append(("dve", kvev_dve[h - 2]))
        for s in range(8):
            m.mm(kvacc(h),
                 qtok[:, s * 128:(s + 1) * 128],
                 vtok[:, s * 128:(s + 1) * 128],
                 start=(s == 0), stop=(s == 7),
                 waits=w if s == 0 else ())
        m.emit("tensor", lambda e: e.nop(), incs=(("pe", 1),))
        kv_pe[h] = m.c["pe"]
        m.emit("vector",
               lambda e, h=h: e.tensor_copy(kv_f[:, h * 128:(h + 1) * 128],
                                            kvacc(h)),
               waits=(("pe", kv_pe[h]),), incs=(("dve", 1),))
        kvev_dve[h] = m.c["dve"]

    for h in range(H):
        if h > 0:
            emit_qT(h - 1)
        emit_qkproj(h)
        if h > 0:
            emit_vT(h - 1)
        emit_elu_dve(h)
        emit_vproj(h)
        if h > 0:
            emit_kv(h - 1)
        if h < 12:
            emit_gamma_pair(2 * h)
            emit_gamma_pair(2 * h + 1)
    emit_qT(H - 1)
    emit_vT(H - 1)
    emit_kv(H - 1)

    ck[2] = m.snapshot()
    m.dma("scalar", gcc_in, gcols[:], "sp", waits=(("dve", gred_dve[-1]),))
    kvevac_dve = kvev_dve[H - 1]
    m.dma("scalar", kv_in, kv_f[:], "sp", waits=(("dve", kvevac_dve),))
    kvput = m.c["sp"]
    if debug:
        m.dma("sync", dbg_kvp, kv_f[:], "dbg", waits=(("dve", kvevac_dve),))
    m.emit("gpsimd",
           lambda e: e.collective_compute(
               "AllReduce", ALU.add,
               replica_groups=[[0, 1, 2, 3, 4, 5, 6, 7]],
               ins=[gcc_in.opt()], outs=[gcc_out.opt()]),
           waits=(("sp", kvput),), incs=(("cc", 1),), raw=True)
    m.emit("gpsimd",
           lambda e: e.collective_compute(
               "AllReduce", ALU.add,
               replica_groups=[[0, 1], [2, 3], [4, 5], [6, 7]],
               ins=[kv_in.opt()], outs=[kv_out.opt()]),
           waits=(("sp", kvput),), incs=(("cc", 1),), raw=True)
    for idx in range(32, 48):
        emit_wa(idx)
    u_act = []
    for p in range(NCH):
        pi = p % 3
        tgt, fs, fw = pairs[pi], pfree[pi], pget(pi)
        w = [wa_wait(32 + p), (fs, fw)]
        for cd in range(NCH):
            for half in range(2):
                m.mm(tgt[:, half * 512:(half + 1) * 512],
                     wa[(32 + p) % 4][:, cd * 128:(cd + 1) * 128],
                     xln[:, cd * T + half * 512: cd * T + (half + 1) * 512],
                     start=(cd == 0), stop=(cd == NCH - 1),
                     waits=w if (cd == 0 and half == 0) else ())
        m.emit("tensor", lambda e: e.nop(), incs=(("pe", 1), ("peW", 1)))
        up_pe = m.c["pe"]
        m.emit("scalar",
               lambda e, p=p, tg=tgt: e.activation(ufm_ch(p), tg[:], AF.Silu,
                                                   bias=bu_c[:, p:p + 1]),
               waits=(("pe", up_pe),), incs=(("act", 1), (fs, 1)))
        pinc(pi)
        u_act.append(m.c["act"])

    m.dma("scalar", kv_f[:], kv_out, "rlk", waits=(("cc", 2),))
    m.dma("scalar", gcols[:], gcc_out, "rlk", waits=(("cc", 2),))
    rlk_done = m.c["rlk"]
    m.emit("vector", lambda e: e.tensor_copy(kv_b[:], kv_f[:]),
           waits=(("rlk", rlk_done),), incs=(("dve", 1),))
    kvb_dve = m.c["dve"]
    if debug:
        m.dma("sync", dbg_kvf, kv_f[:], "dbg", waits=(("rlk", rlk_done),))

    m.mm(pk[0:1, 0:24], onescol_f[:], gcols[:], True, True,
         waits=(("rlk", rlk_done), ("dve", kvevac_dve)))
    m.emit("tensor", lambda e: e.nop(), incs=(("pe", 1),))
    gsum_pe = m.c["pe"]
    m.emit("vector", lambda e: e.tensor_copy(trow[0:1, 0:24], pk[0:1, 0:24]),
           waits=(("pe", gsum_pe),))
    m.emit("vector", lambda e: e.drain())
    for mtx in range(3):
        m.emit("vector",
               lambda e, j=mtx: e.reduce_sum(trow[0:1, 24 + j:25 + j],
                                             trow[0:1, j * 8:(j + 1) * 8], AX))
        m.emit("vector", lambda e: e.drain())
        m.emit("vector",
               lambda e, j=mtx: e.tensor_scalar(
                   trow[0:1, 28 + j:29 + j], trow[0:1, 24 + j:25 + j],
                   0.5 / LFULL, 0.5 * QEPS, ALU.mult, ALU.add))
        m.emit("vector", lambda e: e.drain())
    m.emit("vector", lambda e: e.memset(trow[0:1, 31:32], 0.0),
           incs=(("dve", 1),))
    thr_dve = m.c["dve"]
    m.mm(pk[:, 0:4], ones128f[:], trow[0:1, 28:32], True, True,
         waits=(("dve", thr_dve),))
    m.emit("tensor", lambda e: e.nop(), incs=(("pe", 1),))
    thrb_pe = m.c["pe"]
    m.emit("vector", lambda e: e.tensor_copy(thrc[:], pk[:, 0:4]),
           waits=(("pe", thrb_pe),))
    m.emit("vector", lambda e: e.drain())
    m.emit("vector", lambda e: e.tensor_scalar_mul(nthrc[:], thrc[:], -1.0),
           incs=(("dve", 1),))
    thr_ready = m.c["dve"]
    if debug:
        m.dma("sync", dbg_thr, thrc[:], "dbg", waits=(("dve", thr_ready),))
        m.dma("sync", dbg_gc, gcols[:], "dbg", waits=(("dve", thr_ready),))
        m.dma("sync", dbg_trow, trow[:], "dbg", waits=(("dve", thr_ready),))

    ck[3] = m.snapshot()
    for idx in range(48, 64):
        emit_wa(idx)
    es_xln.close()
    ao = es_a2.enter_context(nc.sbuf_tensor("ao", [128, NCH * T], BF16, **R))
    ao_ch = lambda c: ao[:, c * T:(c + 1) * T]

    if debug:
        m.dma("sync", dbg_qfm, q_fm[:], "dbg", waits=(("dve", kvb_dve),))
    sq2_pe = {}
    for h in range(H):
        if h % 2 == 0:
            tgt, fs, fw = pa, "fa", m.paw
        else:
            tgt, fs, fw = pb, "fb", m.pbw
        w = [("dve", kvb_dve), (fs, fw)]
        for half in range(2):
            m.mm(tgt[:, half * 512:(half + 1) * 512],
                 kv_b[:, h * 128:(h + 1) * 128],
                 q_fm[:, h * T + half * 512: h * T + (half + 1) * 512],
                 True, True, waits=w if half == 0 else ())
        m.emit("tensor", lambda e: e.nop(), incs=(("pe", 1),))
        e_pe = m.c["pe"]
        m.emit("vector",
               lambda e, h=h, tg=tgt: e.tensor_copy(ao_ch(h), tg[:]),
               waits=(("pe", e_pe),), incs=((fs, 1),))
        if h % 2 == 0:
            m.paw += 1
        else:
            m.pbw += 1
        m.emit("vector",
               lambda e, h=h: e.tensor_mul(sqb[h % 2][:], ao_ch(h), ao_ch(h)),
               waits=[("pe", sq2_pe[h - 2])] if h >= 2 else (),
               incs=(("dve", 1),))
        ein_dve = m.c["dve"]
        stw = [("dve", ein_dve)]
        if h == 0:
            stw.append(("dve", thr_ready))
            stw.append(("fkb", m.pkbw))
        for half in range(2):
            m.mm(pk[:, half * 512:(half + 1) * 512], colones[:],
                 ao[:, h * T + half * 512: h * T + (half + 1) * 512],
                 start=(h == 0), stop=(h == H - 1),
                 waits=stw if half == 0 else ())
            m.mm(pk[:, 1024 + half * 512:1024 + (half + 1) * 512],
                 colones[:], sqb[h % 2][:, half * 512:(half + 1) * 512],
                 start=(h == 0), stop=(h == H - 1))
        m.emit("tensor", lambda e: e.nop(), incs=(("pe", 1),))
        sq2_pe[h] = m.c["pe"]
    stats2_pe = m.c["pe"]

    if debug:
        m.dma("sync", dbg_ao, ao[:], "dbg", waits=(("pe", stats2_pe),))
    rows2 = ln_rows(stats2_pe)
    bc2 = ln_bcast(rows2)

    z_dve = []
    for c in range(NCH):
        m.emit("vector",
               lambda e, c=c: e.tensor_tensor(s0b, ao_ch(c), mbb[:],
                                              ALU.subtract),
               waits=(("act", u_act[c]),))
        m.emit("vector",
               lambda e, c=c: e.tensor_tensor(s1b, s0b, rbb[:], ALU.mult))
        m.emit("vector",
               lambda e, c=c: e.tensor_scalar(s1b, s1b, g_c[:, c:c + 1],
                                              bln_c[:, c:c + 1], ALU.mult,
                                              ALU.add))
        m.emit("vector",
               lambda e, c=c: e.tensor_tensor(ao_ch(c), s1b, ufm_ch(c),
                                              ALU.mult),
               incs=(("dve", 1),))
        z_dve.append(m.c["dve"])

    wo_act = []
    wo_sq_pe = {}
    for p in range(NCH):
        if p % 2 == 0:
            tgt, fs, fw = pa, "fa", m.paw
        else:
            tgt, fs, fw = pb, "fb", m.pbw
        w0 = [wa_wait(48 + p), (fs, fw)]
        for cd in range(NCH):
            cw = list(w0) if cd == 0 else []
            if p == 0:
                cw.append(("dve", z_dve[cd]))
            elif p == 1 and cd == 0:
                cw.append(("dve", z_dve[-1]))
            for half in range(2):
                m.mm(tgt[:, half * 512:(half + 1) * 512],
                     wa[(48 + p) % 4][:, cd * 128:(cd + 1) * 128],
                     ao[:, cd * T + half * 512: cd * T + (half + 1) * 512],
                     start=(cd == 0), stop=(cd == NCH - 1),
                     waits=cw if half == 0 else ())
        m.emit("tensor", lambda e: e.nop(), incs=(("pe", 1), ("peW", 1)))
        wo_pe = m.c["pe"]
        m.emit("vector",
               lambda e, p=p, tg=tgt: e.tensor_scalar(
                   ufm_ch(p), tg[:], bo_c[:, p:p + 1], None, ALU.add),
               waits=(("pe", wo_pe),), incs=((fs, 1),))
        if p % 2 == 0:
            m.paw += 1
        else:
            m.pbw += 1
        m.emit("vector",
               lambda e, p=p: e.tensor_mul(sqb[p % 2][:], ufm_ch(p),
                                           ufm_ch(p)),
               waits=[("pe", wo_sq_pe[p - 2])] if p >= 2 else (),
               incs=(("dve", 1),))
        wo_act.append(m.c["dve"])
        stw = [("dve", wo_act[p])]
        for half in range(2):
            m.mm(pk[:, half * 512:(half + 1) * 512], colones[:],
                 u_fm[:, p * T + half * 512: p * T + (half + 1) * 512],
                 start=(p == 0), stop=(p == NCH - 1),
                 waits=stw if half == 0 else ())
            m.mm(pk[:, 1024 + half * 512:1024 + (half + 1) * 512],
                 colones[:], sqb[p % 2][:, half * 512:(half + 1) * 512],
                 start=(p == 0), stop=(p == NCH - 1))
        m.emit("tensor", lambda e: e.nop(), incs=(("pe", 1),))
        wo_sq_pe[p] = m.c["pe"]
    stats3_pe = m.c["pe"]

    rows3 = ln_rows(stats3_pe)
    bc3 = ln_bcast(rows3)

    ck[4] = m.snapshot()
    es_a2.close()
    es_g.close()
    es_xln3 = contextlib.ExitStack()
    xln3 = es_xln3.enter_context(nc.sbuf_tensor("xln3", [128, NCH * T], BF16,
                                                **L))
    xln3_ch = lambda c: xln3[:, c * T:(c + 1) * T]

    x2_sp = []
    c2_dve = []
    sq3_pe = []
    cast_acts = []
    for i in range(2):
        m.dma("gpsimd", xin[i][:], x_t[i * 128:(i + 1) * 128, :], f"rlx{i}")
    for c in range(NCH):
        if c >= 2:
            m.dma("gpsimd", xin[c % 2][:], x_t[c * 128:(c + 1) * 128, :],
                  f"rlx{c % 2}", waits=(("dve", c2_dve[c - 2]),))
        w = [(f"rlx{c % 2}", m.c[f"rlx{c % 2}"])]
        if c >= 2:
            w.append(("sp", x2_sp[c - 2]))
            w.append(("act", cast_acts[c - 2]))
        m.emit("vector",
               lambda e, c=c: e.tensor_tensor(s0b, ufm_ch(c), mbb[:],
                                              ALU.subtract), waits=w)
        m.emit("vector",
               lambda e, c=c: e.tensor_tensor(s0b, s0b, rbb[:], ALU.mult))
        m.emit("vector",
               lambda e, c=c: e.tensor_tensor(x2r[c % 2][:], xin[c % 2][:],
                                              s0b, ALU.add))
        m.emit("vector", lambda e: e.engine_nop(), incs=(("dve", 1),))
        c2_dve.append(m.c["dve"])
        m.emit("scalar",
               lambda e, c=c: e.copy(xln3_ch(c), x2r[c % 2][:]),
               waits=(("dve", c2_dve[c]),), incs=(("act", 1),))
        cast_acts.append(m.c["act"])
        m.dma("scalar", x2_d[c * 128:(c + 1) * 128, :], x2r[c % 2][:], "sp",
              waits=(("dve", c2_dve[c]),))
        x2_sp.append(m.c["sp"])
        aw = []
        if c >= 2:
            aw.append(("pe", sq3_pe[c - 2]))
        m.emit("scalar",
               lambda e, c=c: e.activation(sqb[c % 2][:], xln3_ch(c),
                                           AF.Square),
               waits=aw, incs=(("act", 1),))
        sq3_act = m.c["act"]
        stw = [("act", sq3_act)]
        for half in range(2):
            m.mm(pk[:, half * 512:(half + 1) * 512], colones[:],
                 xln3[:, c * T + half * 512: c * T + (half + 1) * 512],
                 start=(c == 0), stop=(c == NCH - 1),
                 waits=stw if half == 0 else ())
            m.mm(pk[:, 1024 + half * 512:1024 + (half + 1) * 512],
                 colones[:], sqb[c % 2][:, half * 512:(half + 1) * 512],
                 start=(c == 0), stop=(c == NCH - 1))
        m.emit("tensor", lambda e: e.nop(), incs=(("pe", 1),))
        sq3_pe.append(m.c["pe"])
    stats4_pe = m.c["pe"]

    ck[5] = m.snapshot()
    es_u.close()
    hidden = es.enter_context(nc.sbuf_tensor("hidden", [128, FCH * T], BF16,
                                             side="right"))
    es_ffn = contextlib.ExitStack()
    wf = [es_ffn.enter_context(nc.sbuf_tensor(f"wf{i}", [128, D], F32, **R))
          for i in range(2)]
    qp = [es_ffn.enter_context(nc.sbuf_tensor(f"qp{i}", [128, D], BF16, **R))
          for i in range(4)]
    hid_ch = lambda c: hidden[:, c * T:(c + 1) * T]

    q_dve = {}
    qmm_pe = {}
    hid_dve = []

    def emit_quant(ci):
        for which in range(2):
            widx = 2 * ci + which
            if widx == 0:
                w = [("pe", stats2_pe)]
            elif widx == 1:
                w = [("pe", stats3_pe)]
            else:
                w = []
            if widx >= 2:
                w.append(("dve", q_dve[widx - 2]))
            m.dma("sync", wf[widx % 2][:],
                  (w1t_d if which == 0 else w2t_d)[ci, :, :],
                  f"wf{widx % 2}", waits=w)
            aw = [(f"wf{widx % 2}", 16 * (widx // 2 + 1))]
            if widx == 0:
                aw.append(("dve", thr_ready))
            if widx >= 4:
                aw.append(("pe", qmm_pe.get(widx - 4, 0)))
            m.emit("vector",
                   lambda e, i=widx, wh=which: e.tensor_scalar(
                       qp[i % 4][:], wf[i % 2][:], thrc[:, wh:wh + 1], None,
                       ALU.is_ge),
                   waits=aw)
            m.emit("vector",
                   lambda e, i=widx, wh=which: e.tensor_scalar(
                       scr1[:].bitcast(BF16), wf[i % 2][:],
                       nthrc[:, wh:wh + 1], None, ALU.is_le))
            m.emit("vector",
                   lambda e, i=widx: e.tensor_tensor(
                       qp[i % 4][:], qp[i % 4][:], scr1[:].bitcast(BF16),
                       ALU.subtract),
                   incs=(("dve", 1),))
            q_dve[widx] = m.c["dve"]

    emit_quant(0)
    emit_quant(1)

    rows4 = ln_rows(stats4_pe)
    bc4 = ln_bcast(rows4)
    ln3_dve = []
    for c in range(NCH):
        m.emit("vector",
               lambda e, c=c: e.tensor_tensor(s0b, xln3_ch(c), mbb[:],
                                              ALU.subtract))
        m.emit("vector",
               lambda e, c=c: e.tensor_tensor(xln3_ch(c), s0b, rbb[:],
                                              ALU.mult),
               incs=(("dve", 1),))
        ln3_dve.append(m.c["dve"])
    ln3_done = ln3_dve[-1]

    for ci in range(FCH):
        if 1 <= ci + 1 < FCH:
            emit_quant(ci + 1)

        if ci % 2 == 0:
            tg1, f1, f1w = pa, "fa", m.paw
            tg2, f2, f2w = pb, "fb", m.pbw
        else:
            tg1, f1, f1w = pk[:, 0:1024], "fka", m.pkaw
            tg2, f2, f2w = pk[:, 1024:2048], "fkb", m.pkbw
        for which, tg, fs, fw in ((0, tg1, f1, f1w), (1, tg2, f2, f2w)):
            w = [("dve", q_dve[2 * ci + which]), (fs, fw)]
            for cd in range(NCH):
                for half in range(2):
                    cw = list(w) if (cd == 0 and half == 0) else []
                    if ci == 0 and which == 0 and half == 0:
                        cw.append(("dve", ln3_dve[cd]))
                    m.mm(tg[:, half * 512:(half + 1) * 512],
                         qp[(2 * ci + which) % 4][:, cd * 128:(cd + 1) * 128],
                         xln3[:, cd * T + half * 512: cd * T + (half + 1) * 512],
                         start=(cd == 0), stop=(cd == NCH - 1),
                         waits=cw)
            m.emit("tensor", lambda e: e.nop(), incs=(("pe", 1),))
            qmm_pe[2 * ci + which] = m.c["pe"]
        aw = [("pe", qmm_pe[2 * ci + 1])]
        if ci >= 1:
            aw.append(("dve", hid_dve[ci - 1]))
        m.emit("scalar",
               lambda e, tg=tg1: e.activation(scr0[:], tg[:], AF.Sigmoid),
               waits=aw, incs=(("act", 1), (f1, 1)))
        if f1 == "fa":
            m.paw += 1
        else:
            m.pkaw += 1
        sig_act = m.c["act"]
        m.emit("vector",
               lambda e, ci=ci, tg=tg2: e.tensor_tensor(hid_ch(ci), scr0[:],
                                                        tg[:], ALU.mult),
               waits=(("act", sig_act),), incs=(("dve", 1), (f2, 1)))
        if f2 == "fb":
            m.pbw += 1
        else:
            m.pkbw += 1
        hid_dve.append(m.c["dve"])
    last_x2mm_pe = qmm_pe[2 * FCH - 1]

    ck[6] = m.snapshot()
    es_ffn.close()
    es_xln3.close()
    w3f = es_ffn.enter_context(nc.sbuf_tensor("w3f", [128, FF], F32, **R))
    q3p = [es_ffn.enter_context(nc.sbuf_tensor(f"q3p{i}", [128, FF], BF16, **R))
           for i in range(2)]
    q3tmp = es_ffn.enter_context(nc.sbuf_tensor("q3tmp", [128, FF], BF16, **R))

    o3_sp = []
    q3_dve = {}
    o3_pe = {}
    o3sq_pe = {}
    o3_act = []
    for co in range(NCH):
        w = [("pe", last_x2mm_pe)] if co == 0 else [("dve", q3_dve[co - 1])]
        m.dma("sync", w3f[:], w3t_d[co, :, :], "w3", waits=w)
        aw = [("w3", 16 * (co + 1))]
        if co >= 2:
            aw.append(("pe", o3_pe[co - 2]))
        m.emit("vector",
               lambda e, co=co: e.tensor_scalar(
                   q3p[co % 2][:], w3f[:], thrc[:, 2:3], None, ALU.is_ge),
               waits=aw)
        m.emit("vector",
               lambda e: e.tensor_scalar(q3tmp[:], w3f[:], nthrc[:, 2:3],
                                         None, ALU.is_le))
        m.emit("vector",
               lambda e, co=co: e.tensor_tensor(
                   q3p[co % 2][:], q3p[co % 2][:], q3tmp[:], ALU.subtract),
               incs=(("dve", 1),))
        q3_dve[co] = m.c["dve"]

        if co % 2 == 0:
            tgt, fs, fw = pa, "fa", m.paw
        else:
            tgt, fs, fw = pb, "fb", m.pbw
        w0 = [("dve", q3_dve[co]), (fs, fw)]
        for cj in range(FCH):
            cw = list(w0) if cj == 0 else []
            if co == 0:
                cw.append(("dve", hid_dve[cj]))
            for half in range(2):
                m.mm(tgt[:, half * 512:(half + 1) * 512],
                     q3p[co % 2][:, cj * 128:(cj + 1) * 128],
                     hidden[:, cj * T + half * 512: cj * T + (half + 1) * 512],
                     start=(cj == 0), stop=(cj == FCH - 1),
                     waits=cw if half == 0 else ())
        m.emit("tensor", lambda e: e.nop(), incs=(("pe", 1),))
        o3_pe[co] = m.c["pe"]
        aw = [("pe", o3_pe[co])]
        if co >= 1:
            aw.append(("sp", o3_sp[co - 1]))
        m.emit("scalar",
               lambda e, co=co, tg=tgt: e.activation(arb2[co % 2][:], tg[:],
                                                     AF.Identity,
                                                     bias=zcol[:]),
               waits=aw, incs=((fs, 1),))
        if co % 2 == 0:
            m.paw += 1
        else:
            m.pbw += 1
        m.emit("scalar",
               lambda e, co=co: e.activation(sqb[co % 2][:], arb2[co % 2][:],
                                             AF.Square),
               waits=[("pe", o3sq_pe[co - 2])] if co >= 2 else (),
               incs=(("act", 1),))
        o3_act.append(m.c["act"])
        m.dma("scalar", o3_d[co * 128:(co + 1) * 128, :], arb2[co % 2][:],
              "sp", waits=(("act", o3_act[co]),))
        o3_sp.append(m.c["sp"])
        stw = [("act", o3_act[co])]
        for half in range(2):
            m.mm(pk[:, half * 512:(half + 1) * 512], colones[:],
                 arb2[co % 2][:, half * 512:(half + 1) * 512],
                 start=(co == 0), stop=(co == NCH - 1),
                 waits=stw if half == 0 else ())
            m.mm(pk[:, 1024 + half * 512:1024 + (half + 1) * 512],
                 colones[:], sqb[co % 2][:, half * 512:(half + 1) * 512],
                 start=(co == 0), stop=(co == NCH - 1))
        m.emit("tensor", lambda e: e.nop(), incs=(("pe", 1),))
        o3sq_pe[co] = m.c["pe"]
    stats5_pe = m.c["pe"]

    o3r = [es.enter_context(nc.sbuf_tensor(f"o3r{i}", [128, T], BF16, **L))
           for i in range(2)]
    for c in range(2):
        m.dma("gpsimd", x2r[c][:], x2_d[c * 128:(c + 1) * 128, :],
              f"rlx{c}", waits=(("sp", x2_sp[-1]), ("act", cast_acts[-1])))
        m.dma("scalar", o3r[c][:], o3_d[c * 128:(c + 1) * 128, :],
              f"rla{c}", waits=(("sp", o3_sp[c]),))

    rows5 = ln_rows(stats5_pe)
    bc5 = ln_bcast(rows5)

    ck[7] = m.snapshot()
    es_ffn.close()
    outb = [es_ffn.enter_context(nc.sbuf_tensor(f"outb{i}", [128, T], F32,
                                                **R))
            for i in range(2)]
    fin_dve = []
    for c in range(NCH):
        if c >= 2:
            m.dma("gpsimd", x2r[c % 2][:], x2_d[c * 128:(c + 1) * 128, :],
                  f"rlx{c % 2}", waits=(("dve", fin_dve[c - 2]),))
            m.dma("scalar", o3r[c % 2][:], o3_d[c * 128:(c + 1) * 128, :],
                  f"rla{c % 2}", waits=(("dve", fin_dve[c - 2]),))
        w = [(f"rlx{c % 2}", m.c[f"rlx{c % 2}"]),
             (f"rla{c % 2}", m.c[f"rla{c % 2}"])]
        if c >= 2:
            w.append(("st", 16 * (c - 1)))
        m.emit("vector",
               lambda e, c=c: e.tensor_tensor(s0b, o3r[c % 2][:], mbb[:],
                                              ALU.subtract), waits=w)
        m.emit("vector",
               lambda e, c=c: e.tensor_tensor(s0b, s0b, rbb[:], ALU.mult))
        m.emit("vector",
               lambda e, c=c: e.tensor_tensor(outb[c % 2][:], x2r[c % 2][:],
                                              s0b, ALU.add),
               incs=(("dve", 1),))
        fin_dve.append(m.c["dve"])
        m.dma("sync", out_t[c * 128:(c + 1) * 128, :], outb[c % 2][:], "st",
              waits=(("dve", fin_dve[c]),))
    fw = [("st", 16 * NCH)]
    if debug:
        fw.append(("dbg", m.c["dbg"]))
    m.emit("gpsimd", lambda e: e.nop(), waits=fw)
    if phases < 99:
        m.truncate(ck[phases])
        m.c["st"] = 0
        for c in range(NCH):
            m.dma("sync", out_t[c * 128:(c + 1) * 128, :], mb_f[:], "st")
        def _final(e):
            e.wait_ge(m.sems["st"], 16 * NCH)
            return e.nop()
        m.ops["sync"].append(lambda e: _final(e))
        m.meta["sync"].append((((("st", 16 * NCH)),), (), "final"))
    build_program.last_mk = m

    with nc.Block() as block:
        @block.sync
        def _(e):
            for f in m.ops["sync"]:
                f(e)

        @block.scalar
        def _(e):
            for f in m.ops["scalar"]:
                f(e)

        @block.vector
        def _(e):
            for f in m.ops["vector"]:
                f(e)

        @block.tensor
        def _(e):
            for f in m.ops["tensor"]:
                f(e)

        @block.gpsimd
        def _(e):
            for f in m.ops["gpsimd"]:
                f(e)

    es_ffn.close()
    es_xln3.close()
    es_a2.close()
    m.es.close()
    es_a.close()
    return nc


_CACHE = {}


def _stage(inputs):
    x = np.ascontiguousarray(np.asarray(inputs["L0_attn_x"], np.float32))
    Wqk = np.asarray(inputs["L0_attn_Wqk"], np.float32)
    Wv = np.asarray(inputs["L0_attn_Wv"], np.float32)
    Wu = np.asarray(inputs["L0_attn_Wu"], np.float32)
    Wo = np.asarray(inputs["L0_attn_Wo"], np.float32)
    w1 = np.asarray(inputs["L0_ffn_w1"], np.float32)
    w2 = np.asarray(inputs["L0_ffn_w2"], np.float32)
    w3 = np.asarray(inputs["L0_ffn_w3"], np.float32)

    def attn_w(W):
        return np.ascontiguousarray(
            W.reshape(NCH, 128, NCH, 128).transpose(2, 1, 0, 3)
        ).reshape(NCH, 128, D)

    def ffn_w12(W):
        return np.ascontiguousarray(
            W.reshape(FCH, 128, NCH, 128).transpose(0, 3, 2, 1)
        ).reshape(FCH, 128, D)

    def ffn_w3(W):
        return np.ascontiguousarray(
            W.reshape(NCH, 128, FCH, 128).transpose(0, 3, 2, 1)
        ).reshape(NCH, 128, FF)

    def col(v):
        return np.ascontiguousarray(
            np.asarray(v, np.float32).reshape(NCH, 128).T)

    shared = {
        "wqk": attn_w(Wqk), "wv": attn_w(Wv), "wu": attn_w(Wu),
        "wo": attn_w(Wo),
        "bqk": col(inputs["L0_attn_bqk"]), "bv": col(inputs["L0_attn_bv"]),
        "bu": col(inputs["L0_attn_bu"]), "bo": col(inputs["L0_attn_bo"]),
        "lng": col(inputs["L0_attn_ln_g"]), "lnb": col(inputs["L0_attn_ln_b"]),
        "w1t": ffn_w12(w1), "w2t": ffn_w12(w2), "w3t": ffn_w3(w3),
    }
    flats = (shared["w1t"].reshape(-1), shared["w2t"].reshape(-1),
             shared["w3t"].reshape(-1))
    in_maps = []
    for core in range(8):
        b, half = divmod(core, 2)
        d = dict(shared)
        d["x_t"] = np.ascontiguousarray(x[b, half * T:(half + 1) * T, :].T)
        for j, name in enumerate(("gs1", "gs2", "gs3")):
            d[name] = np.ascontiguousarray(
                flats[j][core * L8:(core + 1) * L8].reshape(128, GW))
        in_maps.append(d)
    return in_maps


def kernel(**inputs):
    in_maps = _stage(inputs)
    if "nc" not in _CACHE:
        _CACHE["nc"] = build_program()
    nc = _CACHE["nc"]
    res = run_bass_kernel_spmd(nc, in_maps, core_ids=list(range(8)))
    out = np.empty((B, SEQ, D), np.float32)
    for core in range(8):
        b, half = divmod(core, 2)
        out[b, half * T:(half + 1) * T, :] = res.results[core]["out_t"].T
    return out


# version 26
# speedup vs baseline: 1.0536x; 1.0536x over previous
import contextlib
from collections import defaultdict

import numpy as np

import concourse.bass as bass
import concourse.mybir as mybir
from concourse.bass_utils import run_bass_kernel_spmd

F32 = mybir.dt.float32
BF16 = mybir.dt.bfloat16
AF = mybir.ActivationFunctionType
ALU = mybir.AluOpType
AX = mybir.AxisListType.X

B = 4
SEQ = 2048
D = 2048
T = 1024
NCH = 16
H = 16
FF = 5632
FCH = 44
LN_EPS = 1e-5
QEPS = 1e-8
LFULL = FF * D
L8 = LFULL // 8
GW = L8 // 128
GTS = 8
GTW = GW // GTS


class MK:
    def __init__(self, nc):
        self.nc = nc
        self.ops = {k: [] for k in ("sync", "scalar", "vector", "tensor", "gpsimd")}
        self.meta = {k: [] for k in ("sync", "scalar", "vector", "tensor", "gpsimd")}
        self.c = defaultdict(int)
        self.sems = {}
        self.es = contextlib.ExitStack()
        self.paw = 0
        self.pbw = 0
        self.pkaw = 0
        self.pkbw = 0

    def snapshot(self):
        return {k: len(v) for k, v in self.ops.items()}

    def truncate(self, snap):
        for k in self.ops:
            self.ops[k] = self.ops[k][:snap[k]]
            self.meta[k] = self.meta[k][:snap[k]]

    def sem(self, name):
        self.sems[name] = self.es.enter_context(self.nc.semaphore(name))

    def emit(self, eng, build, waits=(), incs=(), raw=False):
        w = tuple((s, v) for (s, v) in waits if v > 0)
        for s, a in incs:
            self.c[s] += a
        i = tuple(incs)

        def fn(e, _b=build, _w=w, _i=i, _raw=raw):
            for s, v in _w:
                e.wait_ge(self.sems[s], v)
            inst = _b(e)
            if _i:
                if _raw:
                    inst.then_inc(self.sems[_i[0][0]], _i[0][1])
                    rest = _i[1:]
                else:
                    e.drain().then_inc(self.sems[_i[0][0]], _i[0][1])
                    rest = _i[1:]
                for s, a in rest:
                    e.nop().then_inc(self.sems[s], a)
        self.ops[eng].append(fn)
        self.meta[eng].append((w, i, getattr(build, "__qualname__", "?")))

    def dma(self, eng, out, in_, sem, waits=()):
        self.emit(eng, lambda e, o=out, x=in_: e.dma_start(out=o, in_=x),
                  waits=waits, incs=((sem, 16),), raw=True)

    def mm(self, out, lhsT, rhs, start, stop, waits=(), incs=()):
        self.emit("tensor",
                  lambda e, o=out, l=lhsT, r=rhs, s0=start, s1=stop:
                      e.matmul(o, l, r, start=s0, stop=s1, skip_group_check=True),
                  waits=waits, incs=incs)


def build_program(phases=99, debug=False):
    nc = bass.Bass("TRN2", target_bir_lowering=False, detect_race_conditions=False)
    m = MK(nc)
    es = m.es

    x_t = nc.dram_tensor("x_t", [D, T], F32, kind="ExternalInput").ap()
    wqk_d = nc.dram_tensor("wqk", [NCH, 128, D], F32, kind="ExternalInput").ap()
    wv_d = nc.dram_tensor("wv", [NCH, 128, D], F32, kind="ExternalInput").ap()
    wu_d = nc.dram_tensor("wu", [NCH, 128, D], F32, kind="ExternalInput").ap()
    wo_d = nc.dram_tensor("wo", [NCH, 128, D], F32, kind="ExternalInput").ap()
    bqk_d = nc.dram_tensor("bqk", [128, NCH], F32, kind="ExternalInput").ap()
    bv_d = nc.dram_tensor("bv", [128, NCH], F32, kind="ExternalInput").ap()
    bu_d = nc.dram_tensor("bu", [128, NCH], F32, kind="ExternalInput").ap()
    bo_d = nc.dram_tensor("bo", [128, NCH], F32, kind="ExternalInput").ap()
    lng_d = nc.dram_tensor("lng", [128, NCH], F32, kind="ExternalInput").ap()
    lnb_d = nc.dram_tensor("lnb", [128, NCH], F32, kind="ExternalInput").ap()
    w1t_d = nc.dram_tensor("w1t", [FCH, 128, D], F32, kind="ExternalInput").ap()
    w2t_d = nc.dram_tensor("w2t", [FCH, 128, D], F32, kind="ExternalInput").ap()
    w3t_d = nc.dram_tensor("w3t", [NCH, 128, FF], F32, kind="ExternalInput").ap()
    gs1_d = nc.dram_tensor("gs1", [128, GW], F32, kind="ExternalInput").ap()
    gs2_d = nc.dram_tensor("gs2", [128, GW], F32, kind="ExternalInput").ap()
    gs3_d = nc.dram_tensor("gs3", [128, GW], F32, kind="ExternalInput").ap()
    out_t = nc.dram_tensor("out_t", [D, T], F32, kind="ExternalOutput").ap()

    gcc_in = nc.dram_tensor("gcc_in", [128, 24], F32, kind="Internal").ap()
    gcc_out = nc.dram_tensor("gcc_out", [128, 24], F32, kind="Internal").ap()
    kv_in = nc.dram_tensor("kv_in", [128, D], F32, kind="Internal").ap()
    kv_out = nc.dram_tensor("kv_out", [128, D], F32, kind="Internal").ap()
    x2_d = nc.dram_tensor("x2_d", [D, T], F32, kind="Internal").ap()
    o3_d = nc.dram_tensor("o3_d", [D, T], BF16, kind="Internal").ap()

    for s in ("x0", "x1", "x2", "x3", "wa0", "wa1", "wa2", "wa3", "wf0", "wf1", "wf2",
              "w3", "g0", "g1", "gb", "sp", "rlx0", "rlx1", "rla0", "rla1",
              "rlk", "cc", "pe", "peW", "act", "dve", "pl", "st",
              "fa", "fb", "fka", "fkb", "dbg"):
        m.sem(s)
    if debug:
        dbg_kvp = nc.dram_tensor("dbg_kvp", [128, D], F32, kind="ExternalOutput").ap()
        dbg_kvf = nc.dram_tensor("dbg_kvf", [128, D], F32, kind="ExternalOutput").ap()
        dbg_thr = nc.dram_tensor("dbg_thr", [128, 4], F32, kind="ExternalOutput").ap()
        dbg_gc = nc.dram_tensor("dbg_gc", [128, 24], F32, kind="ExternalOutput").ap()
        dbg_qfm = nc.dram_tensor("dbg_qfm", [128, 16384], BF16, kind="ExternalOutput").ap()
        dbg_xln = nc.dram_tensor("dbg_xln", [128, 16384], BF16, kind="ExternalOutput").ap()
        dbg_trow = nc.dram_tensor("dbg_trow", [1, 32], F32, kind="ExternalOutput").ap()
        dbg_ao = nc.dram_tensor("dbg_ao", [128, 16384], BF16, kind="ExternalOutput").ap()

    L = dict(side="left")
    ident = es.enter_context(nc.sbuf_tensor("ident", [128, 128], BF16, **L))
    onest = es.enter_context(nc.sbuf_tensor("onest", [128, 128], BF16, **L))
    colones = es.enter_context(nc.sbuf_tensor("colones", [128, 128], BF16, **L))
    invd = es.enter_context(nc.sbuf_tensor("invd", [128, 128], BF16, **L))
    ones128b = es.enter_context(nc.sbuf_tensor("ones128b", [1, 128], BF16, **L))
    m_rb = es.enter_context(nc.sbuf_tensor("m_rb", [1, T], BF16, **L))
    r_rb = es.enter_context(nc.sbuf_tensor("r_rb", [1, T], BF16, **L))
    ones128f = es.enter_context(nc.sbuf_tensor("ones128f", [1, 128], F32, **L))
    onescol_b = es.enter_context(nc.sbuf_tensor("onescol_b", [128, 1], BF16, **L))
    onescol_f = es.enter_context(nc.sbuf_tensor("onescol_f", [128, 1], F32, **L))
    eps_ap = es.enter_context(nc.sbuf_tensor("eps_ap", [1, 1], F32, **L))
    zcol = es.enter_context(nc.sbuf_tensor("zcol", [128, 1], F32, **L))
    bqk_c = es.enter_context(nc.sbuf_tensor("bqk_c", [128, NCH], F32, **L))
    bv_c = es.enter_context(nc.sbuf_tensor("bv_c", [128, NCH], F32, **L))
    bu_c = es.enter_context(nc.sbuf_tensor("bu_c", [128, NCH], F32, **L))
    bo_c = es.enter_context(nc.sbuf_tensor("bo_c", [128, NCH], F32, **L))
    g_c = es.enter_context(nc.sbuf_tensor("g_c", [128, NCH], F32, **L))
    bln_c = es.enter_context(nc.sbuf_tensor("bln_c", [128, NCH], F32, **L))
    thrc = es.enter_context(nc.sbuf_tensor("thrc", [128, 4], F32, **L))
    nthrc = es.enter_context(nc.sbuf_tensor("nthrc", [128, 4], F32, **L))
    gcols = es.enter_context(nc.sbuf_tensor("gcols", [128, 24], F32, **L))
    m_r = es.enter_context(nc.sbuf_tensor("m_r", [1, T], F32, **L))
    r_r = es.enter_context(nc.sbuf_tensor("r_r", [1, T], F32, **L))
    trow = es.enter_context(nc.sbuf_tensor("trow", [1, 32], F32, **L))
    mbb = es.enter_context(nc.sbuf_tensor("mbb", [128, T], BF16, **L))
    rbb = es.enter_context(nc.sbuf_tensor("rbb", [128, T], BF16, **L))
    arb2 = [es.enter_context(nc.sbuf_tensor(f"arb2{i}", [128, T], BF16, **L))
            for i in range(2)]
    x2r = [es.enter_context(nc.sbuf_tensor(f"x2r{i}", [128, T], F32, **L))
           for i in range(2)]
    scr0 = es.enter_context(nc.sbuf_tensor("scr0", [128, T], F32, **L))
    scr1 = es.enter_context(nc.sbuf_tensor("scr1", [128, T], F32, **L))
    sqb0 = es.enter_context(nc.sbuf_tensor("sqb0", [128, T], BF16, **L))
    sqb1 = es.enter_context(nc.sbuf_tensor("sqb1", [128, T], BF16, **L))
    sqb = [sqb0, sqb1]
    t1_r = scr0[0:1, :]
    t2_r = scr1[0:1, :]
    s0b = scr0[:].bitcast(BF16)[:, 0:T]
    s1b = scr1[:].bitcast(BF16)[:, 0:T]

    es_g = contextlib.ExitStack()
    gbuf = [es_g.enter_context(nc.sbuf_tensor(f"gbuf{i}", [128, GTW], F32, **L))
            for i in range(2)]

    R = dict(side="right")
    es_a = contextlib.ExitStack()
    xin = [es_a.enter_context(nc.sbuf_tensor(f"xin{i}", [128, T], F32, **R))
           for i in range(2)]
    es_u = contextlib.ExitStack()
    u_fm = es_u.enter_context(nc.sbuf_tensor("u_fm", [128, NCH * T], BF16, **R))
    es_a2 = contextlib.ExitStack()
    kv_f = es_a2.enter_context(nc.sbuf_tensor("kv_f", [128, D], F32, **R))
    kv_b = es_a2.enter_context(nc.sbuf_tensor("kv_b", [128, D], BF16, **R))
    wa = [es_a2.enter_context(nc.sbuf_tensor(f"wa{i}", [128, D], BF16, **R))
          for i in range(4)]
    scr2 = es_a2.enter_context(nc.sbuf_tensor("scr2", [128, T], F32, **R))
    scr3 = es_a2.enter_context(nc.sbuf_tensor("scr3", [128, T], F32, **R))
    q_fm = es_a2.enter_context(nc.sbuf_tensor("q_fm", [128, NCH * T], BF16, **R))
    es_xln = contextlib.ExitStack()
    xln = es_xln.enter_context(nc.sbuf_tensor("xln", [128, NCH * T], BF16, **R))
    vpan = es_xln.enter_context(nc.sbuf_tensor("vpan", [128, T], BF16, **R))
    qtok = es_xln.enter_context(nc.sbuf_tensor("qtok", [128, T], BF16, **R))
    vtok = es_xln.enter_context(nc.sbuf_tensor("vtok", [128, T], BF16, **R))

    pk = es.enter_context(nc.psum_tensor("pk", [128, 2048], F32))
    pa = es.enter_context(nc.psum_tensor("pa", [128, 1024], F32))
    pb = es.enter_context(nc.psum_tensor("pb", [128, 1024], F32))

    xln_ch = lambda c: xln[:, c * T:(c + 1) * T]
    qfm_ch = lambda c: q_fm[:, c * T:(c + 1) * T]
    ufm_ch = lambda c: u_fm[:, c * T:(c + 1) * T]

    m.emit("vector", lambda e: e.memset(onest[:], 1.0))
    m.emit("vector", lambda e: e.memset(invd[:], 1.0 / D))
    m.emit("vector", lambda e: e.memset(ones128f[:], 1.0))
    m.emit("vector", lambda e: e.memset(ones128b[:], 1.0))
    m.emit("vector", lambda e: e.memset(onescol_b[:], 1.0))
    m.emit("vector", lambda e: e.memset(onescol_f[:], 1.0))
    m.emit("vector", lambda e: e.memset(zcol[:], 0.0))
    m.emit("vector", lambda e: e.memset(eps_ap[:], LN_EPS), incs=(("dve", 1),))
    consts_dve = m.c["dve"]
    m.emit("gpsimd",
           lambda e: e.affine_select(ident[:], onest[:], [[-1, 128]],
                                     ALU.is_equal, 0.0, base=0,
                                     channel_multiplier=1),
           waits=(("dve", consts_dve),))
    m.emit("gpsimd",
           lambda e: e.affine_select(colones[:], invd[:], [[1, 128]],
                                     ALU.is_equal, 0.0, base=0,
                                     channel_multiplier=0),
           incs=(("pl", 1),))
    ident_pl = m.c["pl"]
    for dst, srcd in ((bqk_c, bqk_d), (bv_c, bv_d), (bu_c, bu_d), (bo_c, bo_d),
                     (g_c, lng_d), (bln_c, lnb_d)):
        m.dma("scalar", dst[:], srcd, "gb")
    bias_gb = m.c["gb"]
    ck = {}
    ck[0] = m.snapshot()

    gsrc = (gs1_d, gs2_d, gs3_d)
    gred_dve = []

    def emit_gamma_pair(idx):
        mtx, k = divmod(idx, GTS)
        gsem = f"g{idx % 2}"
        srcap = gsrc[mtx][:, k * GTW:(k + 1) * GTW]
        waits = [("dve", gred_dve[idx - 2])] if idx >= 2 else []
        m.dma("scalar", gbuf[idx % 2][:], srcap, gsem, waits=waits)
        m.emit("vector",
               lambda e, i=idx: e.tensor_reduce(
                   gcols[:, i:i + 1], gbuf[i % 2][:], AX, ALU.add,
                   apply_absolute_value=True),
               waits=((gsem, 16 * (idx // 2 + 1)),),
               incs=(("dve", 1),))
        gred_dve.append(m.c["dve"])

    def wa_panel_src(idx):
        if idx < 32:
            h, which = divmod(idx, 2)
            return (wqk_d if which == 0 else wv_d)[h, :, :]
        if idx < 48:
            return wu_d[idx - 32, :, :]
        return wo_d[idx - 48, :, :]

    def emit_wa(idx, extra_waits=()):
        w = list(extra_waits)
        if idx >= 4:
            w.append(("peW", idx - 3))
        m.emit("gpsimd",
               lambda e, i=idx: e.dma_start(out=wa[i % 4][:],
                                            in_=wa_panel_src(i)),
               waits=w, incs=((f"wa{idx % 4}", 16),), raw=True)

    def wa_wait(idx):
        return (f"wa{idx % 4}", 16 * (idx // 4 + 1))

    sq_act = []
    sq_pe = []
    for c in range(NCH):
        w = [("act", sq_act[c - 4])] if c >= 4 else []
        if c == 0:
            w.append(("pl", ident_pl))
        m.emit("gpsimd",
               lambda e, c=c: e.dma_start(out=xln_ch(c),
                                          in_=x_t[c * 128:(c + 1) * 128, :]),
               waits=w, incs=((f"x{c % 4}", 16),), raw=True)
        aw = [(f"x{c % 4}", 16 * (c // 4 + 1))]
        if c >= 2:
            aw.append(("pe", sq_pe[c - 2]))
        if c % 2 == 1 and c < 9:
            emit_wa((c - 1) // 2)
        m.emit("scalar",
               lambda e, c=c: e.activation(sqb[c % 2][:], xln_ch(c), AF.Square),
               waits=aw, incs=(("act", 1),))
        sq_act.append(m.c["act"])
        stw = [("act", sq_act[c])]
        for half in range(2):
            m.mm(pk[:, half * 512:(half + 1) * 512], colones[:],
                 xln[:, c * T + half * 512: c * T + (half + 1) * 512],
                 start=(c == 0), stop=(c == NCH - 1),
                 waits=stw if half == 0 else ())
            m.mm(pk[:, 1024 + half * 512:1024 + (half + 1) * 512],
                 colones[:], sqb[c % 2][:, half * 512:(half + 1) * 512],
                 start=(c == 0), stop=(c == NCH - 1))
        m.emit("tensor", lambda e: e.nop(), incs=(("pe", 1),))
        sq_pe.append(m.c["pe"])
    stats0_pe = m.c["pe"]
    ck[8] = m.snapshot()

    def ln_rows(wait_pe):
        m.emit("vector", lambda e: e.tensor_copy(m_r[:], pk[0:1, 0:1024]),
               waits=(("pe", wait_pe),))
        m.emit("vector", lambda e: e.tensor_copy(t2_r, pk[0:1, 1024:2048]))
        m.emit("vector", lambda e: e.drain())
        m.emit("vector", lambda e: e.tensor_mul(t1_r, m_r[:], m_r[:]))
        m.emit("vector", lambda e: e.drain())
        m.emit("vector",
               lambda e: e.tensor_tensor(t2_r, t2_r, t1_r, ALU.subtract),
               incs=(("dve", 1),))
        rows_dve = m.c["dve"]
        m.emit("scalar",
               lambda e: e.activation(t1_r, t2_r, AF.Ln, bias=eps_ap[:]),
               waits=(("dve", rows_dve),))
        m.emit("scalar", lambda e: e.drain())
        m.emit("scalar",
               lambda e: e.activation(r_r[:], t1_r, AF.Exp, scale=-0.5),
               incs=(("act", 1),))
        rows_act2 = m.c["act"]
        m.emit("vector", lambda e: e.tensor_copy(m_rb[:], m_r[:]),
               waits=(("act", rows_act2),))
        m.emit("vector", lambda e: e.tensor_copy(r_rb[:], r_r[:]),
               incs=(("dve", 1),))
        return m.c["dve"]

    def ln_bcast(wait_dve):
        w = [("dve", wait_dve), ("fa", m.paw)]
        for half in range(2):
            m.mm(pa[:, half * 512:(half + 1) * 512], ones128b[:],
                 m_rb[0:1, half * 512:(half + 1) * 512], True, True,
                 waits=w if half == 0 else ())
        m.emit("tensor", lambda e: e.nop(), incs=(("pe", 1),))
        bpe_a = m.c["pe"]
        for half in range(2):
            m.mm(pb[:, half * 512:(half + 1) * 512], ones128b[:],
                 r_rb[0:1, half * 512:(half + 1) * 512], True, True,
                 waits=[("fb", m.pbw)] if half == 0 else ())
        m.emit("tensor", lambda e: e.nop(), incs=(("pe", 1),))
        bpe_b = m.c["pe"]
        m.emit("vector", lambda e: e.tensor_copy(mbb[:], pa[:]),
               waits=(("pe", bpe_a),), incs=(("fa", 1),))
        m.paw += 1
        m.emit("vector", lambda e: e.tensor_copy(rbb[:], pb[:]),
               waits=(("pe", bpe_b),), incs=(("fb", 1), ("dve", 1)))
        m.pbw += 1
        return m.c["dve"]

    rows1 = ln_rows(stats0_pe)
    ck[9] = m.snapshot()
    bc1 = ln_bcast(rows1)
    ck[10] = m.snapshot()
    ln1_dve = []
    for c in range(NCH):
        m.emit("vector",
               lambda e, c=c: e.tensor_tensor(s0b, xln_ch(c), mbb[:],
                                              ALU.subtract))
        m.emit("vector",
               lambda e, c=c: e.tensor_tensor(xln_ch(c), s0b, rbb[:],
                                              ALU.mult),
               incs=(("dve", 1),))
        ln1_dve.append(m.c["dve"])
    ln1_done = ln1_dve[-1]
    if debug:
        m.dma("sync", dbg_xln, xln[:], "dbg", waits=(("dve", ln1_done),))
    ck[1] = m.snapshot()


    for idx in range(4, 32):
        emit_wa(idx)

    pairs = [pa, pb, pk[:, 1024:2048]]
    pfree = ["fa", "fb", "fkb"]
    pattr = ["paw", "pbw", "pkbw"]
    pget = lambda pi: getattr(m, pattr[pi])
    def pinc(pi):
        setattr(m, pattr[pi], pget(pi) + 1)
    kvacc = lambda h: pk[:, (h % 2) * 512:(h % 2) * 512 + 128]

    elu_dve = [None] * H
    xb_acts = [None] * H
    silu_acts = [None] * H
    qt_pe = [None] * H
    vt_pe = [None] * H
    qtok_dve = [None] * H
    vtok_dve = [None] * H
    kv_pe = [None] * H
    kvev_dve = [None] * H

    def emit_qkproj(h):
        pi = (2 * h) % 3
        tgt = pairs[pi]
        w = [wa_wait(2 * h), (pfree[pi], pget(pi))]
        if h == 0:
            w += [("gb", bias_gb)]
        for cd in range(NCH):
            for half in range(2):
                cw = list(w) if (cd == 0 and half == 0) else []
                if h == 0 and half == 0:
                    cw.append(("dve", ln1_dve[cd]))
                m.mm(tgt[:, half * 512:(half + 1) * 512],
                     wa[(2 * h) % 4][:, cd * 128:(cd + 1) * 128],
                     xln[:, cd * T + half * 512: cd * T + (half + 1) * 512],
                     start=(cd == 0), stop=(cd == NCH - 1),
                     waits=cw)
        m.emit("tensor", lambda e: e.nop(), incs=(("pe", 1), ("peW", 1)))
        qk_pe = m.c["pe"]
        sA = scr0 if h % 2 == 0 else scr2
        sB = scr1 if h % 2 == 0 else scr3
        aw = [("pe", qk_pe)]
        if h > 1:
            aw.append(("dve", elu_dve[h - 2]))
        m.emit("scalar",
               lambda e, h=h, sA=sA, tg=tgt: e.activation(
                   sA[:], tg[:], AF.Exp, bias=bqk_c[:, h:h + 1]),
               waits=aw)
        m.emit("scalar",
               lambda e, h=h, sB=sB, tg=tgt: e.activation(
                   sB[:], tg[:], AF.Identity, bias=bqk_c[:, h:h + 1]),
               incs=(("act", 1), (pfree[pi], 1)))
        pinc(pi)
        xb_acts[h] = m.c["act"]

    def emit_elu_dve(h):
        sA = scr0 if h % 2 == 0 else scr2
        sB = scr1 if h % 2 == 0 else scr3
        m.emit("vector",
               lambda e, sA=sA: e.tensor_scalar(sA[:], sA[:], 1.0, -1.0,
                                                ALU.min, ALU.add),
               waits=(("act", xb_acts[h]),))
        m.emit("vector",
               lambda e, h=h, sA=sA, sB=sB: e.scalar_tensor_tensor(
                   qfm_ch(h), sB[:], 0.0, sA[:], ALU.max, ALU.add),
               incs=(("dve", 1),))
        elu_dve[h] = m.c["dve"]

    def emit_vproj(h):
        pi = (2 * h + 1) % 3
        tgt = pairs[pi]
        w = [wa_wait(2 * h + 1), (pfree[pi], pget(pi))]
        for cd in range(NCH):
            for half in range(2):
                m.mm(tgt[:, half * 512:(half + 1) * 512],
                     wa[(2 * h + 1) % 4][:, cd * 128:(cd + 1) * 128],
                     xln[:, cd * T + half * 512: cd * T + (half + 1) * 512],
                     start=(cd == 0), stop=(cd == NCH - 1),
                     waits=w if (cd == 0 and half == 0) else ())
        m.emit("tensor", lambda e: e.nop(), incs=(("pe", 1), ("peW", 1)))
        v_pe = m.c["pe"]
        m.emit("scalar",
               lambda e, h=h, tg=tgt: e.activation(vpan[:], tg[:], AF.Silu,
                                                   bias=bv_c[:, h:h + 1]),
               waits=(("pe", v_pe),), incs=(("act", 1), (pfree[pi], 1)))
        pinc(pi)
        silu_acts[h] = m.c["act"]

    def emit_qT(h):
        pi = (2 * h) % 3
        tgt = pairs[pi]
        w = [("dve", elu_dve[h]), (pfree[pi], pget(pi))]
        for s in range(8):
            m.mm(tgt[:, s * 128:(s + 1) * 128],
                 q_fm[:, h * T + s * 128: h * T + (s + 1) * 128],
                 ident[:], True, True, waits=w if s == 0 else ())
        m.emit("tensor", lambda e: e.nop(), incs=(("pe", 1),))
        qt_pe[h] = m.c["pe"]
        aw = [("pe", qt_pe[h])]
        if h > 0:
            aw.append(("pe", kv_pe[h - 1]))
        m.emit("vector", lambda e, tg=tgt: e.tensor_copy(qtok[:], tg[:]),
               waits=aw, incs=(("dve", 1), (pfree[pi], 1)))
        pinc(pi)
        qtok_dve[h] = m.c["dve"]

    def emit_vT(h):
        pi = (2 * h + 1) % 3
        tgt = pairs[pi]
        w = [("act", silu_acts[h]), (pfree[pi], pget(pi))]
        for s in range(8):
            m.mm(tgt[:, s * 128:(s + 1) * 128],
                 vpan[:, s * 128:(s + 1) * 128],
                 ident[:], True, True, waits=w if s == 0 else ())
        m.emit("tensor", lambda e: e.nop(), incs=(("pe", 1),))
        vt_pe[h] = m.c["pe"]
        aw = [("pe", vt_pe[h])]
        if h > 0:
            aw.append(("pe", kv_pe[h - 1]))
        m.emit("vector", lambda e, tg=tgt: e.tensor_copy(vtok[:], tg[:]),
               waits=aw, incs=(("dve", 1), (pfree[pi], 1)))
        pinc(pi)
        vtok_dve[h] = m.c["dve"]

    def emit_kv(h):
        w = [("dve", vtok_dve[h])]
        if h >= 2:
            w.append(("dve", kvev_dve[h - 2]))
        for s in range(8):
            m.mm(kvacc(h),
                 qtok[:, s * 128:(s + 1) * 128],
                 vtok[:, s * 128:(s + 1) * 128],
                 start=(s == 0), stop=(s == 7),
                 waits=w if s == 0 else ())
        m.emit("tensor", lambda e: e.nop(), incs=(("pe", 1),))
        kv_pe[h] = m.c["pe"]
        m.emit("vector",
               lambda e, h=h: e.tensor_copy(kv_f[:, h * 128:(h + 1) * 128],
                                            kvacc(h)),
               waits=(("pe", kv_pe[h]),), incs=(("dve", 1),))
        kvev_dve[h] = m.c["dve"]

    for h in range(H):
        if h > 0:
            emit_qT(h - 1)
        emit_qkproj(h)
        if h > 0:
            emit_vT(h - 1)
        emit_elu_dve(h)
        emit_vproj(h)
        if h > 0:
            emit_kv(h - 1)
        if h < 12:
            emit_gamma_pair(2 * h)
            emit_gamma_pair(2 * h + 1)
    emit_qT(H - 1)
    emit_vT(H - 1)
    emit_kv(H - 1)

    ck[2] = m.snapshot()
    m.dma("scalar", gcc_in, gcols[:], "sp", waits=(("dve", gred_dve[-1]),))
    kvevac_dve = kvev_dve[H - 1]
    m.dma("scalar", kv_in, kv_f[:], "sp", waits=(("dve", kvevac_dve),))
    kvput = m.c["sp"]
    if debug:
        m.dma("sync", dbg_kvp, kv_f[:], "dbg", waits=(("dve", kvevac_dve),))
    m.emit("gpsimd",
           lambda e: e.collective_compute(
               "AllReduce", ALU.add,
               replica_groups=[[0, 1, 2, 3, 4, 5, 6, 7]],
               ins=[gcc_in.opt()], outs=[gcc_out.opt()]),
           waits=(("sp", kvput),), incs=(("cc", 1),), raw=True)
    m.emit("gpsimd",
           lambda e: e.collective_compute(
               "AllReduce", ALU.add,
               replica_groups=[[0, 1], [2, 3], [4, 5], [6, 7]],
               ins=[kv_in.opt()], outs=[kv_out.opt()]),
           waits=(("sp", kvput),), incs=(("cc", 1),), raw=True)
    for idx in range(32, 48):
        emit_wa(idx)
    u_act = []
    for p in range(NCH):
        pi = p % 3
        tgt, fs, fw = pairs[pi], pfree[pi], pget(pi)
        w = [wa_wait(32 + p), (fs, fw)]
        for cd in range(NCH):
            for half in range(2):
                m.mm(tgt[:, half * 512:(half + 1) * 512],
                     wa[(32 + p) % 4][:, cd * 128:(cd + 1) * 128],
                     xln[:, cd * T + half * 512: cd * T + (half + 1) * 512],
                     start=(cd == 0), stop=(cd == NCH - 1),
                     waits=w if (cd == 0 and half == 0) else ())
        m.emit("tensor", lambda e: e.nop(), incs=(("pe", 1), ("peW", 1)))
        up_pe = m.c["pe"]
        m.emit("scalar",
               lambda e, p=p, tg=tgt: e.activation(ufm_ch(p), tg[:], AF.Silu,
                                                   bias=bu_c[:, p:p + 1]),
               waits=(("pe", up_pe),), incs=(("act", 1), (fs, 1)))
        pinc(pi)
        u_act.append(m.c["act"])

    m.dma("scalar", kv_f[:], kv_out, "rlk", waits=(("cc", 2),))
    m.dma("scalar", gcols[:], gcc_out, "rlk", waits=(("cc", 2),))
    rlk_done = m.c["rlk"]
    m.emit("vector", lambda e: e.tensor_copy(kv_b[:], kv_f[:]),
           waits=(("rlk", rlk_done),), incs=(("dve", 1),))
    kvb_dve = m.c["dve"]
    if debug:
        m.dma("sync", dbg_kvf, kv_f[:], "dbg", waits=(("rlk", rlk_done),))

    m.mm(pk[0:1, 0:24], onescol_f[:], gcols[:], True, True,
         waits=(("rlk", rlk_done), ("dve", kvevac_dve)))
    m.emit("tensor", lambda e: e.nop(), incs=(("pe", 1),))
    gsum_pe = m.c["pe"]
    m.emit("vector", lambda e: e.tensor_copy(trow[0:1, 0:24], pk[0:1, 0:24]),
           waits=(("pe", gsum_pe),))
    m.emit("vector", lambda e: e.drain())
    for mtx in range(3):
        m.emit("vector",
               lambda e, j=mtx: e.reduce_sum(trow[0:1, 24 + j:25 + j],
                                             trow[0:1, j * 8:(j + 1) * 8], AX))
        m.emit("vector", lambda e: e.drain())
        m.emit("vector",
               lambda e, j=mtx: e.tensor_scalar(
                   trow[0:1, 28 + j:29 + j], trow[0:1, 24 + j:25 + j],
                   0.5 / LFULL, 0.5 * QEPS, ALU.mult, ALU.add))
        m.emit("vector", lambda e: e.drain())
    m.emit("vector", lambda e: e.memset(trow[0:1, 31:32], 0.0),
           incs=(("dve", 1),))
    thr_dve = m.c["dve"]
    m.mm(pk[:, 0:4], ones128f[:], trow[0:1, 28:32], True, True,
         waits=(("dve", thr_dve),))
    m.emit("tensor", lambda e: e.nop(), incs=(("pe", 1),))
    thrb_pe = m.c["pe"]
    m.emit("vector", lambda e: e.tensor_copy(thrc[:], pk[:, 0:4]),
           waits=(("pe", thrb_pe),))
    m.emit("vector", lambda e: e.drain())
    m.emit("vector", lambda e: e.tensor_scalar_mul(nthrc[:], thrc[:], -1.0),
           incs=(("dve", 1),))
    thr_ready = m.c["dve"]
    if debug:
        m.dma("sync", dbg_thr, thrc[:], "dbg", waits=(("dve", thr_ready),))
        m.dma("sync", dbg_gc, gcols[:], "dbg", waits=(("dve", thr_ready),))
        m.dma("sync", dbg_trow, trow[:], "dbg", waits=(("dve", thr_ready),))

    ck[3] = m.snapshot()
    for idx in range(48, 64):
        emit_wa(idx)
    es_xln.close()
    ao = es_a2.enter_context(nc.sbuf_tensor("ao", [128, NCH * T], BF16, **R))
    ao_ch = lambda c: ao[:, c * T:(c + 1) * T]

    if debug:
        m.dma("sync", dbg_qfm, q_fm[:], "dbg", waits=(("dve", kvb_dve),))
    sq2_pe = {}
    for h in range(H):
        if h % 2 == 0:
            tgt, fs, fw = pa, "fa", m.paw
        else:
            tgt, fs, fw = pb, "fb", m.pbw
        w = [("dve", kvb_dve), (fs, fw)]
        for half in range(2):
            m.mm(tgt[:, half * 512:(half + 1) * 512],
                 kv_b[:, h * 128:(h + 1) * 128],
                 q_fm[:, h * T + half * 512: h * T + (half + 1) * 512],
                 True, True, waits=w if half == 0 else ())
        m.emit("tensor", lambda e: e.nop(), incs=(("pe", 1),))
        e_pe = m.c["pe"]
        m.emit("vector",
               lambda e, h=h, tg=tgt: e.tensor_copy(ao_ch(h), tg[:]),
               waits=(("pe", e_pe),), incs=((fs, 1),))
        if h % 2 == 0:
            m.paw += 1
        else:
            m.pbw += 1
        m.emit("vector",
               lambda e, h=h: e.tensor_mul(sqb[h % 2][:], ao_ch(h), ao_ch(h)),
               waits=[("pe", sq2_pe[h - 2])] if h >= 2 else (),
               incs=(("dve", 1),))
        ein_dve = m.c["dve"]
        stw = [("dve", ein_dve)]
        if h == 0:
            stw.append(("dve", thr_ready))
            stw.append(("fkb", m.pkbw))
        for half in range(2):
            m.mm(pk[:, half * 512:(half + 1) * 512], colones[:],
                 ao[:, h * T + half * 512: h * T + (half + 1) * 512],
                 start=(h == 0), stop=(h == H - 1),
                 waits=stw if half == 0 else ())
            m.mm(pk[:, 1024 + half * 512:1024 + (half + 1) * 512],
                 colones[:], sqb[h % 2][:, half * 512:(half + 1) * 512],
                 start=(h == 0), stop=(h == H - 1))
        m.emit("tensor", lambda e: e.nop(), incs=(("pe", 1),))
        sq2_pe[h] = m.c["pe"]
    stats2_pe = m.c["pe"]

    if debug:
        m.dma("sync", dbg_ao, ao[:], "dbg", waits=(("pe", stats2_pe),))
    rows2 = ln_rows(stats2_pe)
    bc2 = ln_bcast(rows2)

    z_dve = []
    for c in range(NCH):
        m.emit("vector",
               lambda e, c=c: e.tensor_tensor(s0b, ao_ch(c), mbb[:],
                                              ALU.subtract),
               waits=(("act", u_act[c]),))
        m.emit("vector",
               lambda e, c=c: e.tensor_tensor(s1b, s0b, rbb[:], ALU.mult))
        m.emit("vector",
               lambda e, c=c: e.tensor_scalar(s1b, s1b, g_c[:, c:c + 1],
                                              bln_c[:, c:c + 1], ALU.mult,
                                              ALU.add))
        m.emit("vector",
               lambda e, c=c: e.tensor_tensor(ao_ch(c), s1b, ufm_ch(c),
                                              ALU.mult),
               incs=(("dve", 1),))
        z_dve.append(m.c["dve"])

    wo_act = []
    wo_sq_pe = {}
    for p in range(NCH):
        if p % 2 == 0:
            tgt, fs, fw = pa, "fa", m.paw
        else:
            tgt, fs, fw = pb, "fb", m.pbw
        w0 = [wa_wait(48 + p), (fs, fw)]
        for cd in range(NCH):
            cw = list(w0) if cd == 0 else []
            if p == 0:
                cw.append(("dve", z_dve[cd]))
            elif p == 1 and cd == 0:
                cw.append(("dve", z_dve[-1]))
            for half in range(2):
                m.mm(tgt[:, half * 512:(half + 1) * 512],
                     wa[(48 + p) % 4][:, cd * 128:(cd + 1) * 128],
                     ao[:, cd * T + half * 512: cd * T + (half + 1) * 512],
                     start=(cd == 0), stop=(cd == NCH - 1),
                     waits=cw if half == 0 else ())
        m.emit("tensor", lambda e: e.nop(), incs=(("pe", 1), ("peW", 1)))
        wo_pe = m.c["pe"]
        m.emit("vector",
               lambda e, p=p, tg=tgt: e.tensor_scalar(
                   ufm_ch(p), tg[:], bo_c[:, p:p + 1], None, ALU.add),
               waits=(("pe", wo_pe),), incs=((fs, 1),))
        if p % 2 == 0:
            m.paw += 1
        else:
            m.pbw += 1
        m.emit("vector",
               lambda e, p=p: e.tensor_mul(sqb[p % 2][:], ufm_ch(p),
                                           ufm_ch(p)),
               waits=[("pe", wo_sq_pe[p - 2])] if p >= 2 else (),
               incs=(("dve", 1),))
        wo_act.append(m.c["dve"])
        stw = [("dve", wo_act[p])]
        for half in range(2):
            m.mm(pk[:, half * 512:(half + 1) * 512], colones[:],
                 u_fm[:, p * T + half * 512: p * T + (half + 1) * 512],
                 start=(p == 0), stop=(p == NCH - 1),
                 waits=stw if half == 0 else ())
            m.mm(pk[:, 1024 + half * 512:1024 + (half + 1) * 512],
                 colones[:], sqb[p % 2][:, half * 512:(half + 1) * 512],
                 start=(p == 0), stop=(p == NCH - 1))
        m.emit("tensor", lambda e: e.nop(), incs=(("pe", 1),))
        wo_sq_pe[p] = m.c["pe"]
    stats3_pe = m.c["pe"]

    rows3 = ln_rows(stats3_pe)
    bc3 = ln_bcast(rows3)

    ck[4] = m.snapshot()
    es_a2.close()
    es_g.close()
    es_xln3 = contextlib.ExitStack()
    xln3 = es_xln3.enter_context(nc.sbuf_tensor("xln3", [128, NCH * T], BF16,
                                                **L))
    xln3_ch = lambda c: xln3[:, c * T:(c + 1) * T]

    x2_sp = []
    c2_dve = []
    sq3_pe = []
    cast_acts = []
    for i in range(2):
        m.dma("gpsimd", xin[i][:], x_t[i * 128:(i + 1) * 128, :], f"rlx{i}")
    for c in range(NCH):
        if c >= 2:
            m.dma("gpsimd", xin[c % 2][:], x_t[c * 128:(c + 1) * 128, :],
                  f"rlx{c % 2}", waits=(("dve", c2_dve[c - 2]),))
        w = [(f"rlx{c % 2}", m.c[f"rlx{c % 2}"])]
        if c >= 2:
            w.append(("sp", x2_sp[c - 2]))
            w.append(("act", cast_acts[c - 2]))
        m.emit("vector",
               lambda e, c=c: e.tensor_tensor(s0b, ufm_ch(c), mbb[:],
                                              ALU.subtract), waits=w)
        m.emit("vector",
               lambda e, c=c: e.tensor_tensor(s0b, s0b, rbb[:], ALU.mult))
        m.emit("vector",
               lambda e, c=c: e.tensor_tensor(x2r[c % 2][:], xin[c % 2][:],
                                              s0b, ALU.add))
        m.emit("vector", lambda e: e.engine_nop(), incs=(("dve", 1),))
        c2_dve.append(m.c["dve"])
        m.emit("scalar",
               lambda e, c=c: e.copy(xln3_ch(c), x2r[c % 2][:]),
               waits=(("dve", c2_dve[c]),), incs=(("act", 1),))
        cast_acts.append(m.c["act"])
        m.dma("scalar", x2_d[c * 128:(c + 1) * 128, :], x2r[c % 2][:], "sp",
              waits=(("dve", c2_dve[c]),))
        x2_sp.append(m.c["sp"])
        aw = []
        if c >= 2:
            aw.append(("pe", sq3_pe[c - 2]))
        m.emit("scalar",
               lambda e, c=c: e.activation(sqb[c % 2][:], xln3_ch(c),
                                           AF.Square),
               waits=aw, incs=(("act", 1),))
        sq3_act = m.c["act"]
        stw = [("act", sq3_act)]
        for half in range(2):
            m.mm(pk[:, half * 512:(half + 1) * 512], colones[:],
                 xln3[:, c * T + half * 512: c * T + (half + 1) * 512],
                 start=(c == 0), stop=(c == NCH - 1),
                 waits=stw if half == 0 else ())
            m.mm(pk[:, 1024 + half * 512:1024 + (half + 1) * 512],
                 colones[:], sqb[c % 2][:, half * 512:(half + 1) * 512],
                 start=(c == 0), stop=(c == NCH - 1))
        m.emit("tensor", lambda e: e.nop(), incs=(("pe", 1),))
        sq3_pe.append(m.c["pe"])
    stats4_pe = m.c["pe"]

    ck[5] = m.snapshot()
    es_u.close()
    hidden = es.enter_context(nc.sbuf_tensor("hidden", [128, FCH * T], BF16,
                                             side="right"))
    es_ffn = contextlib.ExitStack()
    wf = [es_ffn.enter_context(nc.sbuf_tensor(f"wf{i}", [128, D], F32, **R))
          for i in range(2)]
    qp = [es_ffn.enter_context(nc.sbuf_tensor(f"qp{i}", [128, D], BF16, **R))
          for i in range(4)]
    hid_ch = lambda c: hidden[:, c * T:(c + 1) * T]

    q_dve = {}
    qmm_pe = {}
    hid_dve = []

    def emit_quant(ci):
        for which in range(2):
            widx = 2 * ci + which
            if widx == 0:
                w = [("pe", stats2_pe)]
            elif widx == 1:
                w = [("pe", stats3_pe)]
            else:
                w = []
            if widx >= 2:
                w.append(("dve", q_dve[widx - 2]))
            m.dma("sync", wf[widx % 2][:],
                  (w1t_d if which == 0 else w2t_d)[ci, :, :],
                  f"wf{widx % 2}", waits=w)
            aw = [(f"wf{widx % 2}", 16 * (widx // 2 + 1))]
            if widx == 0:
                aw.append(("dve", thr_ready))
            if widx >= 4:
                aw.append(("pe", qmm_pe.get(widx - 4, 0)))
            m.emit("vector",
                   lambda e, i=widx, wh=which: e.tensor_scalar(
                       qp[i % 4][:], wf[i % 2][:], thrc[:, wh:wh + 1], None,
                       ALU.is_ge),
                   waits=aw)
            m.emit("vector",
                   lambda e, i=widx, wh=which: e.tensor_scalar(
                       scr1[:].bitcast(BF16), wf[i % 2][:],
                       nthrc[:, wh:wh + 1], None, ALU.is_le))
            m.emit("vector",
                   lambda e, i=widx: e.tensor_tensor(
                       qp[i % 4][:], qp[i % 4][:], scr1[:].bitcast(BF16),
                       ALU.subtract),
                   incs=(("dve", 1),))
            q_dve[widx] = m.c["dve"]

    emit_quant(0)
    emit_quant(1)

    rows4 = ln_rows(stats4_pe)
    bc4 = ln_bcast(rows4)
    ln3_dve = []
    for c in range(NCH):
        m.emit("vector",
               lambda e, c=c: e.tensor_tensor(s0b, xln3_ch(c), mbb[:],
                                              ALU.subtract))
        m.emit("vector",
               lambda e, c=c: e.tensor_tensor(xln3_ch(c), s0b, rbb[:],
                                              ALU.mult),
               incs=(("dve", 1),))
        ln3_dve.append(m.c["dve"])
    ln3_done = ln3_dve[-1]

    for ci in range(FCH):
        if 1 <= ci + 1 < FCH:
            emit_quant(ci + 1)

        if ci % 2 == 0:
            tg1, f1, f1w = pa, "fa", m.paw
            tg2, f2, f2w = pb, "fb", m.pbw
        else:
            tg1, f1, f1w = pk[:, 0:1024], "fka", m.pkaw
            tg2, f2, f2w = pk[:, 1024:2048], "fkb", m.pkbw
        for which, tg, fs, fw in ((0, tg1, f1, f1w), (1, tg2, f2, f2w)):
            w = [("dve", q_dve[2 * ci + which]), (fs, fw)]
            for cd in range(NCH):
                for half in range(2):
                    cw = list(w) if (cd == 0 and half == 0) else []
                    if ci == 0 and which == 0 and half == 0:
                        cw.append(("dve", ln3_dve[cd]))
                    m.mm(tg[:, half * 512:(half + 1) * 512],
                         qp[(2 * ci + which) % 4][:, cd * 128:(cd + 1) * 128],
                         xln3[:, cd * T + half * 512: cd * T + (half + 1) * 512],
                         start=(cd == 0), stop=(cd == NCH - 1),
                         waits=cw)
            m.emit("tensor", lambda e: e.nop(), incs=(("pe", 1),))
            qmm_pe[2 * ci + which] = m.c["pe"]
        aw = [("pe", qmm_pe[2 * ci + 1])]
        if ci >= 1:
            aw.append(("dve", hid_dve[ci - 1]))
        m.emit("scalar",
               lambda e, tg=tg1: e.activation(scr0[:], tg[:], AF.Sigmoid),
               waits=aw, incs=(("act", 1), (f1, 1)))
        if f1 == "fa":
            m.paw += 1
        else:
            m.pkaw += 1
        sig_act = m.c["act"]
        m.emit("vector",
               lambda e, ci=ci, tg=tg2: e.tensor_tensor(hid_ch(ci), scr0[:],
                                                        tg[:], ALU.mult),
               waits=(("act", sig_act),), incs=(("dve", 1), (f2, 1)))
        if f2 == "fb":
            m.pbw += 1
        else:
            m.pkbw += 1
        hid_dve.append(m.c["dve"])
    last_x2mm_pe = qmm_pe[2 * FCH - 1]

    ck[6] = m.snapshot()
    es_ffn.close()
    es_xln3.close()
    w3f = es_ffn.enter_context(nc.sbuf_tensor("w3f", [128, FF], F32, **R))
    q3p = [es_ffn.enter_context(nc.sbuf_tensor(f"q3p{i}", [128, FF], BF16, **R))
           for i in range(2)]
    q3tmp = es_ffn.enter_context(nc.sbuf_tensor("q3tmp", [128, FF], BF16, **R))

    o3_sp = []
    q3_dve = {}
    o3_pe = {}
    o3sq_pe = {}
    o3_act = []
    for co in range(NCH):
        w = [("pe", last_x2mm_pe)] if co == 0 else [("dve", q3_dve[co - 1])]
        m.dma("sync", w3f[:], w3t_d[co, :, :], "w3", waits=w)
        aw = [("w3", 16 * (co + 1))]
        if co >= 2:
            aw.append(("pe", o3_pe[co - 2]))
        m.emit("vector",
               lambda e, co=co: e.tensor_scalar(
                   q3p[co % 2][:], w3f[:], thrc[:, 2:3], None, ALU.is_ge),
               waits=aw)
        m.emit("vector",
               lambda e: e.tensor_scalar(q3tmp[:], w3f[:], nthrc[:, 2:3],
                                         None, ALU.is_le))
        m.emit("vector",
               lambda e, co=co: e.tensor_tensor(
                   q3p[co % 2][:], q3p[co % 2][:], q3tmp[:], ALU.subtract),
               incs=(("dve", 1),))
        q3_dve[co] = m.c["dve"]

        if co % 2 == 0:
            tgt, fs, fw = pa, "fa", m.paw
        else:
            tgt, fs, fw = pb, "fb", m.pbw
        w0 = [("dve", q3_dve[co]), (fs, fw)]
        for cj in range(FCH):
            cw = list(w0) if cj == 0 else []
            if co == 0:
                cw.append(("dve", hid_dve[cj]))
            for half in range(2):
                m.mm(tgt[:, half * 512:(half + 1) * 512],
                     q3p[co % 2][:, cj * 128:(cj + 1) * 128],
                     hidden[:, cj * T + half * 512: cj * T + (half + 1) * 512],
                     start=(cj == 0), stop=(cj == FCH - 1),
                     waits=cw if half == 0 else ())
        m.emit("tensor", lambda e: e.nop(), incs=(("pe", 1),))
        o3_pe[co] = m.c["pe"]
        aw = [("pe", o3_pe[co])]
        if co >= 1:
            aw.append(("sp", o3_sp[co - 1]))
        m.emit("scalar",
               lambda e, co=co, tg=tgt: e.activation(arb2[co % 2][:], tg[:],
                                                     AF.Identity,
                                                     bias=zcol[:]),
               waits=aw, incs=((fs, 1),))
        if co % 2 == 0:
            m.paw += 1
        else:
            m.pbw += 1
        m.emit("scalar",
               lambda e, co=co: e.activation(sqb[co % 2][:], arb2[co % 2][:],
                                             AF.Square),
               waits=[("pe", o3sq_pe[co - 2])] if co >= 2 else (),
               incs=(("act", 1),))
        o3_act.append(m.c["act"])
        m.dma("scalar", o3_d[co * 128:(co + 1) * 128, :], arb2[co % 2][:],
              "sp", waits=(("act", o3_act[co]),))
        o3_sp.append(m.c["sp"])
        stw = [("act", o3_act[co])]
        for half in range(2):
            m.mm(pk[:, half * 512:(half + 1) * 512], colones[:],
                 arb2[co % 2][:, half * 512:(half + 1) * 512],
                 start=(co == 0), stop=(co == NCH - 1),
                 waits=stw if half == 0 else ())
            m.mm(pk[:, 1024 + half * 512:1024 + (half + 1) * 512],
                 colones[:], sqb[co % 2][:, half * 512:(half + 1) * 512],
                 start=(co == 0), stop=(co == NCH - 1))
        m.emit("tensor", lambda e: e.nop(), incs=(("pe", 1),))
        o3sq_pe[co] = m.c["pe"]
    stats5_pe = m.c["pe"]

    o3r = [es.enter_context(nc.sbuf_tensor(f"o3r{i}", [128, T], BF16, **L))
           for i in range(4)]
    for c in range(2):
        m.dma("gpsimd", x2r[c][:], x2_d[c * 128:(c + 1) * 128, :],
              f"rlx{c}", waits=(("sp", x2_sp[-1]), ("act", cast_acts[-1])))
    for c in range(4):
        m.dma("scalar", o3r[c][:], o3_d[c * 128:(c + 1) * 128, :],
              f"rla{c % 2}", waits=(("sp", o3_sp[c]),))

    rows5 = ln_rows(stats5_pe)
    bc5 = ln_bcast(rows5)

    ck[7] = m.snapshot()
    es_ffn.close()
    outb = [es_ffn.enter_context(nc.sbuf_tensor(f"outb{i}", [128, T], F32,
                                                **R))
            for i in range(2)]
    fin_dve = []
    for c in range(NCH):
        if c >= 2:
            m.dma("gpsimd", x2r[c % 2][:], x2_d[c * 128:(c + 1) * 128, :],
                  f"rlx{c % 2}", waits=(("dve", fin_dve[c - 2]),))
        if c >= 4:
            m.dma("scalar", o3r[c % 4][:], o3_d[c * 128:(c + 1) * 128, :],
                  f"rla{c % 2}", waits=(("dve", fin_dve[c - 4]),))
        w = [(f"rlx{c % 2}", m.c[f"rlx{c % 2}"]),
             (f"rla{c % 2}", m.c[f"rla{c % 2}"])]
        if c >= 2:
            w.append(("st", 16 * (c - 1)))
        m.emit("vector",
               lambda e, c=c: e.tensor_tensor(s0b, o3r[c % 4][:], mbb[:],
                                              ALU.subtract), waits=w)
        m.emit("vector",
               lambda e, c=c: e.tensor_tensor(s0b, s0b, rbb[:], ALU.mult))
        m.emit("vector",
               lambda e, c=c: e.tensor_tensor(outb[c % 2][:], x2r[c % 2][:],
                                              s0b, ALU.add),
               incs=(("dve", 1),))
        fin_dve.append(m.c["dve"])
        m.dma("sync", out_t[c * 128:(c + 1) * 128, :], outb[c % 2][:], "st",
              waits=(("dve", fin_dve[c]),))
    fw = [("st", 16 * NCH)]
    if debug:
        fw.append(("dbg", m.c["dbg"]))
    m.emit("gpsimd", lambda e: e.nop(), waits=fw)
    if phases < 99:
        m.truncate(ck[phases])
        m.c["st"] = 0
        for c in range(NCH):
            m.dma("sync", out_t[c * 128:(c + 1) * 128, :], mb_f[:], "st")
        def _final(e):
            e.wait_ge(m.sems["st"], 16 * NCH)
            return e.nop()
        m.ops["sync"].append(lambda e: _final(e))
        m.meta["sync"].append((((("st", 16 * NCH)),), (), "final"))
    build_program.last_mk = m

    with nc.Block() as block:
        @block.sync
        def _(e):
            for f in m.ops["sync"]:
                f(e)

        @block.scalar
        def _(e):
            for f in m.ops["scalar"]:
                f(e)

        @block.vector
        def _(e):
            for f in m.ops["vector"]:
                f(e)

        @block.tensor
        def _(e):
            for f in m.ops["tensor"]:
                f(e)

        @block.gpsimd
        def _(e):
            for f in m.ops["gpsimd"]:
                f(e)

    es_ffn.close()
    es_xln3.close()
    es_a2.close()
    m.es.close()
    es_a.close()
    return nc


_CACHE = {}


def _stage(inputs):
    x = np.ascontiguousarray(np.asarray(inputs["L0_attn_x"], np.float32))
    Wqk = np.asarray(inputs["L0_attn_Wqk"], np.float32)
    Wv = np.asarray(inputs["L0_attn_Wv"], np.float32)
    Wu = np.asarray(inputs["L0_attn_Wu"], np.float32)
    Wo = np.asarray(inputs["L0_attn_Wo"], np.float32)
    w1 = np.asarray(inputs["L0_ffn_w1"], np.float32)
    w2 = np.asarray(inputs["L0_ffn_w2"], np.float32)
    w3 = np.asarray(inputs["L0_ffn_w3"], np.float32)

    def attn_w(W):
        return np.ascontiguousarray(
            W.reshape(NCH, 128, NCH, 128).transpose(2, 1, 0, 3)
        ).reshape(NCH, 128, D)

    def ffn_w12(W):
        return np.ascontiguousarray(
            W.reshape(FCH, 128, NCH, 128).transpose(0, 3, 2, 1)
        ).reshape(FCH, 128, D)

    def ffn_w3(W):
        return np.ascontiguousarray(
            W.reshape(NCH, 128, FCH, 128).transpose(0, 3, 2, 1)
        ).reshape(NCH, 128, FF)

    def col(v):
        return np.ascontiguousarray(
            np.asarray(v, np.float32).reshape(NCH, 128).T)

    shared = {
        "wqk": attn_w(Wqk), "wv": attn_w(Wv), "wu": attn_w(Wu),
        "wo": attn_w(Wo),
        "bqk": col(inputs["L0_attn_bqk"]), "bv": col(inputs["L0_attn_bv"]),
        "bu": col(inputs["L0_attn_bu"]), "bo": col(inputs["L0_attn_bo"]),
        "lng": col(inputs["L0_attn_ln_g"]), "lnb": col(inputs["L0_attn_ln_b"]),
        "w1t": ffn_w12(w1), "w2t": ffn_w12(w2), "w3t": ffn_w3(w3),
    }
    flats = (shared["w1t"].reshape(-1), shared["w2t"].reshape(-1),
             shared["w3t"].reshape(-1))
    in_maps = []
    for core in range(8):
        b, half = divmod(core, 2)
        d = dict(shared)
        d["x_t"] = np.ascontiguousarray(x[b, half * T:(half + 1) * T, :].T)
        for j, name in enumerate(("gs1", "gs2", "gs3")):
            d[name] = np.ascontiguousarray(
                flats[j][core * L8:(core + 1) * L8].reshape(128, GW))
        in_maps.append(d)
    return in_maps


def kernel(**inputs):
    in_maps = _stage(inputs)
    if "nc" not in _CACHE:
        _CACHE["nc"] = build_program()
    nc = _CACHE["nc"]
    res = run_bass_kernel_spmd(nc, in_maps, core_ids=list(range(8)))
    out = np.empty((B, SEQ, D), np.float32)
    for core in range(8):
        b, half = divmod(core, 2)
        out[b, half * T:(half + 1) * T, :] = res.results[core]["out_t"].T
    return out


# version 27
# speedup vs baseline: 1.0558x; 1.0020x over previous
import contextlib
from collections import defaultdict

import numpy as np

import concourse.bass as bass
import concourse.mybir as mybir
from concourse.bass_utils import run_bass_kernel_spmd

F32 = mybir.dt.float32
BF16 = mybir.dt.bfloat16
AF = mybir.ActivationFunctionType
ALU = mybir.AluOpType
AX = mybir.AxisListType.X

B = 4
SEQ = 2048
D = 2048
T = 1024
NCH = 16
H = 16
FF = 5632
FCH = 44
LN_EPS = 1e-5
QEPS = 1e-8
LFULL = FF * D
L8 = LFULL // 8
GW = L8 // 128
GTS = 8
GTW = GW // GTS


class MK:
    def __init__(self, nc):
        self.nc = nc
        self.ops = {k: [] for k in ("sync", "scalar", "vector", "tensor", "gpsimd")}
        self.meta = {k: [] for k in ("sync", "scalar", "vector", "tensor", "gpsimd")}
        self.c = defaultdict(int)
        self.sems = {}
        self.es = contextlib.ExitStack()
        self.paw = 0
        self.pbw = 0
        self.pkaw = 0
        self.pkbw = 0

    def snapshot(self):
        return {k: len(v) for k, v in self.ops.items()}

    def truncate(self, snap):
        for k in self.ops:
            self.ops[k] = self.ops[k][:snap[k]]
            self.meta[k] = self.meta[k][:snap[k]]

    def sem(self, name):
        self.sems[name] = self.es.enter_context(self.nc.semaphore(name))

    def emit(self, eng, build, waits=(), incs=(), raw=False):
        w = tuple((s, v) for (s, v) in waits if v > 0)
        for s, a in incs:
            self.c[s] += a
        i = tuple(incs)

        def fn(e, _b=build, _w=w, _i=i, _raw=raw):
            for s, v in _w:
                e.wait_ge(self.sems[s], v)
            inst = _b(e)
            if _i:
                if _raw:
                    inst.then_inc(self.sems[_i[0][0]], _i[0][1])
                    rest = _i[1:]
                else:
                    e.drain().then_inc(self.sems[_i[0][0]], _i[0][1])
                    rest = _i[1:]
                for s, a in rest:
                    e.nop().then_inc(self.sems[s], a)
        self.ops[eng].append(fn)
        self.meta[eng].append((w, i, getattr(build, "__qualname__", "?")))

    def dma(self, eng, out, in_, sem, waits=()):
        self.emit(eng, lambda e, o=out, x=in_: e.dma_start(out=o, in_=x),
                  waits=waits, incs=((sem, 16),), raw=True)

    def mm(self, out, lhsT, rhs, start, stop, waits=(), incs=()):
        self.emit("tensor",
                  lambda e, o=out, l=lhsT, r=rhs, s0=start, s1=stop:
                      e.matmul(o, l, r, start=s0, stop=s1, skip_group_check=True),
                  waits=waits, incs=incs)


def build_program(phases=99, debug=False):
    nc = bass.Bass("TRN2", target_bir_lowering=False, detect_race_conditions=False)
    m = MK(nc)
    es = m.es

    x_t = nc.dram_tensor("x_t", [D, T], F32, kind="ExternalInput").ap()
    wqk_d = nc.dram_tensor("wqk", [NCH, 128, D], F32, kind="ExternalInput").ap()
    wv_d = nc.dram_tensor("wv", [NCH, 128, D], F32, kind="ExternalInput").ap()
    wu_d = nc.dram_tensor("wu", [NCH, 128, D], F32, kind="ExternalInput").ap()
    wo_d = nc.dram_tensor("wo", [NCH, 128, D], F32, kind="ExternalInput").ap()
    bqk_d = nc.dram_tensor("bqk", [128, NCH], F32, kind="ExternalInput").ap()
    bv_d = nc.dram_tensor("bv", [128, NCH], F32, kind="ExternalInput").ap()
    bu_d = nc.dram_tensor("bu", [128, NCH], F32, kind="ExternalInput").ap()
    bo_d = nc.dram_tensor("bo", [128, NCH], F32, kind="ExternalInput").ap()
    lng_d = nc.dram_tensor("lng", [128, NCH], F32, kind="ExternalInput").ap()
    lnb_d = nc.dram_tensor("lnb", [128, NCH], F32, kind="ExternalInput").ap()
    w1t_d = nc.dram_tensor("w1t", [FCH, 128, D], F32, kind="ExternalInput").ap()
    w2t_d = nc.dram_tensor("w2t", [FCH, 128, D], F32, kind="ExternalInput").ap()
    w3t_d = nc.dram_tensor("w3t", [NCH, 128, FF], F32, kind="ExternalInput").ap()
    gs1_d = nc.dram_tensor("gs1", [128, GW], F32, kind="ExternalInput").ap()
    gs2_d = nc.dram_tensor("gs2", [128, GW], F32, kind="ExternalInput").ap()
    gs3_d = nc.dram_tensor("gs3", [128, GW], F32, kind="ExternalInput").ap()
    out_t = nc.dram_tensor("out_t", [D, T], F32, kind="ExternalOutput").ap()

    gcc_in = nc.dram_tensor("gcc_in", [128, 24], F32, kind="Internal").ap()
    gcc_out = nc.dram_tensor("gcc_out", [128, 24], F32, kind="Internal").ap()
    kv_in = nc.dram_tensor("kv_in", [128, D], F32, kind="Internal").ap()
    kv_out = nc.dram_tensor("kv_out", [128, D], F32, kind="Internal").ap()
    x2_d = nc.dram_tensor("x2_d", [D, T], F32, kind="Internal").ap()
    o3_d = nc.dram_tensor("o3_d", [D, T], BF16, kind="Internal").ap()

    for s in ("x0", "x1", "x2", "x3", "wa0", "wa1", "wa2", "wa3", "wf0", "wf1", "wf2",
              "w3", "g0", "g1", "gb", "sp", "rlx0", "rlx1", "rla0", "rla1",
              "rlk", "cc", "pe", "peW", "act", "dve", "pl", "st",
              "fa", "fb", "fka", "fkb", "dbg"):
        m.sem(s)
    if debug:
        dbg_kvp = nc.dram_tensor("dbg_kvp", [128, D], F32, kind="ExternalOutput").ap()
        dbg_kvf = nc.dram_tensor("dbg_kvf", [128, D], F32, kind="ExternalOutput").ap()
        dbg_thr = nc.dram_tensor("dbg_thr", [128, 4], F32, kind="ExternalOutput").ap()
        dbg_gc = nc.dram_tensor("dbg_gc", [128, 24], F32, kind="ExternalOutput").ap()
        dbg_qfm = nc.dram_tensor("dbg_qfm", [128, 16384], BF16, kind="ExternalOutput").ap()
        dbg_xln = nc.dram_tensor("dbg_xln", [128, 16384], BF16, kind="ExternalOutput").ap()
        dbg_trow = nc.dram_tensor("dbg_trow", [1, 32], F32, kind="ExternalOutput").ap()
        dbg_ao = nc.dram_tensor("dbg_ao", [128, 16384], BF16, kind="ExternalOutput").ap()

    L = dict(side="left")
    ident = es.enter_context(nc.sbuf_tensor("ident", [128, 128], BF16, **L))
    onest = es.enter_context(nc.sbuf_tensor("onest", [128, 128], BF16, **L))
    colones = es.enter_context(nc.sbuf_tensor("colones", [128, 128], BF16, **L))
    invd = es.enter_context(nc.sbuf_tensor("invd", [128, 128], BF16, **L))
    ones128b = es.enter_context(nc.sbuf_tensor("ones128b", [1, 128], BF16, **L))
    m_rb = es.enter_context(nc.sbuf_tensor("m_rb", [1, T], BF16, **L))
    r_rb = es.enter_context(nc.sbuf_tensor("r_rb", [1, T], BF16, **L))
    ones128f = es.enter_context(nc.sbuf_tensor("ones128f", [1, 128], F32, **L))
    onescol_b = es.enter_context(nc.sbuf_tensor("onescol_b", [128, 1], BF16, **L))
    onescol_f = es.enter_context(nc.sbuf_tensor("onescol_f", [128, 1], F32, **L))
    eps_ap = es.enter_context(nc.sbuf_tensor("eps_ap", [1, 1], F32, **L))
    zcol = es.enter_context(nc.sbuf_tensor("zcol", [128, 1], F32, **L))
    bqk_c = es.enter_context(nc.sbuf_tensor("bqk_c", [128, NCH], F32, **L))
    bv_c = es.enter_context(nc.sbuf_tensor("bv_c", [128, NCH], F32, **L))
    bu_c = es.enter_context(nc.sbuf_tensor("bu_c", [128, NCH], F32, **L))
    bo_c = es.enter_context(nc.sbuf_tensor("bo_c", [128, NCH], F32, **L))
    g_c = es.enter_context(nc.sbuf_tensor("g_c", [128, NCH], F32, **L))
    bln_c = es.enter_context(nc.sbuf_tensor("bln_c", [128, NCH], F32, **L))
    thrc = es.enter_context(nc.sbuf_tensor("thrc", [128, 4], F32, **L))
    nthrc = es.enter_context(nc.sbuf_tensor("nthrc", [128, 4], F32, **L))
    gcols = es.enter_context(nc.sbuf_tensor("gcols", [128, 24], F32, **L))
    m_r = es.enter_context(nc.sbuf_tensor("m_r", [1, T], F32, **L))
    r_r = es.enter_context(nc.sbuf_tensor("r_r", [1, T], F32, **L))
    trow = es.enter_context(nc.sbuf_tensor("trow", [1, 32], F32, **L))
    mbb = es.enter_context(nc.sbuf_tensor("mbb", [128, T], BF16, **L))
    rbb = es.enter_context(nc.sbuf_tensor("rbb", [128, T], BF16, **L))
    arb2 = [es.enter_context(nc.sbuf_tensor(f"arb2{i}", [128, T], BF16, **L))
            for i in range(2)]
    x2r = [es.enter_context(nc.sbuf_tensor(f"x2r{i}", [128, T], F32, **L))
           for i in range(2)]
    scr0 = es.enter_context(nc.sbuf_tensor("scr0", [128, T], F32, **L))
    scr1 = es.enter_context(nc.sbuf_tensor("scr1", [128, T], F32, **L))
    sqb0 = es.enter_context(nc.sbuf_tensor("sqb0", [128, T], BF16, **L))
    sqb1 = es.enter_context(nc.sbuf_tensor("sqb1", [128, T], BF16, **L))
    sqb = [sqb0, sqb1]
    t1_r = scr0[0:1, :]
    t2_r = scr1[0:1, :]
    s0b = scr0[:].bitcast(BF16)[:, 0:T]
    s1b = scr1[:].bitcast(BF16)[:, 0:T]

    es_g = contextlib.ExitStack()
    gbuf = [es_g.enter_context(nc.sbuf_tensor(f"gbuf{i}", [128, GTW], F32, **L))
            for i in range(2)]

    R = dict(side="right")
    es_a = contextlib.ExitStack()
    xin = [es_a.enter_context(nc.sbuf_tensor(f"xin{i}", [128, T], F32, **R))
           for i in range(2)]
    es_u = contextlib.ExitStack()
    u_fm = es_u.enter_context(nc.sbuf_tensor("u_fm", [128, NCH * T], BF16, **R))
    es_a2 = contextlib.ExitStack()
    kv_f = es_a2.enter_context(nc.sbuf_tensor("kv_f", [128, D], F32, **R))
    kv_b = es_a2.enter_context(nc.sbuf_tensor("kv_b", [128, D], BF16, **R))
    wa = [es_a2.enter_context(nc.sbuf_tensor(f"wa{i}", [128, D], BF16, **R))
          for i in range(4)]
    scr2 = es_a2.enter_context(nc.sbuf_tensor("scr2", [128, T], F32, **R))
    scr3 = es_a2.enter_context(nc.sbuf_tensor("scr3", [128, T], F32, **R))
    q_fm = es_a2.enter_context(nc.sbuf_tensor("q_fm", [128, NCH * T], BF16, **R))
    es_xln = contextlib.ExitStack()
    xln = es_xln.enter_context(nc.sbuf_tensor("xln", [128, NCH * T], BF16, **R))
    vpan = es_xln.enter_context(nc.sbuf_tensor("vpan", [128, T], BF16, **R))
    qtok = es_xln.enter_context(nc.sbuf_tensor("qtok", [128, T], BF16, **R))
    vtok = es_xln.enter_context(nc.sbuf_tensor("vtok", [128, T], BF16, **R))

    pk = es.enter_context(nc.psum_tensor("pk", [128, 2048], F32))
    pa = es.enter_context(nc.psum_tensor("pa", [128, 1024], F32))
    pb = es.enter_context(nc.psum_tensor("pb", [128, 1024], F32))

    xln_ch = lambda c: xln[:, c * T:(c + 1) * T]
    qfm_ch = lambda c: q_fm[:, c * T:(c + 1) * T]
    ufm_ch = lambda c: u_fm[:, c * T:(c + 1) * T]

    m.emit("vector", lambda e: e.memset(onest[:], 1.0))
    m.emit("vector", lambda e: e.memset(invd[:], 1.0 / D))
    m.emit("vector", lambda e: e.memset(ones128f[:], 1.0))
    m.emit("vector", lambda e: e.memset(ones128b[:], 1.0))
    m.emit("vector", lambda e: e.memset(onescol_b[:], 1.0))
    m.emit("vector", lambda e: e.memset(onescol_f[:], 1.0))
    m.emit("vector", lambda e: e.memset(zcol[:], 0.0))
    m.emit("vector", lambda e: e.memset(eps_ap[:], LN_EPS), incs=(("dve", 1),))
    consts_dve = m.c["dve"]
    m.emit("gpsimd",
           lambda e: e.affine_select(ident[:], onest[:], [[-1, 128]],
                                     ALU.is_equal, 0.0, base=0,
                                     channel_multiplier=1),
           waits=(("dve", consts_dve),))
    m.emit("gpsimd",
           lambda e: e.affine_select(colones[:], invd[:], [[1, 128]],
                                     ALU.is_equal, 0.0, base=0,
                                     channel_multiplier=0),
           incs=(("pl", 1),))
    ident_pl = m.c["pl"]
    for dst, srcd in ((bqk_c, bqk_d), (bv_c, bv_d), (bu_c, bu_d), (bo_c, bo_d),
                     (g_c, lng_d), (bln_c, lnb_d)):
        m.dma("scalar", dst[:], srcd, "gb")
    bias_gb = m.c["gb"]
    ck = {}
    ck[0] = m.snapshot()

    gsrc = (gs1_d, gs2_d, gs3_d)
    gred_dve = []

    def emit_gamma_pair(idx):
        mtx, k = divmod(idx, GTS)
        gsem = f"g{idx % 2}"
        srcap = gsrc[mtx][:, k * GTW:(k + 1) * GTW]
        waits = [("dve", gred_dve[idx - 2])] if idx >= 2 else []
        m.dma("scalar", gbuf[idx % 2][:], srcap, gsem, waits=waits)
        m.emit("vector",
               lambda e, i=idx: e.tensor_reduce(
                   gcols[:, i:i + 1], gbuf[i % 2][:], AX, ALU.add,
                   apply_absolute_value=True),
               waits=((gsem, 16 * (idx // 2 + 1)),),
               incs=(("dve", 1),))
        gred_dve.append(m.c["dve"])

    def wa_panel_src(idx):
        if idx < 32:
            h, which = divmod(idx, 2)
            return (wqk_d if which == 0 else wv_d)[h, :, :]
        if idx < 48:
            return wu_d[idx - 32, :, :]
        return wo_d[idx - 48, :, :]

    def emit_wa(idx, extra_waits=()):
        w = list(extra_waits)
        if idx >= 4:
            w.append(("peW", idx - 3))
        m.emit("gpsimd",
               lambda e, i=idx: e.dma_start(out=wa[i % 4][:],
                                            in_=wa_panel_src(i)),
               waits=w, incs=((f"wa{idx % 4}", 16),), raw=True)

    def wa_wait(idx):
        return (f"wa{idx % 4}", 16 * (idx // 4 + 1))

    sq_act = []
    sq_pe = []
    for c in range(NCH):
        w = [("act", sq_act[c - 4])] if c >= 4 else []
        if c == 0:
            w.append(("pl", ident_pl))
        m.emit("gpsimd",
               lambda e, c=c: e.dma_start(out=xln_ch(c),
                                          in_=x_t[c * 128:(c + 1) * 128, :]),
               waits=w, incs=((f"x{c % 4}", 16),), raw=True)
        aw = [(f"x{c % 4}", 16 * (c // 4 + 1))]
        if c >= 2:
            aw.append(("pe", sq_pe[c - 2]))
        if c % 2 == 1 and c < 9:
            emit_wa((c - 1) // 2)
        m.emit("scalar",
               lambda e, c=c: e.activation(sqb[c % 2][:], xln_ch(c), AF.Square),
               waits=aw, incs=(("act", 1),))
        sq_act.append(m.c["act"])
        stw = [("act", sq_act[c])]
        for half in range(2):
            m.mm(pk[:, half * 512:(half + 1) * 512], colones[:],
                 xln[:, c * T + half * 512: c * T + (half + 1) * 512],
                 start=(c == 0), stop=(c == NCH - 1),
                 waits=stw if half == 0 else ())
            m.mm(pk[:, 1024 + half * 512:1024 + (half + 1) * 512],
                 colones[:], sqb[c % 2][:, half * 512:(half + 1) * 512],
                 start=(c == 0), stop=(c == NCH - 1))
        m.emit("tensor", lambda e: e.nop(), incs=(("pe", 1),))
        sq_pe.append(m.c["pe"])
    stats0_pe = m.c["pe"]
    ck[8] = m.snapshot()

    def ln_rows(wait_pe):
        m.emit("vector", lambda e: e.tensor_copy(m_r[:], pk[0:1, 0:1024]),
               waits=(("pe", wait_pe),))
        m.emit("vector", lambda e: e.tensor_copy(t2_r, pk[0:1, 1024:2048]))
        m.emit("vector", lambda e: e.drain())
        m.emit("vector", lambda e: e.tensor_mul(t1_r, m_r[:], m_r[:]))
        m.emit("vector", lambda e: e.drain())
        m.emit("vector",
               lambda e: e.tensor_tensor(t2_r, t2_r, t1_r, ALU.subtract),
               incs=(("dve", 1),))
        rows_dve = m.c["dve"]
        m.emit("scalar",
               lambda e: e.activation(t1_r, t2_r, AF.Ln, bias=eps_ap[:]),
               waits=(("dve", rows_dve),))
        m.emit("scalar", lambda e: e.drain())
        m.emit("scalar",
               lambda e: e.activation(r_r[:], t1_r, AF.Exp, scale=-0.5),
               incs=(("act", 1),))
        rows_act2 = m.c["act"]
        m.emit("vector", lambda e: e.tensor_copy(m_rb[:], m_r[:]),
               waits=(("act", rows_act2),))
        m.emit("vector", lambda e: e.tensor_copy(r_rb[:], r_r[:]),
               incs=(("dve", 1),))
        return m.c["dve"]

    def ln_bcast(wait_dve):
        w = [("dve", wait_dve), ("fa", m.paw)]
        for half in range(2):
            m.mm(pa[:, half * 512:(half + 1) * 512], ones128b[:],
                 m_rb[0:1, half * 512:(half + 1) * 512], True, True,
                 waits=w if half == 0 else ())
        m.emit("tensor", lambda e: e.nop(), incs=(("pe", 1),))
        bpe_a = m.c["pe"]
        for half in range(2):
            m.mm(pb[:, half * 512:(half + 1) * 512], ones128b[:],
                 r_rb[0:1, half * 512:(half + 1) * 512], True, True,
                 waits=[("fb", m.pbw)] if half == 0 else ())
        m.emit("tensor", lambda e: e.nop(), incs=(("pe", 1),))
        bpe_b = m.c["pe"]
        m.emit("vector", lambda e: e.tensor_copy(mbb[:], pa[:]),
               waits=(("pe", bpe_a),), incs=(("fa", 1),))
        m.paw += 1
        m.emit("vector", lambda e: e.tensor_copy(rbb[:], pb[:]),
               waits=(("pe", bpe_b),), incs=(("fb", 1), ("dve", 1)))
        m.pbw += 1
        return m.c["dve"]

    rows1 = ln_rows(stats0_pe)
    ck[9] = m.snapshot()
    bc1 = ln_bcast(rows1)
    ck[10] = m.snapshot()
    ln1_dve = []
    for c in range(NCH):
        m.emit("vector",
               lambda e, c=c: e.tensor_tensor(s0b, xln_ch(c), mbb[:],
                                              ALU.subtract))
        m.emit("vector",
               lambda e, c=c: e.tensor_tensor(xln_ch(c), s0b, rbb[:],
                                              ALU.mult),
               incs=(("dve", 1),))
        ln1_dve.append(m.c["dve"])
    ln1_done = ln1_dve[-1]
    if debug:
        m.dma("sync", dbg_xln, xln[:], "dbg", waits=(("dve", ln1_done),))
    ck[1] = m.snapshot()


    for idx in range(4, 32):
        emit_wa(idx)

    pairs = [pa, pb, pk[:, 1024:2048]]
    pfree = ["fa", "fb", "fkb"]
    pattr = ["paw", "pbw", "pkbw"]
    pget = lambda pi: getattr(m, pattr[pi])
    def pinc(pi):
        setattr(m, pattr[pi], pget(pi) + 1)
    kvacc = lambda h: pk[:, (h % 2) * 512:(h % 2) * 512 + 128]

    elu_dve = [None] * H
    xb_acts = [None] * H
    silu_acts = [None] * H
    qt_pe = [None] * H
    vt_pe = [None] * H
    qtok_dve = [None] * H
    vtok_dve = [None] * H
    kv_pe = [None] * H
    kvev_dve = [None] * H

    def emit_qkproj(h):
        pi = (2 * h) % 3
        tgt = pairs[pi]
        w = [wa_wait(2 * h), (pfree[pi], pget(pi))]
        if h == 0:
            w += [("gb", bias_gb)]
        for cd in range(NCH):
            for half in range(2):
                cw = list(w) if (cd == 0 and half == 0) else []
                if h == 0 and half == 0:
                    cw.append(("dve", ln1_dve[cd]))
                m.mm(tgt[:, half * 512:(half + 1) * 512],
                     wa[(2 * h) % 4][:, cd * 128:(cd + 1) * 128],
                     xln[:, cd * T + half * 512: cd * T + (half + 1) * 512],
                     start=(cd == 0), stop=(cd == NCH - 1),
                     waits=cw)
        m.emit("tensor", lambda e: e.nop(), incs=(("pe", 1), ("peW", 1)))
        qk_pe = m.c["pe"]
        sA = scr0 if h % 2 == 0 else scr2
        sB = scr1 if h % 2 == 0 else scr3
        aw = [("pe", qk_pe)]
        if h > 1:
            aw.append(("dve", elu_dve[h - 2]))
        m.emit("scalar",
               lambda e, h=h, sA=sA, tg=tgt: e.activation(
                   sA[:], tg[:], AF.Exp, bias=bqk_c[:, h:h + 1]),
               waits=aw)
        m.emit("scalar",
               lambda e, h=h, sB=sB, tg=tgt: e.activation(
                   sB[:], tg[:], AF.Identity, bias=bqk_c[:, h:h + 1]),
               incs=(("act", 1), (pfree[pi], 1)))
        pinc(pi)
        xb_acts[h] = m.c["act"]

    def emit_elu_dve(h):
        sA = scr0 if h % 2 == 0 else scr2
        sB = scr1 if h % 2 == 0 else scr3
        m.emit("vector",
               lambda e, sA=sA: e.tensor_scalar(sA[:], sA[:], 1.0, -1.0,
                                                ALU.min, ALU.add),
               waits=(("act", xb_acts[h]),))
        m.emit("vector",
               lambda e, h=h, sA=sA, sB=sB: e.scalar_tensor_tensor(
                   qfm_ch(h), sB[:], 0.0, sA[:], ALU.max, ALU.add),
               incs=(("dve", 1),))
        elu_dve[h] = m.c["dve"]

    def emit_vproj(h):
        pi = (2 * h + 1) % 3
        tgt = pairs[pi]
        w = [wa_wait(2 * h + 1), (pfree[pi], pget(pi))]
        for cd in range(NCH):
            for half in range(2):
                m.mm(tgt[:, half * 512:(half + 1) * 512],
                     wa[(2 * h + 1) % 4][:, cd * 128:(cd + 1) * 128],
                     xln[:, cd * T + half * 512: cd * T + (half + 1) * 512],
                     start=(cd == 0), stop=(cd == NCH - 1),
                     waits=w if (cd == 0 and half == 0) else ())
        m.emit("tensor", lambda e: e.nop(), incs=(("pe", 1), ("peW", 1)))
        v_pe = m.c["pe"]
        m.emit("scalar",
               lambda e, h=h, tg=tgt: e.activation(vpan[:], tg[:], AF.Silu,
                                                   bias=bv_c[:, h:h + 1]),
               waits=(("pe", v_pe),), incs=(("act", 1), (pfree[pi], 1)))
        pinc(pi)
        silu_acts[h] = m.c["act"]

    def emit_qT(h):
        pi = (2 * h) % 3
        tgt = pairs[pi]
        w = [("dve", elu_dve[h]), (pfree[pi], pget(pi))]
        for s in range(8):
            m.mm(tgt[:, s * 128:(s + 1) * 128],
                 q_fm[:, h * T + s * 128: h * T + (s + 1) * 128],
                 ident[:], True, True, waits=w if s == 0 else ())
        m.emit("tensor", lambda e: e.nop(), incs=(("pe", 1),))
        qt_pe[h] = m.c["pe"]
        aw = [("pe", qt_pe[h])]
        if h > 0:
            aw.append(("pe", kv_pe[h - 1]))
        m.emit("vector", lambda e, tg=tgt: e.tensor_copy(qtok[:], tg[:]),
               waits=aw, incs=(("dve", 1), (pfree[pi], 1)))
        pinc(pi)
        qtok_dve[h] = m.c["dve"]

    def emit_vT(h):
        pi = (2 * h + 1) % 3
        tgt = pairs[pi]
        w = [("act", silu_acts[h]), (pfree[pi], pget(pi))]
        for s in range(8):
            m.mm(tgt[:, s * 128:(s + 1) * 128],
                 vpan[:, s * 128:(s + 1) * 128],
                 ident[:], True, True, waits=w if s == 0 else ())
        m.emit("tensor", lambda e: e.nop(), incs=(("pe", 1),))
        vt_pe[h] = m.c["pe"]
        aw = [("pe", vt_pe[h])]
        if h > 0:
            aw.append(("pe", kv_pe[h - 1]))
        m.emit("vector", lambda e, tg=tgt: e.tensor_copy(vtok[:], tg[:]),
               waits=aw, incs=(("dve", 1), (pfree[pi], 1)))
        pinc(pi)
        vtok_dve[h] = m.c["dve"]

    def emit_kv(h):
        w = [("dve", vtok_dve[h])]
        if h >= 2:
            w.append(("dve", kvev_dve[h - 2]))
        for s in range(8):
            m.mm(kvacc(h),
                 qtok[:, s * 128:(s + 1) * 128],
                 vtok[:, s * 128:(s + 1) * 128],
                 start=(s == 0), stop=(s == 7),
                 waits=w if s == 0 else ())
        m.emit("tensor", lambda e: e.nop(), incs=(("pe", 1),))
        kv_pe[h] = m.c["pe"]
        m.emit("vector",
               lambda e, h=h: e.tensor_copy(kv_f[:, h * 128:(h + 1) * 128],
                                            kvacc(h)),
               waits=(("pe", kv_pe[h]),), incs=(("dve", 1),))
        kvev_dve[h] = m.c["dve"]

    for h in range(H):
        if h > 0:
            emit_qT(h - 1)
        emit_qkproj(h)
        if h > 0:
            emit_vT(h - 1)
        emit_elu_dve(h)
        emit_vproj(h)
        if h > 0:
            emit_kv(h - 1)
        if h < 12:
            emit_gamma_pair(2 * h)
            emit_gamma_pair(2 * h + 1)
    emit_qT(H - 1)
    emit_vT(H - 1)
    emit_kv(H - 1)

    ck[2] = m.snapshot()
    m.dma("scalar", gcc_in, gcols[:], "sp", waits=(("dve", gred_dve[-1]),))
    kvevac_dve = kvev_dve[H - 1]
    m.dma("scalar", kv_in, kv_f[:], "sp", waits=(("dve", kvevac_dve),))
    kvput = m.c["sp"]
    if debug:
        m.dma("sync", dbg_kvp, kv_f[:], "dbg", waits=(("dve", kvevac_dve),))
    m.emit("gpsimd",
           lambda e: e.collective_compute(
               "AllReduce", ALU.add,
               replica_groups=[[0, 1, 2, 3, 4, 5, 6, 7]],
               ins=[gcc_in.opt()], outs=[gcc_out.opt()]),
           waits=(("sp", kvput),), incs=(("cc", 1),), raw=True)
    m.emit("gpsimd",
           lambda e: e.collective_compute(
               "AllReduce", ALU.add,
               replica_groups=[[0, 1], [2, 3], [4, 5], [6, 7]],
               ins=[kv_in.opt()], outs=[kv_out.opt()]),
           waits=(("sp", kvput),), incs=(("cc", 1),), raw=True)
    for idx in range(32, 48):
        emit_wa(idx)
    u_act = []
    for p in range(NCH):
        pi = p % 3
        tgt, fs, fw = pairs[pi], pfree[pi], pget(pi)
        w = [wa_wait(32 + p), (fs, fw)]
        for cd in range(NCH):
            for half in range(2):
                m.mm(tgt[:, half * 512:(half + 1) * 512],
                     wa[(32 + p) % 4][:, cd * 128:(cd + 1) * 128],
                     xln[:, cd * T + half * 512: cd * T + (half + 1) * 512],
                     start=(cd == 0), stop=(cd == NCH - 1),
                     waits=w if (cd == 0 and half == 0) else ())
        m.emit("tensor", lambda e: e.nop(), incs=(("pe", 1), ("peW", 1)))
        up_pe = m.c["pe"]
        m.emit("scalar",
               lambda e, p=p, tg=tgt: e.activation(ufm_ch(p), tg[:], AF.Silu,
                                                   bias=bu_c[:, p:p + 1]),
               waits=(("pe", up_pe),), incs=(("act", 1), (fs, 1)))
        pinc(pi)
        u_act.append(m.c["act"])

    m.dma("scalar", kv_f[:], kv_out, "rlk", waits=(("cc", 2),))
    m.dma("scalar", gcols[:], gcc_out, "rlk", waits=(("cc", 2),))
    rlk_done = m.c["rlk"]
    m.emit("vector", lambda e: e.tensor_copy(kv_b[:], kv_f[:]),
           waits=(("rlk", rlk_done),), incs=(("dve", 1),))
    kvb_dve = m.c["dve"]
    if debug:
        m.dma("sync", dbg_kvf, kv_f[:], "dbg", waits=(("rlk", rlk_done),))

    m.mm(pk[0:1, 0:24], onescol_f[:], gcols[:], True, True,
         waits=(("rlk", rlk_done), ("dve", kvevac_dve)))
    m.emit("tensor", lambda e: e.nop(), incs=(("pe", 1),))
    gsum_pe = m.c["pe"]
    m.emit("vector", lambda e: e.tensor_copy(trow[0:1, 0:24], pk[0:1, 0:24]),
           waits=(("pe", gsum_pe),))
    m.emit("vector", lambda e: e.drain())
    for mtx in range(3):
        m.emit("vector",
               lambda e, j=mtx: e.reduce_sum(trow[0:1, 24 + j:25 + j],
                                             trow[0:1, j * 8:(j + 1) * 8], AX))
        m.emit("vector", lambda e: e.drain())
        m.emit("vector",
               lambda e, j=mtx: e.tensor_scalar(
                   trow[0:1, 28 + j:29 + j], trow[0:1, 24 + j:25 + j],
                   0.5 / LFULL, 0.5 * QEPS, ALU.mult, ALU.add))
        m.emit("vector", lambda e: e.drain())
    m.emit("vector", lambda e: e.memset(trow[0:1, 31:32], 0.0),
           incs=(("dve", 1),))
    thr_dve = m.c["dve"]
    m.mm(pk[:, 0:4], ones128f[:], trow[0:1, 28:32], True, True,
         waits=(("dve", thr_dve),))
    m.emit("tensor", lambda e: e.nop(), incs=(("pe", 1),))
    thrb_pe = m.c["pe"]
    m.emit("vector", lambda e: e.tensor_copy(thrc[:], pk[:, 0:4]),
           waits=(("pe", thrb_pe),))
    m.emit("vector", lambda e: e.drain())
    m.emit("vector", lambda e: e.tensor_scalar_mul(nthrc[:], thrc[:], -1.0),
           incs=(("dve", 1),))
    thr_ready = m.c["dve"]
    if debug:
        m.dma("sync", dbg_thr, thrc[:], "dbg", waits=(("dve", thr_ready),))
        m.dma("sync", dbg_gc, gcols[:], "dbg", waits=(("dve", thr_ready),))
        m.dma("sync", dbg_trow, trow[:], "dbg", waits=(("dve", thr_ready),))

    ck[3] = m.snapshot()
    for idx in range(48, 64):
        emit_wa(idx)
    es_xln.close()
    ao = es_a2.enter_context(nc.sbuf_tensor("ao", [128, NCH * T], BF16, **R))
    ao_ch = lambda c: ao[:, c * T:(c + 1) * T]

    if debug:
        m.dma("sync", dbg_qfm, q_fm[:], "dbg", waits=(("dve", kvb_dve),))
    sq2_pe = {}
    for h in range(H):
        if h % 2 == 0:
            tgt, fs, fw = pa, "fa", m.paw
        else:
            tgt, fs, fw = pb, "fb", m.pbw
        w = [("dve", kvb_dve), (fs, fw)]
        for half in range(2):
            m.mm(tgt[:, half * 512:(half + 1) * 512],
                 kv_b[:, h * 128:(h + 1) * 128],
                 q_fm[:, h * T + half * 512: h * T + (half + 1) * 512],
                 True, True, waits=w if half == 0 else ())
        m.emit("tensor", lambda e: e.nop(), incs=(("pe", 1),))
        e_pe = m.c["pe"]
        m.emit("vector",
               lambda e, h=h, tg=tgt: e.tensor_copy(ao_ch(h), tg[:]),
               waits=(("pe", e_pe),), incs=((fs, 1),))
        if h % 2 == 0:
            m.paw += 1
        else:
            m.pbw += 1
        m.emit("vector",
               lambda e, h=h: e.tensor_mul(sqb[h % 2][:], ao_ch(h), ao_ch(h)),
               waits=[("pe", sq2_pe[h - 2])] if h >= 2 else (),
               incs=(("dve", 1),))
        ein_dve = m.c["dve"]
        stw = [("dve", ein_dve)]
        if h == 0:
            stw.append(("dve", thr_ready))
            stw.append(("fkb", m.pkbw))
        for half in range(2):
            m.mm(pk[:, half * 512:(half + 1) * 512], colones[:],
                 ao[:, h * T + half * 512: h * T + (half + 1) * 512],
                 start=(h == 0), stop=(h == H - 1),
                 waits=stw if half == 0 else ())
            m.mm(pk[:, 1024 + half * 512:1024 + (half + 1) * 512],
                 colones[:], sqb[h % 2][:, half * 512:(half + 1) * 512],
                 start=(h == 0), stop=(h == H - 1))
        m.emit("tensor", lambda e: e.nop(), incs=(("pe", 1),))
        sq2_pe[h] = m.c["pe"]
    stats2_pe = m.c["pe"]

    if debug:
        m.dma("sync", dbg_ao, ao[:], "dbg", waits=(("pe", stats2_pe),))
    rows2 = ln_rows(stats2_pe)
    bc2 = ln_bcast(rows2)

    z_dve = []
    for c in range(NCH):
        m.emit("vector",
               lambda e, c=c: e.tensor_tensor(s0b, ao_ch(c), mbb[:],
                                              ALU.subtract),
               waits=(("act", u_act[c]),))
        m.emit("vector",
               lambda e, c=c: e.tensor_tensor(s1b, s0b, rbb[:], ALU.mult))
        m.emit("vector",
               lambda e, c=c: e.tensor_scalar(s1b, s1b, g_c[:, c:c + 1],
                                              bln_c[:, c:c + 1], ALU.mult,
                                              ALU.add))
        m.emit("vector",
               lambda e, c=c: e.tensor_tensor(ao_ch(c), s1b, ufm_ch(c),
                                              ALU.mult),
               incs=(("dve", 1),))
        z_dve.append(m.c["dve"])

    wo_act = []
    wo_sq_pe = {}
    for p in range(NCH):
        if p % 2 == 0:
            tgt, fs, fw = pa, "fa", m.paw
        else:
            tgt, fs, fw = pb, "fb", m.pbw
        w0 = [wa_wait(48 + p), (fs, fw)]
        for cd in range(NCH):
            cw = list(w0) if cd == 0 else []
            if p == 0:
                cw.append(("dve", z_dve[cd]))
            elif p == 1 and cd == 0:
                cw.append(("dve", z_dve[-1]))
            for half in range(2):
                m.mm(tgt[:, half * 512:(half + 1) * 512],
                     wa[(48 + p) % 4][:, cd * 128:(cd + 1) * 128],
                     ao[:, cd * T + half * 512: cd * T + (half + 1) * 512],
                     start=(cd == 0), stop=(cd == NCH - 1),
                     waits=cw if half == 0 else ())
        m.emit("tensor", lambda e: e.nop(), incs=(("pe", 1), ("peW", 1)))
        wo_pe = m.c["pe"]
        m.emit("vector",
               lambda e, p=p, tg=tgt: e.tensor_scalar(
                   ufm_ch(p), tg[:], bo_c[:, p:p + 1], None, ALU.add),
               waits=(("pe", wo_pe),), incs=((fs, 1),))
        if p % 2 == 0:
            m.paw += 1
        else:
            m.pbw += 1
        m.emit("vector",
               lambda e, p=p: e.tensor_mul(sqb[p % 2][:], ufm_ch(p),
                                           ufm_ch(p)),
               waits=[("pe", wo_sq_pe[p - 2])] if p >= 2 else (),
               incs=(("dve", 1),))
        wo_act.append(m.c["dve"])
        stw = [("dve", wo_act[p])]
        for half in range(2):
            m.mm(pk[:, half * 512:(half + 1) * 512], colones[:],
                 u_fm[:, p * T + half * 512: p * T + (half + 1) * 512],
                 start=(p == 0), stop=(p == NCH - 1),
                 waits=stw if half == 0 else ())
            m.mm(pk[:, 1024 + half * 512:1024 + (half + 1) * 512],
                 colones[:], sqb[p % 2][:, half * 512:(half + 1) * 512],
                 start=(p == 0), stop=(p == NCH - 1))
        m.emit("tensor", lambda e: e.nop(), incs=(("pe", 1),))
        wo_sq_pe[p] = m.c["pe"]
    stats3_pe = m.c["pe"]

    rows3 = ln_rows(stats3_pe)
    bc3 = ln_bcast(rows3)

    ck[4] = m.snapshot()
    es_a2.close()
    es_g.close()
    es_xln3 = contextlib.ExitStack()
    xln3 = es_xln3.enter_context(nc.sbuf_tensor("xln3", [128, NCH * T], BF16,
                                                **L))
    xln3_ch = lambda c: xln3[:, c * T:(c + 1) * T]

    x2_sp = []
    c2_dve = []
    sq3_pe = []
    cast_acts = []
    for i in range(2):
        m.dma("sync", xin[i][:], x_t[i * 128:(i + 1) * 128, :], f"rlx{i}")
    for c in range(NCH):
        if c >= 2:
            m.dma("sync", xin[c % 2][:], x_t[c * 128:(c + 1) * 128, :],
                  f"rlx{c % 2}", waits=(("dve", c2_dve[c - 2]),))
        w = [(f"rlx{c % 2}", m.c[f"rlx{c % 2}"])]
        if c >= 2:
            w.append(("sp", x2_sp[c - 2]))
            w.append(("act", cast_acts[c - 2]))
        m.emit("vector",
               lambda e, c=c: e.tensor_tensor(s0b, ufm_ch(c), mbb[:],
                                              ALU.subtract), waits=w)
        m.emit("vector",
               lambda e, c=c: e.tensor_tensor(s0b, s0b, rbb[:], ALU.mult))
        m.emit("vector",
               lambda e, c=c: e.tensor_tensor(x2r[c % 2][:], xin[c % 2][:],
                                              s0b, ALU.add))
        m.emit("vector", lambda e: e.engine_nop(), incs=(("dve", 1),))
        c2_dve.append(m.c["dve"])
        m.emit("scalar",
               lambda e, c=c: e.copy(xln3_ch(c), x2r[c % 2][:]),
               waits=(("dve", c2_dve[c]),), incs=(("act", 1),))
        cast_acts.append(m.c["act"])
        m.dma("scalar", x2_d[c * 128:(c + 1) * 128, :], x2r[c % 2][:], "sp",
              waits=(("dve", c2_dve[c]),))
        x2_sp.append(m.c["sp"])
        aw = []
        if c >= 2:
            aw.append(("pe", sq3_pe[c - 2]))
        m.emit("scalar",
               lambda e, c=c: e.activation(sqb[c % 2][:], xln3_ch(c),
                                           AF.Square),
               waits=aw, incs=(("act", 1),))
        sq3_act = m.c["act"]
        stw = [("act", sq3_act)]
        for half in range(2):
            m.mm(pk[:, half * 512:(half + 1) * 512], colones[:],
                 xln3[:, c * T + half * 512: c * T + (half + 1) * 512],
                 start=(c == 0), stop=(c == NCH - 1),
                 waits=stw if half == 0 else ())
            m.mm(pk[:, 1024 + half * 512:1024 + (half + 1) * 512],
                 colones[:], sqb[c % 2][:, half * 512:(half + 1) * 512],
                 start=(c == 0), stop=(c == NCH - 1))
        m.emit("tensor", lambda e: e.nop(), incs=(("pe", 1),))
        sq3_pe.append(m.c["pe"])
    stats4_pe = m.c["pe"]

    ck[5] = m.snapshot()
    es_u.close()
    hidden = es.enter_context(nc.sbuf_tensor("hidden", [128, FCH * T], BF16,
                                             side="right"))
    es_ffn = contextlib.ExitStack()
    wf = [es_ffn.enter_context(nc.sbuf_tensor(f"wf{i}", [128, D], F32, **R))
          for i in range(2)]
    qp = [es_ffn.enter_context(nc.sbuf_tensor(f"qp{i}", [128, D], BF16, **R))
          for i in range(4)]
    hid_ch = lambda c: hidden[:, c * T:(c + 1) * T]

    q_dve = {}
    qmm_pe = {}
    hid_dve = []

    def emit_quant(ci):
        for which in range(2):
            widx = 2 * ci + which
            if widx == 0:
                w = [("pe", stats2_pe)]
            elif widx == 1:
                w = [("pe", stats3_pe)]
            else:
                w = []
            if widx >= 2:
                w.append(("dve", q_dve[widx - 2]))
            m.dma("sync", wf[widx % 2][:],
                  (w1t_d if which == 0 else w2t_d)[ci, :, :],
                  f"wf{widx % 2}", waits=w)
            aw = [(f"wf{widx % 2}", 16 * (widx // 2 + 1))]
            if widx == 0:
                aw.append(("dve", thr_ready))
            if widx >= 4:
                aw.append(("pe", qmm_pe.get(widx - 4, 0)))
            m.emit("vector",
                   lambda e, i=widx, wh=which: e.tensor_scalar(
                       qp[i % 4][:], wf[i % 2][:], thrc[:, wh:wh + 1], None,
                       ALU.is_ge),
                   waits=aw)
            m.emit("vector",
                   lambda e, i=widx, wh=which: e.tensor_scalar(
                       scr1[:].bitcast(BF16), wf[i % 2][:],
                       nthrc[:, wh:wh + 1], None, ALU.is_le))
            m.emit("vector",
                   lambda e, i=widx: e.tensor_tensor(
                       qp[i % 4][:], qp[i % 4][:], scr1[:].bitcast(BF16),
                       ALU.subtract),
                   incs=(("dve", 1),))
            q_dve[widx] = m.c["dve"]

    emit_quant(0)
    emit_quant(1)

    rows4 = ln_rows(stats4_pe)
    bc4 = ln_bcast(rows4)
    ln3_dve = []
    for c in range(NCH):
        m.emit("vector",
               lambda e, c=c: e.tensor_tensor(s0b, xln3_ch(c), mbb[:],
                                              ALU.subtract))
        m.emit("vector",
               lambda e, c=c: e.tensor_tensor(xln3_ch(c), s0b, rbb[:],
                                              ALU.mult),
               incs=(("dve", 1),))
        ln3_dve.append(m.c["dve"])
    ln3_done = ln3_dve[-1]

    for ci in range(FCH):
        if 1 <= ci + 1 < FCH:
            emit_quant(ci + 1)

        if ci % 2 == 0:
            tg1, f1, f1w = pa, "fa", m.paw
            tg2, f2, f2w = pb, "fb", m.pbw
        else:
            tg1, f1, f1w = pk[:, 0:1024], "fka", m.pkaw
            tg2, f2, f2w = pk[:, 1024:2048], "fkb", m.pkbw
        for which, tg, fs, fw in ((0, tg1, f1, f1w), (1, tg2, f2, f2w)):
            w = [("dve", q_dve[2 * ci + which]), (fs, fw)]
            for cd in range(NCH):
                for half in range(2):
                    cw = list(w) if (cd == 0 and half == 0) else []
                    if ci == 0 and which == 0 and half == 0:
                        cw.append(("dve", ln3_dve[cd]))
                    m.mm(tg[:, half * 512:(half + 1) * 512],
                         qp[(2 * ci + which) % 4][:, cd * 128:(cd + 1) * 128],
                         xln3[:, cd * T + half * 512: cd * T + (half + 1) * 512],
                         start=(cd == 0), stop=(cd == NCH - 1),
                         waits=cw)
            m.emit("tensor", lambda e: e.nop(), incs=(("pe", 1),))
            qmm_pe[2 * ci + which] = m.c["pe"]
        aw = [("pe", qmm_pe[2 * ci + 1])]
        if ci >= 1:
            aw.append(("dve", hid_dve[ci - 1]))
        m.emit("scalar",
               lambda e, tg=tg1: e.activation(scr0[:], tg[:], AF.Sigmoid),
               waits=aw, incs=(("act", 1), (f1, 1)))
        if f1 == "fa":
            m.paw += 1
        else:
            m.pkaw += 1
        sig_act = m.c["act"]
        m.emit("vector",
               lambda e, ci=ci, tg=tg2: e.tensor_tensor(hid_ch(ci), scr0[:],
                                                        tg[:], ALU.mult),
               waits=(("act", sig_act),), incs=(("dve", 1), (f2, 1)))
        if f2 == "fb":
            m.pbw += 1
        else:
            m.pkbw += 1
        hid_dve.append(m.c["dve"])
    last_x2mm_pe = qmm_pe[2 * FCH - 1]

    ck[6] = m.snapshot()
    es_ffn.close()
    es_xln3.close()
    w3f = es_ffn.enter_context(nc.sbuf_tensor("w3f", [128, FF], F32, **R))
    q3p = [es_ffn.enter_context(nc.sbuf_tensor(f"q3p{i}", [128, FF], BF16, **R))
           for i in range(2)]
    q3tmp = es_ffn.enter_context(nc.sbuf_tensor("q3tmp", [128, FF], BF16, **R))

    o3_sp = []
    q3_dve = {}
    o3_pe = {}
    o3sq_pe = {}
    o3_act = []
    for co in range(NCH):
        w = [("pe", last_x2mm_pe)] if co == 0 else [("dve", q3_dve[co - 1])]
        m.dma("sync", w3f[:], w3t_d[co, :, :], "w3", waits=w)
        aw = [("w3", 16 * (co + 1))]
        if co >= 2:
            aw.append(("pe", o3_pe[co - 2]))
        m.emit("vector",
               lambda e, co=co: e.tensor_scalar(
                   q3p[co % 2][:], w3f[:], thrc[:, 2:3], None, ALU.is_ge),
               waits=aw)
        m.emit("vector",
               lambda e: e.tensor_scalar(q3tmp[:], w3f[:], nthrc[:, 2:3],
                                         None, ALU.is_le))
        m.emit("vector",
               lambda e, co=co: e.tensor_tensor(
                   q3p[co % 2][:], q3p[co % 2][:], q3tmp[:], ALU.subtract),
               incs=(("dve", 1),))
        q3_dve[co] = m.c["dve"]

        if co % 2 == 0:
            tgt, fs, fw = pa, "fa", m.paw
        else:
            tgt, fs, fw = pb, "fb", m.pbw
        w0 = [("dve", q3_dve[co]), (fs, fw)]
        for cj in range(FCH):
            cw = list(w0) if cj == 0 else []
            if co == 0:
                cw.append(("dve", hid_dve[cj]))
            for half in range(2):
                m.mm(tgt[:, half * 512:(half + 1) * 512],
                     q3p[co % 2][:, cj * 128:(cj + 1) * 128],
                     hidden[:, cj * T + half * 512: cj * T + (half + 1) * 512],
                     start=(cj == 0), stop=(cj == FCH - 1),
                     waits=cw if half == 0 else ())
        m.emit("tensor", lambda e: e.nop(), incs=(("pe", 1),))
        o3_pe[co] = m.c["pe"]
        aw = [("pe", o3_pe[co])]
        if co >= 1:
            aw.append(("sp", o3_sp[co - 1]))
        m.emit("scalar",
               lambda e, co=co, tg=tgt: e.activation(arb2[co % 2][:], tg[:],
                                                     AF.Identity,
                                                     bias=zcol[:]),
               waits=aw, incs=((fs, 1),))
        if co % 2 == 0:
            m.paw += 1
        else:
            m.pbw += 1
        m.emit("scalar",
               lambda e, co=co: e.activation(sqb[co % 2][:], arb2[co % 2][:],
                                             AF.Square),
               waits=[("pe", o3sq_pe[co - 2])] if co >= 2 else (),
               incs=(("act", 1),))
        o3_act.append(m.c["act"])
        m.dma("scalar", o3_d[co * 128:(co + 1) * 128, :], arb2[co % 2][:],
              "sp", waits=(("act", o3_act[co]),))
        o3_sp.append(m.c["sp"])
        stw = [("act", o3_act[co])]
        for half in range(2):
            m.mm(pk[:, half * 512:(half + 1) * 512], colones[:],
                 arb2[co % 2][:, half * 512:(half + 1) * 512],
                 start=(co == 0), stop=(co == NCH - 1),
                 waits=stw if half == 0 else ())
            m.mm(pk[:, 1024 + half * 512:1024 + (half + 1) * 512],
                 colones[:], sqb[co % 2][:, half * 512:(half + 1) * 512],
                 start=(co == 0), stop=(co == NCH - 1))
        m.emit("tensor", lambda e: e.nop(), incs=(("pe", 1),))
        o3sq_pe[co] = m.c["pe"]
    stats5_pe = m.c["pe"]

    o3r = [es.enter_context(nc.sbuf_tensor(f"o3r{i}", [128, T], BF16, **L))
           for i in range(4)]
    for c in range(2):
        m.dma("scalar", x2r[c][:], x2_d[c * 128:(c + 1) * 128, :],
              f"rlx{c}", waits=(("sp", x2_sp[-1]), ("act", cast_acts[-1])))
    for c in range(4):
        m.dma("scalar", o3r[c][:], o3_d[c * 128:(c + 1) * 128, :],
              f"rla{c % 2}", waits=(("sp", o3_sp[c]),))

    rows5 = ln_rows(stats5_pe)
    bc5 = ln_bcast(rows5)

    ck[7] = m.snapshot()
    es_ffn.close()
    outb = [es_ffn.enter_context(nc.sbuf_tensor(f"outb{i}", [128, T], F32,
                                                **R))
            for i in range(2)]
    fin_dve = []
    for c in range(NCH):
        if c >= 2:
            m.dma("scalar", x2r[c % 2][:], x2_d[c * 128:(c + 1) * 128, :],
                  f"rlx{c % 2}", waits=(("dve", fin_dve[c - 2]),))
        if c >= 4:
            m.dma("scalar", o3r[c % 4][:], o3_d[c * 128:(c + 1) * 128, :],
                  f"rla{c % 2}", waits=(("dve", fin_dve[c - 4]),))
        w = [(f"rlx{c % 2}", m.c[f"rlx{c % 2}"]),
             (f"rla{c % 2}", m.c[f"rla{c % 2}"])]
        if c >= 2:
            w.append(("st", 16 * (c - 1)))
        m.emit("vector",
               lambda e, c=c: e.tensor_tensor(s0b, o3r[c % 4][:], mbb[:],
                                              ALU.subtract), waits=w)
        m.emit("vector",
               lambda e, c=c: e.tensor_tensor(s0b, s0b, rbb[:], ALU.mult))
        m.emit("vector",
               lambda e, c=c: e.tensor_tensor(outb[c % 2][:], x2r[c % 2][:],
                                              s0b, ALU.add),
               incs=(("dve", 1),))
        fin_dve.append(m.c["dve"])
        m.dma("sync", out_t[c * 128:(c + 1) * 128, :], outb[c % 2][:], "st",
              waits=(("dve", fin_dve[c]),))
    fw = [("st", 16 * NCH)]
    if debug:
        fw.append(("dbg", m.c["dbg"]))
    m.emit("gpsimd", lambda e: e.nop(), waits=fw)
    if phases < 99:
        m.truncate(ck[phases])
        m.c["st"] = 0
        for c in range(NCH):
            m.dma("sync", out_t[c * 128:(c + 1) * 128, :], mb_f[:], "st")
        def _final(e):
            e.wait_ge(m.sems["st"], 16 * NCH)
            return e.nop()
        m.ops["sync"].append(lambda e: _final(e))
        m.meta["sync"].append((((("st", 16 * NCH)),), (), "final"))
    build_program.last_mk = m

    with nc.Block() as block:
        @block.sync
        def _(e):
            for f in m.ops["sync"]:
                f(e)

        @block.scalar
        def _(e):
            for f in m.ops["scalar"]:
                f(e)

        @block.vector
        def _(e):
            for f in m.ops["vector"]:
                f(e)

        @block.tensor
        def _(e):
            for f in m.ops["tensor"]:
                f(e)

        @block.gpsimd
        def _(e):
            for f in m.ops["gpsimd"]:
                f(e)

    es_ffn.close()
    es_xln3.close()
    es_a2.close()
    m.es.close()
    es_a.close()
    return nc


_CACHE = {}


def _stage(inputs):
    x = np.ascontiguousarray(np.asarray(inputs["L0_attn_x"], np.float32))
    Wqk = np.asarray(inputs["L0_attn_Wqk"], np.float32)
    Wv = np.asarray(inputs["L0_attn_Wv"], np.float32)
    Wu = np.asarray(inputs["L0_attn_Wu"], np.float32)
    Wo = np.asarray(inputs["L0_attn_Wo"], np.float32)
    w1 = np.asarray(inputs["L0_ffn_w1"], np.float32)
    w2 = np.asarray(inputs["L0_ffn_w2"], np.float32)
    w3 = np.asarray(inputs["L0_ffn_w3"], np.float32)

    def attn_w(W):
        return np.ascontiguousarray(
            W.reshape(NCH, 128, NCH, 128).transpose(2, 1, 0, 3)
        ).reshape(NCH, 128, D)

    def ffn_w12(W):
        return np.ascontiguousarray(
            W.reshape(FCH, 128, NCH, 128).transpose(0, 3, 2, 1)
        ).reshape(FCH, 128, D)

    def ffn_w3(W):
        return np.ascontiguousarray(
            W.reshape(NCH, 128, FCH, 128).transpose(0, 3, 2, 1)
        ).reshape(NCH, 128, FF)

    def col(v):
        return np.ascontiguousarray(
            np.asarray(v, np.float32).reshape(NCH, 128).T)

    shared = {
        "wqk": attn_w(Wqk), "wv": attn_w(Wv), "wu": attn_w(Wu),
        "wo": attn_w(Wo),
        "bqk": col(inputs["L0_attn_bqk"]), "bv": col(inputs["L0_attn_bv"]),
        "bu": col(inputs["L0_attn_bu"]), "bo": col(inputs["L0_attn_bo"]),
        "lng": col(inputs["L0_attn_ln_g"]), "lnb": col(inputs["L0_attn_ln_b"]),
        "w1t": ffn_w12(w1), "w2t": ffn_w12(w2), "w3t": ffn_w3(w3),
    }
    flats = (shared["w1t"].reshape(-1), shared["w2t"].reshape(-1),
             shared["w3t"].reshape(-1))
    in_maps = []
    for core in range(8):
        b, half = divmod(core, 2)
        d = dict(shared)
        d["x_t"] = np.ascontiguousarray(x[b, half * T:(half + 1) * T, :].T)
        for j, name in enumerate(("gs1", "gs2", "gs3")):
            d[name] = np.ascontiguousarray(
                flats[j][core * L8:(core + 1) * L8].reshape(128, GW))
        in_maps.append(d)
    return in_maps


def kernel(**inputs):
    in_maps = _stage(inputs)
    if "nc" not in _CACHE:
        _CACHE["nc"] = build_program()
    nc = _CACHE["nc"]
    res = run_bass_kernel_spmd(nc, in_maps, core_ids=list(range(8)))
    out = np.empty((B, SEQ, D), np.float32)
    for core in range(8):
        b, half = divmod(core, 2)
        out[b, half * T:(half + 1) * T, :] = res.results[core]["out_t"].T
    return out


# version 28
# speedup vs baseline: 1.0759x; 1.0190x over previous
import contextlib
from collections import defaultdict

import numpy as np

import concourse.bass as bass
import concourse.mybir as mybir
from concourse.bass_utils import run_bass_kernel_spmd

F32 = mybir.dt.float32
BF16 = mybir.dt.bfloat16
AF = mybir.ActivationFunctionType
ALU = mybir.AluOpType
AX = mybir.AxisListType.X

B = 4
SEQ = 2048
D = 2048
T = 1024
NCH = 16
H = 16
FF = 5632
FCH = 44
LN_EPS = 1e-5
QEPS = 1e-8
LFULL = FF * D
L8 = LFULL // 8
GW = L8 // 128
GTS = 8
GTW = GW // GTS


class MK:
    def __init__(self, nc):
        self.nc = nc
        self.ops = {k: [] for k in ("sync", "scalar", "vector", "tensor", "gpsimd")}
        self.meta = {k: [] for k in ("sync", "scalar", "vector", "tensor", "gpsimd")}
        self.c = defaultdict(int)
        self.sems = {}
        self.es = contextlib.ExitStack()
        self.paw = 0
        self.pbw = 0
        self.pkaw = 0
        self.pkbw = 0

    def snapshot(self):
        return {k: len(v) for k, v in self.ops.items()}

    def truncate(self, snap):
        for k in self.ops:
            self.ops[k] = self.ops[k][:snap[k]]
            self.meta[k] = self.meta[k][:snap[k]]

    def sem(self, name):
        self.sems[name] = self.es.enter_context(self.nc.semaphore(name))

    def emit(self, eng, build, waits=(), incs=(), raw=False):
        w = tuple((s, v) for (s, v) in waits if v > 0)
        for s, a in incs:
            self.c[s] += a
        i = tuple(incs)

        def fn(e, _b=build, _w=w, _i=i, _raw=raw):
            for s, v in _w:
                e.wait_ge(self.sems[s], v)
            inst = _b(e)
            if _i:
                if _raw:
                    inst.then_inc(self.sems[_i[0][0]], _i[0][1])
                    rest = _i[1:]
                else:
                    e.drain().then_inc(self.sems[_i[0][0]], _i[0][1])
                    rest = _i[1:]
                for s, a in rest:
                    e.nop().then_inc(self.sems[s], a)
        self.ops[eng].append(fn)
        self.meta[eng].append((w, i, getattr(build, "__qualname__", "?")))

    def dma(self, eng, out, in_, sem, waits=()):
        self.emit(eng, lambda e, o=out, x=in_: e.dma_start(out=o, in_=x),
                  waits=waits, incs=((sem, 16),), raw=True)

    def mm(self, out, lhsT, rhs, start, stop, waits=(), incs=()):
        self.emit("tensor",
                  lambda e, o=out, l=lhsT, r=rhs, s0=start, s1=stop:
                      e.matmul(o, l, r, start=s0, stop=s1, skip_group_check=True),
                  waits=waits, incs=incs)


def build_program(phases=99, debug=False):
    nc = bass.Bass("TRN2", target_bir_lowering=False, detect_race_conditions=False)
    m = MK(nc)
    es = m.es

    x_t = nc.dram_tensor("x_t", [D, T], F32, kind="ExternalInput").ap()
    wqk_d = nc.dram_tensor("wqk", [NCH, 128, D], F32, kind="ExternalInput").ap()
    wv_d = nc.dram_tensor("wv", [NCH, 128, D], F32, kind="ExternalInput").ap()
    wu_d = nc.dram_tensor("wu", [NCH, 128, D], F32, kind="ExternalInput").ap()
    wo_d = nc.dram_tensor("wo", [NCH, 128, D], F32, kind="ExternalInput").ap()
    bqk_d = nc.dram_tensor("bqk", [128, NCH], F32, kind="ExternalInput").ap()
    bv_d = nc.dram_tensor("bv", [128, NCH], F32, kind="ExternalInput").ap()
    bu_d = nc.dram_tensor("bu", [128, NCH], F32, kind="ExternalInput").ap()
    bo_d = nc.dram_tensor("bo", [128, NCH], F32, kind="ExternalInput").ap()
    lng_d = nc.dram_tensor("lng", [128, NCH], F32, kind="ExternalInput").ap()
    lnb_d = nc.dram_tensor("lnb", [128, NCH], F32, kind="ExternalInput").ap()
    w1t_d = nc.dram_tensor("w1t", [FCH, 128, D], F32, kind="ExternalInput").ap()
    w2t_d = nc.dram_tensor("w2t", [FCH, 128, D], F32, kind="ExternalInput").ap()
    w3t_d = nc.dram_tensor("w3t", [NCH, 128, FF], F32, kind="ExternalInput").ap()
    gs1_d = nc.dram_tensor("gs1", [128, GW], F32, kind="ExternalInput").ap()
    gs2_d = nc.dram_tensor("gs2", [128, GW], F32, kind="ExternalInput").ap()
    gs3_d = nc.dram_tensor("gs3", [128, GW], F32, kind="ExternalInput").ap()
    out_t = nc.dram_tensor("out_t", [D, T], F32, kind="ExternalOutput").ap()

    gcc_in = nc.dram_tensor("gcc_in", [128, 24], F32, kind="Internal").ap()
    gcc_out = nc.dram_tensor("gcc_out", [128, 24], F32, kind="Internal").ap()
    kv_in = nc.dram_tensor("kv_in", [128, D], F32, kind="Internal").ap()
    kv_out = nc.dram_tensor("kv_out", [128, D], F32, kind="Internal").ap()
    x2_d = nc.dram_tensor("x2_d", [D, T], F32, kind="Internal").ap()
    o3_d = nc.dram_tensor("o3_d", [D, T], BF16, kind="Internal").ap()

    for s in ("x0", "x1", "x2", "x3", "wa0", "wa1", "wa2", "wa3", "wf0", "wf1", "wf2",
              "w3", "g0", "g1", "gb", "sp", "rlx0", "rlx1", "rla0", "rla1",
              "rlk", "cc", "pe", "peW", "act", "dve", "pl", "st",
              "fa", "fb", "fka", "fkb", "dbg"):
        m.sem(s)
    if debug:
        dbg_kvp = nc.dram_tensor("dbg_kvp", [128, D], F32, kind="ExternalOutput").ap()
        dbg_kvf = nc.dram_tensor("dbg_kvf", [128, D], F32, kind="ExternalOutput").ap()
        dbg_thr = nc.dram_tensor("dbg_thr", [128, 4], F32, kind="ExternalOutput").ap()
        dbg_gc = nc.dram_tensor("dbg_gc", [128, 24], F32, kind="ExternalOutput").ap()
        dbg_qfm = nc.dram_tensor("dbg_qfm", [128, 16384], BF16, kind="ExternalOutput").ap()
        dbg_xln = nc.dram_tensor("dbg_xln", [128, 16384], BF16, kind="ExternalOutput").ap()
        dbg_trow = nc.dram_tensor("dbg_trow", [1, 32], F32, kind="ExternalOutput").ap()
        dbg_ao = nc.dram_tensor("dbg_ao", [128, 16384], BF16, kind="ExternalOutput").ap()

    L = dict(side="left")
    ident = es.enter_context(nc.sbuf_tensor("ident", [128, 128], BF16, **L))
    onest = es.enter_context(nc.sbuf_tensor("onest", [128, 128], BF16, **L))
    colones = es.enter_context(nc.sbuf_tensor("colones", [128, 128], BF16, **L))
    invd = es.enter_context(nc.sbuf_tensor("invd", [128, 128], BF16, **L))
    ones128b = es.enter_context(nc.sbuf_tensor("ones128b", [1, 128], BF16, **L))
    m_rb = es.enter_context(nc.sbuf_tensor("m_rb", [1, T], BF16, **L))
    r_rb = es.enter_context(nc.sbuf_tensor("r_rb", [1, T], BF16, **L))
    ones128f = es.enter_context(nc.sbuf_tensor("ones128f", [1, 128], F32, **L))
    onescol_b = es.enter_context(nc.sbuf_tensor("onescol_b", [128, 1], BF16, **L))
    onescol_f = es.enter_context(nc.sbuf_tensor("onescol_f", [128, 1], F32, **L))
    eps_ap = es.enter_context(nc.sbuf_tensor("eps_ap", [1, 1], F32, **L))
    zcol = es.enter_context(nc.sbuf_tensor("zcol", [128, 1], F32, **L))
    bqk_c = es.enter_context(nc.sbuf_tensor("bqk_c", [128, NCH], F32, **L))
    bv_c = es.enter_context(nc.sbuf_tensor("bv_c", [128, NCH], F32, **L))
    bu_c = es.enter_context(nc.sbuf_tensor("bu_c", [128, NCH], F32, **L))
    bo_c = es.enter_context(nc.sbuf_tensor("bo_c", [128, NCH], F32, **L))
    g_c = es.enter_context(nc.sbuf_tensor("g_c", [128, NCH], F32, **L))
    bln_c = es.enter_context(nc.sbuf_tensor("bln_c", [128, NCH], F32, **L))
    thrc = es.enter_context(nc.sbuf_tensor("thrc", [128, 4], F32, **L))
    nthrc = es.enter_context(nc.sbuf_tensor("nthrc", [128, 4], F32, **L))
    gcols = es.enter_context(nc.sbuf_tensor("gcols", [128, 24], F32, **L))
    m_r = es.enter_context(nc.sbuf_tensor("m_r", [1, T], F32, **L))
    r_r = es.enter_context(nc.sbuf_tensor("r_r", [1, T], F32, **L))
    trow = es.enter_context(nc.sbuf_tensor("trow", [1, 32], F32, **L))
    mbb = es.enter_context(nc.sbuf_tensor("mbb", [128, T], BF16, **L))
    rbb = es.enter_context(nc.sbuf_tensor("rbb", [128, T], BF16, **L))
    arb2 = [es.enter_context(nc.sbuf_tensor(f"arb2{i}", [128, T], BF16, **L))
            for i in range(2)]
    x2r = [es.enter_context(nc.sbuf_tensor(f"x2r{i}", [128, T], F32, **L))
           for i in range(2)]
    scr0 = es.enter_context(nc.sbuf_tensor("scr0", [128, T], F32, **L))
    scr1 = es.enter_context(nc.sbuf_tensor("scr1", [128, T], F32, **L))
    sqb0 = es.enter_context(nc.sbuf_tensor("sqb0", [128, T], BF16, **L))
    sqb1 = es.enter_context(nc.sbuf_tensor("sqb1", [128, T], BF16, **L))
    sqb = [sqb0, sqb1]
    t1_r = scr0[0:1, :]
    t2_r = scr1[0:1, :]
    s0b = scr0[:].bitcast(BF16)[:, 0:T]
    s1b = scr1[:].bitcast(BF16)[:, 0:T]

    es_g = contextlib.ExitStack()
    gbuf = [es_g.enter_context(nc.sbuf_tensor(f"gbuf{i}", [128, GTW], F32, **L))
            for i in range(2)]

    R = dict(side="right")
    es_a = contextlib.ExitStack()
    xin = [es_a.enter_context(nc.sbuf_tensor(f"xin{i}", [128, T], F32, **R))
           for i in range(2)]
    es_u = contextlib.ExitStack()
    u_fm = es_u.enter_context(nc.sbuf_tensor("u_fm", [128, NCH * T], BF16, **R))
    es_a2 = contextlib.ExitStack()
    kv_f = es_a2.enter_context(nc.sbuf_tensor("kv_f", [128, D], F32, **R))
    kv_b = es_a2.enter_context(nc.sbuf_tensor("kv_b", [128, D], BF16, **R))
    wa = [es_a2.enter_context(nc.sbuf_tensor(f"wa{i}", [128, D], BF16, **R))
          for i in range(4)]
    scr2 = es_a2.enter_context(nc.sbuf_tensor("scr2", [128, T], F32, **R))
    scr3 = es_a2.enter_context(nc.sbuf_tensor("scr3", [128, T], F32, **R))
    q_fm = es_a2.enter_context(nc.sbuf_tensor("q_fm", [128, NCH * T], BF16, **R))
    es_xln = contextlib.ExitStack()
    xln = es_xln.enter_context(nc.sbuf_tensor("xln", [128, NCH * T], BF16, **R))
    vpan = es_xln.enter_context(nc.sbuf_tensor("vpan", [128, T], BF16, **R))
    qtok = es_xln.enter_context(nc.sbuf_tensor("qtok", [128, T], BF16, **R))
    vtok = es_xln.enter_context(nc.sbuf_tensor("vtok", [128, T], BF16, **R))

    pk = es.enter_context(nc.psum_tensor("pk", [128, 2048], F32))
    pa = es.enter_context(nc.psum_tensor("pa", [128, 1024], F32))
    pb = es.enter_context(nc.psum_tensor("pb", [128, 1024], F32))

    xln_ch = lambda c: xln[:, c * T:(c + 1) * T]
    qfm_ch = lambda c: q_fm[:, c * T:(c + 1) * T]
    ufm_ch = lambda c: u_fm[:, c * T:(c + 1) * T]

    m.emit("vector", lambda e: e.memset(onest[:], 1.0))
    m.emit("vector", lambda e: e.memset(invd[:], 1.0 / D))
    m.emit("vector", lambda e: e.memset(ones128f[:], 1.0))
    m.emit("vector", lambda e: e.memset(ones128b[:], 1.0))
    m.emit("vector", lambda e: e.memset(onescol_b[:], 1.0))
    m.emit("vector", lambda e: e.memset(onescol_f[:], 1.0))
    m.emit("vector", lambda e: e.memset(zcol[:], 0.0))
    m.emit("vector", lambda e: e.memset(eps_ap[:], LN_EPS), incs=(("dve", 1),))
    consts_dve = m.c["dve"]
    m.emit("gpsimd",
           lambda e: e.affine_select(ident[:], onest[:], [[-1, 128]],
                                     ALU.is_equal, 0.0, base=0,
                                     channel_multiplier=1),
           waits=(("dve", consts_dve),))
    m.emit("gpsimd",
           lambda e: e.affine_select(colones[:], invd[:], [[1, 128]],
                                     ALU.is_equal, 0.0, base=0,
                                     channel_multiplier=0),
           incs=(("pl", 1),))
    ident_pl = m.c["pl"]
    for dst, srcd in ((bqk_c, bqk_d), (bv_c, bv_d), (bu_c, bu_d), (bo_c, bo_d),
                     (g_c, lng_d), (bln_c, lnb_d)):
        m.dma("scalar", dst[:], srcd, "gb")
    bias_gb = m.c["gb"]
    ck = {}
    ck[0] = m.snapshot()

    gsrc = (gs1_d, gs2_d, gs3_d)
    gred_dve = []

    def emit_gamma_pair(idx):
        mtx, k = divmod(idx, GTS)
        gsem = f"g{idx % 2}"
        srcap = gsrc[mtx][:, k * GTW:(k + 1) * GTW]
        waits = [("dve", gred_dve[idx - 2])] if idx >= 2 else []
        m.dma("scalar", gbuf[idx % 2][:], srcap, gsem, waits=waits)
        m.emit("vector",
               lambda e, i=idx: e.tensor_reduce(
                   gcols[:, i:i + 1], gbuf[i % 2][:], AX, ALU.add,
                   apply_absolute_value=True),
               waits=((gsem, 16 * (idx // 2 + 1)),),
               incs=(("dve", 1),))
        gred_dve.append(m.c["dve"])

    def wa_panel_src(idx):
        if idx < 32:
            h, which = divmod(idx, 2)
            return (wqk_d if which == 0 else wv_d)[h, :, :]
        if idx < 48:
            return wu_d[idx - 32, :, :]
        return wo_d[idx - 48, :, :]

    def emit_wa(idx, extra_waits=()):
        w = list(extra_waits)
        if idx >= 4:
            w.append(("peW", idx - 3))
        m.emit("gpsimd",
               lambda e, i=idx: e.dma_start(out=wa[i % 4][:],
                                            in_=wa_panel_src(i)),
               waits=w, incs=((f"wa{idx % 4}", 16),), raw=True)

    def wa_wait(idx):
        return (f"wa{idx % 4}", 16 * (idx // 4 + 1))

    sq_act = []
    sq_pe = []
    for c in range(NCH):
        w = [("act", sq_act[c - 4])] if c >= 4 else []
        if c == 0:
            w.append(("pl", ident_pl))
        m.emit("gpsimd",
               lambda e, c=c: e.dma_start(out=xln_ch(c),
                                          in_=x_t[c * 128:(c + 1) * 128, :]),
               waits=w, incs=((f"x{c % 4}", 16),), raw=True)
        aw = [(f"x{c % 4}", 16 * (c // 4 + 1))]
        if c >= 2:
            aw.append(("pe", sq_pe[c - 2]))
        if c % 2 == 1 and c < 9:
            emit_wa((c - 1) // 2)
        m.emit("scalar",
               lambda e, c=c: e.activation(sqb[c % 2][:], xln_ch(c), AF.Square),
               waits=aw, incs=(("act", 1),))
        sq_act.append(m.c["act"])
        stw = [("act", sq_act[c])]
        for half in range(2):
            m.mm(pk[:, half * 512:(half + 1) * 512], colones[:],
                 xln[:, c * T + half * 512: c * T + (half + 1) * 512],
                 start=(c == 0), stop=(c == NCH - 1),
                 waits=stw if half == 0 else ())
            m.mm(pk[:, 1024 + half * 512:1024 + (half + 1) * 512],
                 colones[:], sqb[c % 2][:, half * 512:(half + 1) * 512],
                 start=(c == 0), stop=(c == NCH - 1))
        m.emit("tensor", lambda e: e.nop(), incs=(("pe", 1),))
        sq_pe.append(m.c["pe"])
    stats0_pe = m.c["pe"]
    ck[8] = m.snapshot()

    def ln_rows(wait_pe):
        m.emit("vector", lambda e: e.tensor_copy(m_r[:], pk[0:1, 0:1024]),
               waits=(("pe", wait_pe),))
        m.emit("vector", lambda e: e.tensor_copy(t2_r, pk[0:1, 1024:2048]))
        m.emit("vector", lambda e: e.drain())
        m.emit("vector", lambda e: e.tensor_mul(t1_r, m_r[:], m_r[:]))
        m.emit("vector", lambda e: e.drain())
        m.emit("vector",
               lambda e: e.tensor_tensor(t2_r, t2_r, t1_r, ALU.subtract),
               incs=(("dve", 1),))
        rows_dve = m.c["dve"]
        m.emit("scalar",
               lambda e: e.activation(t1_r, t2_r, AF.Ln, bias=eps_ap[:]),
               waits=(("dve", rows_dve),))
        m.emit("scalar", lambda e: e.drain())
        m.emit("scalar",
               lambda e: e.activation(r_r[:], t1_r, AF.Exp, scale=-0.5),
               incs=(("act", 1),))
        rows_act2 = m.c["act"]
        m.emit("vector", lambda e: e.tensor_copy(m_rb[:], m_r[:]),
               waits=(("act", rows_act2),))
        m.emit("vector", lambda e: e.tensor_copy(r_rb[:], r_r[:]),
               incs=(("dve", 1),))
        return m.c["dve"]

    def ln_bcast(wait_dve):
        w = [("dve", wait_dve), ("fa", m.paw)]
        for half in range(2):
            m.mm(pa[:, half * 512:(half + 1) * 512], ones128b[:],
                 m_rb[0:1, half * 512:(half + 1) * 512], True, True,
                 waits=w if half == 0 else ())
        m.emit("tensor", lambda e: e.nop(), incs=(("pe", 1),))
        bpe_a = m.c["pe"]
        for half in range(2):
            m.mm(pb[:, half * 512:(half + 1) * 512], ones128b[:],
                 r_rb[0:1, half * 512:(half + 1) * 512], True, True,
                 waits=[("fb", m.pbw)] if half == 0 else ())
        m.emit("tensor", lambda e: e.nop(), incs=(("pe", 1),))
        bpe_b = m.c["pe"]
        m.emit("vector", lambda e: e.tensor_copy(mbb[:], pa[:]),
               waits=(("pe", bpe_a),), incs=(("fa", 1),))
        m.paw += 1
        m.emit("vector", lambda e: e.tensor_copy(rbb[:], pb[:]),
               waits=(("pe", bpe_b),), incs=(("fb", 1), ("dve", 1)))
        m.pbw += 1
        return m.c["dve"]

    rows1 = ln_rows(stats0_pe)
    ck[9] = m.snapshot()
    bc1 = ln_bcast(rows1)
    ck[10] = m.snapshot()
    ln1_dve = []
    for c in range(NCH):
        m.emit("vector",
               lambda e, c=c: e.tensor_tensor(s0b, xln_ch(c), mbb[:],
                                              ALU.subtract))
        m.emit("vector",
               lambda e, c=c: e.tensor_tensor(xln_ch(c), s0b, rbb[:],
                                              ALU.mult),
               incs=(("dve", 1),))
        ln1_dve.append(m.c["dve"])
    ln1_done = ln1_dve[-1]
    if debug:
        m.dma("sync", dbg_xln, xln[:], "dbg", waits=(("dve", ln1_done),))
    ck[1] = m.snapshot()


    for idx in range(4, 32):
        emit_wa(idx)

    pairs = [pa, pb, pk[:, 1024:2048]]
    pfree = ["fa", "fb", "fkb"]
    pattr = ["paw", "pbw", "pkbw"]
    pget = lambda pi: getattr(m, pattr[pi])
    def pinc(pi):
        setattr(m, pattr[pi], pget(pi) + 1)
    kvacc = lambda h: pk[:, (h % 2) * 512:(h % 2) * 512 + 128]

    elu_dve = [None] * H
    xb_acts = [None] * H
    silu_acts = [None] * H
    qt_pe = [None] * H
    vt_pe = [None] * H
    qtok_dve = [None] * H
    vtok_dve = [None] * H
    kv_pe = [None] * H
    kvev_dve = [None] * H

    def emit_qkproj(h):
        pi = (2 * h) % 3
        tgt = pairs[pi]
        w = [wa_wait(2 * h), (pfree[pi], pget(pi))]
        if h == 0:
            w += [("gb", bias_gb)]
        for cd in range(NCH):
            for half in range(2):
                cw = list(w) if (cd == 0 and half == 0) else []
                if h == 0 and half == 0:
                    cw.append(("dve", ln1_dve[cd]))
                m.mm(tgt[:, half * 512:(half + 1) * 512],
                     wa[(2 * h) % 4][:, cd * 128:(cd + 1) * 128],
                     xln[:, cd * T + half * 512: cd * T + (half + 1) * 512],
                     start=(cd == 0), stop=(cd == NCH - 1),
                     waits=cw)
        m.emit("tensor", lambda e: e.nop(), incs=(("pe", 1), ("peW", 1)))
        qk_pe = m.c["pe"]
        sA = scr0 if h % 2 == 0 else scr2
        sB = scr1 if h % 2 == 0 else scr3
        aw = [("pe", qk_pe)]
        if h > 1:
            aw.append(("dve", elu_dve[h - 2]))
        m.emit("scalar",
               lambda e, h=h, sA=sA, tg=tgt: e.activation(
                   sA[:], tg[:], AF.Exp, bias=bqk_c[:, h:h + 1]),
               waits=aw)
        m.emit("scalar",
               lambda e, h=h, sB=sB, tg=tgt: e.activation(
                   sB[:], tg[:], AF.Identity, bias=bqk_c[:, h:h + 1]),
               incs=(("act", 1), (pfree[pi], 1)))
        pinc(pi)
        xb_acts[h] = m.c["act"]

    def emit_elu_dve(h):
        sA = scr0 if h % 2 == 0 else scr2
        sB = scr1 if h % 2 == 0 else scr3
        m.emit("vector",
               lambda e, sA=sA: e.tensor_scalar(sA[:], sA[:], 1.0, -1.0,
                                                ALU.min, ALU.add),
               waits=(("act", xb_acts[h]),))
        m.emit("vector",
               lambda e, h=h, sA=sA, sB=sB: e.scalar_tensor_tensor(
                   qfm_ch(h), sB[:], 0.0, sA[:], ALU.max, ALU.add),
               incs=(("dve", 1),))
        elu_dve[h] = m.c["dve"]

    def emit_vproj(h):
        pi = (2 * h + 1) % 3
        tgt = pairs[pi]
        w = [wa_wait(2 * h + 1), (pfree[pi], pget(pi))]
        for cd in range(NCH):
            for half in range(2):
                m.mm(tgt[:, half * 512:(half + 1) * 512],
                     wa[(2 * h + 1) % 4][:, cd * 128:(cd + 1) * 128],
                     xln[:, cd * T + half * 512: cd * T + (half + 1) * 512],
                     start=(cd == 0), stop=(cd == NCH - 1),
                     waits=w if (cd == 0 and half == 0) else ())
        m.emit("tensor", lambda e: e.nop(), incs=(("pe", 1), ("peW", 1)))
        v_pe = m.c["pe"]
        m.emit("scalar",
               lambda e, h=h, tg=tgt: e.activation(vpan[:], tg[:], AF.Silu,
                                                   bias=bv_c[:, h:h + 1]),
               waits=(("pe", v_pe),), incs=(("act", 1), (pfree[pi], 1)))
        pinc(pi)
        silu_acts[h] = m.c["act"]

    def emit_qT(h):
        pi = (2 * h) % 3
        tgt = pairs[pi]
        w = [("dve", elu_dve[h]), (pfree[pi], pget(pi))]
        for s in range(8):
            m.mm(tgt[:, s * 128:(s + 1) * 128],
                 q_fm[:, h * T + s * 128: h * T + (s + 1) * 128],
                 ident[:], True, True, waits=w if s == 0 else ())
        m.emit("tensor", lambda e: e.nop(), incs=(("pe", 1),))
        qt_pe[h] = m.c["pe"]
        aw = [("pe", qt_pe[h])]
        if h > 0:
            aw.append(("pe", kv_pe[h - 1]))
        m.emit("vector", lambda e, tg=tgt: e.tensor_copy(qtok[:], tg[:]),
               waits=aw, incs=(("dve", 1), (pfree[pi], 1)))
        pinc(pi)
        qtok_dve[h] = m.c["dve"]

    def emit_vT(h):
        pi = (2 * h + 1) % 3
        tgt = pairs[pi]
        w = [("act", silu_acts[h]), (pfree[pi], pget(pi))]
        for s in range(8):
            m.mm(tgt[:, s * 128:(s + 1) * 128],
                 vpan[:, s * 128:(s + 1) * 128],
                 ident[:], True, True, waits=w if s == 0 else ())
        m.emit("tensor", lambda e: e.nop(), incs=(("pe", 1),))
        vt_pe[h] = m.c["pe"]
        aw = [("pe", vt_pe[h])]
        if h > 0:
            aw.append(("pe", kv_pe[h - 1]))
        m.emit("vector", lambda e, tg=tgt: e.tensor_copy(vtok[:], tg[:]),
               waits=aw, incs=(("dve", 1), (pfree[pi], 1)))
        pinc(pi)
        vtok_dve[h] = m.c["dve"]

    def emit_kv(h):
        w = [("dve", vtok_dve[h])]
        if h >= 2:
            w.append(("dve", kvev_dve[h - 2]))
        for s in range(8):
            m.mm(kvacc(h),
                 qtok[:, s * 128:(s + 1) * 128],
                 vtok[:, s * 128:(s + 1) * 128],
                 start=(s == 0), stop=(s == 7),
                 waits=w if s == 0 else ())
        m.emit("tensor", lambda e: e.nop(), incs=(("pe", 1),))
        kv_pe[h] = m.c["pe"]
        m.emit("vector",
               lambda e, h=h: e.tensor_copy(kv_f[:, h * 128:(h + 1) * 128],
                                            kvacc(h)),
               waits=(("pe", kv_pe[h]),), incs=(("dve", 1),))
        kvev_dve[h] = m.c["dve"]

    for h in range(H):
        if h > 0:
            emit_qT(h - 1)
        emit_qkproj(h)
        if h > 0:
            emit_vT(h - 1)
        emit_elu_dve(h)
        emit_vproj(h)
        if h > 0:
            emit_kv(h - 1)
        if h < 12:
            emit_gamma_pair(2 * h)
            emit_gamma_pair(2 * h + 1)
    emit_qT(H - 1)
    emit_vT(H - 1)
    emit_kv(H - 1)

    ck[2] = m.snapshot()
    m.dma("scalar", gcc_in, gcols[:], "sp", waits=(("dve", gred_dve[-1]),))
    kvevac_dve = kvev_dve[H - 1]
    m.dma("scalar", kv_in, kv_f[:], "sp", waits=(("dve", kvevac_dve),))
    kvput = m.c["sp"]
    if debug:
        m.dma("sync", dbg_kvp, kv_f[:], "dbg", waits=(("dve", kvevac_dve),))
    m.emit("gpsimd",
           lambda e: e.collective_compute(
               "AllReduce", ALU.add,
               replica_groups=[[0, 1, 2, 3, 4, 5, 6, 7]],
               ins=[gcc_in.opt()], outs=[gcc_out.opt()]),
           waits=(("sp", kvput),), incs=(("cc", 1),), raw=True)
    m.emit("gpsimd",
           lambda e: e.collective_compute(
               "AllReduce", ALU.add,
               replica_groups=[[0, 1], [2, 3], [4, 5], [6, 7]],
               ins=[kv_in.opt()], outs=[kv_out.opt()]),
           waits=(("sp", kvput),), incs=(("cc", 1),), raw=True)
    for idx in range(32, 48):
        emit_wa(idx)
    u_act = []
    for p in range(NCH):
        pi = p % 3
        tgt, fs, fw = pairs[pi], pfree[pi], pget(pi)
        w = [wa_wait(32 + p), (fs, fw)]
        for cd in range(NCH):
            for half in range(2):
                m.mm(tgt[:, half * 512:(half + 1) * 512],
                     wa[(32 + p) % 4][:, cd * 128:(cd + 1) * 128],
                     xln[:, cd * T + half * 512: cd * T + (half + 1) * 512],
                     start=(cd == 0), stop=(cd == NCH - 1),
                     waits=w if (cd == 0 and half == 0) else ())
        m.emit("tensor", lambda e: e.nop(), incs=(("pe", 1), ("peW", 1)))
        up_pe = m.c["pe"]
        m.emit("scalar",
               lambda e, p=p, tg=tgt: e.activation(ufm_ch(p), tg[:], AF.Silu,
                                                   bias=bu_c[:, p:p + 1]),
               waits=(("pe", up_pe),), incs=(("act", 1), (fs, 1)))
        pinc(pi)
        u_act.append(m.c["act"])

    m.dma("scalar", kv_f[:], kv_out, "rlk", waits=(("cc", 2),))
    m.dma("scalar", gcols[:], gcc_out, "rlk", waits=(("cc", 2),))
    rlk_done = m.c["rlk"]
    m.emit("vector", lambda e: e.tensor_copy(kv_b[:], kv_f[:]),
           waits=(("rlk", rlk_done),), incs=(("dve", 1),))
    kvb_dve = m.c["dve"]
    if debug:
        m.dma("sync", dbg_kvf, kv_f[:], "dbg", waits=(("rlk", rlk_done),))

    m.mm(pk[0:1, 0:24], onescol_f[:], gcols[:], True, True,
         waits=(("rlk", rlk_done), ("dve", kvevac_dve)))
    m.emit("tensor", lambda e: e.nop(), incs=(("pe", 1),))
    gsum_pe = m.c["pe"]
    m.emit("vector", lambda e: e.tensor_copy(trow[0:1, 0:24], pk[0:1, 0:24]),
           waits=(("pe", gsum_pe),))
    m.emit("vector", lambda e: e.drain())
    for mtx in range(3):
        m.emit("vector",
               lambda e, j=mtx: e.reduce_sum(trow[0:1, 24 + j:25 + j],
                                             trow[0:1, j * 8:(j + 1) * 8], AX))
        m.emit("vector", lambda e: e.drain())
        m.emit("vector",
               lambda e, j=mtx: e.tensor_scalar(
                   trow[0:1, 28 + j:29 + j], trow[0:1, 24 + j:25 + j],
                   0.5 / LFULL, 0.5 * QEPS, ALU.mult, ALU.add))
        m.emit("vector", lambda e: e.drain())
    m.emit("vector", lambda e: e.memset(trow[0:1, 31:32], 0.0),
           incs=(("dve", 1),))
    thr_dve = m.c["dve"]
    m.mm(pk[:, 0:4], ones128f[:], trow[0:1, 28:32], True, True,
         waits=(("dve", thr_dve),))
    m.emit("tensor", lambda e: e.nop(), incs=(("pe", 1),))
    thrb_pe = m.c["pe"]
    m.emit("vector", lambda e: e.tensor_copy(thrc[:], pk[:, 0:4]),
           waits=(("pe", thrb_pe),))
    m.emit("vector", lambda e: e.drain())
    m.emit("vector", lambda e: e.tensor_scalar_mul(nthrc[:], thrc[:], -1.0),
           incs=(("dve", 1),))
    thr_ready = m.c["dve"]
    if debug:
        m.dma("sync", dbg_thr, thrc[:], "dbg", waits=(("dve", thr_ready),))
        m.dma("sync", dbg_gc, gcols[:], "dbg", waits=(("dve", thr_ready),))
        m.dma("sync", dbg_trow, trow[:], "dbg", waits=(("dve", thr_ready),))

    ck[3] = m.snapshot()
    for idx in range(48, 64):
        emit_wa(idx)
    es_xln.close()
    ao = es_a2.enter_context(nc.sbuf_tensor("ao", [128, NCH * T], BF16, **R))
    ao_ch = lambda c: ao[:, c * T:(c + 1) * T]

    if debug:
        m.dma("sync", dbg_qfm, q_fm[:], "dbg", waits=(("dve", kvb_dve),))
    sq2_pe = {}
    for h in range(H):
        if h % 2 == 0:
            tgt, fs, fw = pa, "fa", m.paw
        else:
            tgt, fs, fw = pb, "fb", m.pbw
        w = [("dve", kvb_dve), (fs, fw)]
        for half in range(2):
            m.mm(tgt[:, half * 512:(half + 1) * 512],
                 kv_b[:, h * 128:(h + 1) * 128],
                 q_fm[:, h * T + half * 512: h * T + (half + 1) * 512],
                 True, True, waits=w if half == 0 else ())
        m.emit("tensor", lambda e: e.nop(), incs=(("pe", 1),))
        e_pe = m.c["pe"]
        m.emit("vector",
               lambda e, h=h, tg=tgt: e.tensor_copy(ao_ch(h), tg[:]),
               waits=(("pe", e_pe),), incs=((fs, 1),))
        if h % 2 == 0:
            m.paw += 1
        else:
            m.pbw += 1
        m.emit("vector",
               lambda e, h=h: e.tensor_mul(sqb[h % 2][:], ao_ch(h), ao_ch(h)),
               waits=[("pe", sq2_pe[h - 2])] if h >= 2 else (),
               incs=(("dve", 1),))
        ein_dve = m.c["dve"]
        stw = [("dve", ein_dve)]
        if h == 0:
            stw.append(("dve", thr_ready))
            stw.append(("fkb", m.pkbw))
        for half in range(2):
            m.mm(pk[:, half * 512:(half + 1) * 512], colones[:],
                 ao[:, h * T + half * 512: h * T + (half + 1) * 512],
                 start=(h == 0), stop=(h == H - 1),
                 waits=stw if half == 0 else ())
            m.mm(pk[:, 1024 + half * 512:1024 + (half + 1) * 512],
                 colones[:], sqb[h % 2][:, half * 512:(half + 1) * 512],
                 start=(h == 0), stop=(h == H - 1))
        m.emit("tensor", lambda e: e.nop(), incs=(("pe", 1),))
        sq2_pe[h] = m.c["pe"]
    stats2_pe = m.c["pe"]

    if debug:
        m.dma("sync", dbg_ao, ao[:], "dbg", waits=(("pe", stats2_pe),))
    rows2 = ln_rows(stats2_pe)
    bc2 = ln_bcast(rows2)

    z_dve = []
    for c in range(NCH):
        m.emit("vector",
               lambda e, c=c: e.tensor_tensor(s0b, ao_ch(c), mbb[:],
                                              ALU.subtract),
               waits=(("act", u_act[c]),))
        m.emit("vector",
               lambda e, c=c: e.tensor_tensor(s1b, s0b, rbb[:], ALU.mult))
        m.emit("vector",
               lambda e, c=c: e.tensor_scalar(s1b, s1b, g_c[:, c:c + 1],
                                              bln_c[:, c:c + 1], ALU.mult,
                                              ALU.add))
        m.emit("vector",
               lambda e, c=c: e.tensor_tensor(ao_ch(c), s1b, ufm_ch(c),
                                              ALU.mult),
               incs=(("dve", 1),))
        z_dve.append(m.c["dve"])

    wo_act = []
    wo_sq_pe = {}
    for p in range(NCH):
        if p % 2 == 0:
            tgt, fs, fw = pa, "fa", m.paw
        else:
            tgt, fs, fw = pb, "fb", m.pbw
        w0 = [wa_wait(48 + p), (fs, fw)]
        for cd in range(NCH):
            cw = list(w0) if cd == 0 else []
            if p == 0:
                cw.append(("dve", z_dve[cd]))
            elif p == 1 and cd == 0:
                cw.append(("dve", z_dve[-1]))
            for half in range(2):
                m.mm(tgt[:, half * 512:(half + 1) * 512],
                     wa[(48 + p) % 4][:, cd * 128:(cd + 1) * 128],
                     ao[:, cd * T + half * 512: cd * T + (half + 1) * 512],
                     start=(cd == 0), stop=(cd == NCH - 1),
                     waits=cw if half == 0 else ())
        m.emit("tensor", lambda e: e.nop(), incs=(("pe", 1), ("peW", 1)))
        wo_pe = m.c["pe"]
        m.emit("vector",
               lambda e, p=p, tg=tgt: e.tensor_scalar(
                   ufm_ch(p), tg[:], bo_c[:, p:p + 1], None, ALU.add),
               waits=(("pe", wo_pe),), incs=((fs, 1),))
        if p % 2 == 0:
            m.paw += 1
        else:
            m.pbw += 1
        m.emit("vector",
               lambda e, p=p: e.tensor_mul(sqb[p % 2][:], ufm_ch(p),
                                           ufm_ch(p)),
               waits=[("pe", wo_sq_pe[p - 2])] if p >= 2 else (),
               incs=(("dve", 1),))
        wo_act.append(m.c["dve"])
        stw = [("dve", wo_act[p])]
        for half in range(2):
            m.mm(pk[:, half * 512:(half + 1) * 512], colones[:],
                 u_fm[:, p * T + half * 512: p * T + (half + 1) * 512],
                 start=(p == 0), stop=(p == NCH - 1),
                 waits=stw if half == 0 else ())
            m.mm(pk[:, 1024 + half * 512:1024 + (half + 1) * 512],
                 colones[:], sqb[p % 2][:, half * 512:(half + 1) * 512],
                 start=(p == 0), stop=(p == NCH - 1))
        m.emit("tensor", lambda e: e.nop(), incs=(("pe", 1),))
        wo_sq_pe[p] = m.c["pe"]
    stats3_pe = m.c["pe"]

    rows3 = ln_rows(stats3_pe)
    bc3 = ln_bcast(rows3)

    ck[4] = m.snapshot()
    es_a2.close()
    es_g.close()
    es_xln3 = contextlib.ExitStack()
    xln3 = es_xln3.enter_context(nc.sbuf_tensor("xln3", [128, NCH * T], BF16,
                                                **L))
    xln3_ch = lambda c: xln3[:, c * T:(c + 1) * T]

    x2_sp = []
    c2_dve = []
    sq3_pe = []
    cast_acts = []
    for i in range(2):
        m.dma("sync", xin[i][:], x_t[i * 128:(i + 1) * 128, :], f"rlx{i}")
    for c in range(NCH):
        if c >= 2:
            m.dma("sync", xin[c % 2][:], x_t[c * 128:(c + 1) * 128, :],
                  f"rlx{c % 2}", waits=(("dve", c2_dve[c - 2]),))
        w = [(f"rlx{c % 2}", m.c[f"rlx{c % 2}"])]
        if c >= 2:
            w.append(("sp", x2_sp[c - 2]))
            w.append(("act", cast_acts[c - 2]))
        m.emit("vector",
               lambda e, c=c: e.tensor_tensor(s0b, ufm_ch(c), mbb[:],
                                              ALU.subtract), waits=w)
        m.emit("vector",
               lambda e, c=c: e.tensor_tensor(s0b, s0b, rbb[:], ALU.mult))
        m.emit("vector",
               lambda e, c=c: e.tensor_tensor(x2r[c % 2][:], xin[c % 2][:],
                                              s0b, ALU.add))
        m.emit("vector", lambda e: e.engine_nop(), incs=(("dve", 1),))
        c2_dve.append(m.c["dve"])
        m.emit("scalar",
               lambda e, c=c: e.copy(xln3_ch(c), x2r[c % 2][:]),
               waits=(("dve", c2_dve[c]),), incs=(("act", 1),))
        cast_acts.append(m.c["act"])
        m.dma("scalar", x2_d[c * 128:(c + 1) * 128, :], x2r[c % 2][:], "sp",
              waits=(("dve", c2_dve[c]),))
        x2_sp.append(m.c["sp"])
        aw = []
        if c >= 2:
            aw.append(("pe", sq3_pe[c - 2]))
        m.emit("scalar",
               lambda e, c=c: e.activation(sqb[c % 2][:], xln3_ch(c),
                                           AF.Square),
               waits=aw, incs=(("act", 1),))
        sq3_act = m.c["act"]
        stw = [("act", sq3_act)]
        for half in range(2):
            m.mm(pk[:, half * 512:(half + 1) * 512], colones[:],
                 xln3[:, c * T + half * 512: c * T + (half + 1) * 512],
                 start=(c == 0), stop=(c == NCH - 1),
                 waits=stw if half == 0 else ())
            m.mm(pk[:, 1024 + half * 512:1024 + (half + 1) * 512],
                 colones[:], sqb[c % 2][:, half * 512:(half + 1) * 512],
                 start=(c == 0), stop=(c == NCH - 1))
        m.emit("tensor", lambda e: e.nop(), incs=(("pe", 1),))
        sq3_pe.append(m.c["pe"])
    stats4_pe = m.c["pe"]

    ck[5] = m.snapshot()
    es_u.close()
    hidden = es.enter_context(nc.sbuf_tensor("hidden", [128, FCH * T], BF16,
                                             side="right"))
    es_ffn = contextlib.ExitStack()
    wf = [es_ffn.enter_context(nc.sbuf_tensor(f"wf{i}", [128, D], F32, **R))
          for i in range(2)]
    qp = [es_ffn.enter_context(nc.sbuf_tensor(f"qp{i}", [128, D], BF16, **R))
          for i in range(4)]
    hid_ch = lambda c: hidden[:, c * T:(c + 1) * T]

    q_dve = {}
    qmm_pe = {}
    hid_dve = []

    def emit_quant(ci):
        for which in range(2):
            widx = 2 * ci + which
            if widx == 0:
                w = [("pe", stats2_pe)]
            elif widx == 1:
                w = [("pe", stats3_pe)]
            else:
                w = []
            if widx >= 2:
                w.append(("dve", q_dve[widx - 2]))
            m.dma("sync", wf[widx % 2][:],
                  (w1t_d if which == 0 else w2t_d)[ci, :, :],
                  f"wf{widx % 2}", waits=w)
            aw = [(f"wf{widx % 2}", 16 * (widx // 2 + 1))]
            if widx == 0:
                aw.append(("dve", thr_ready))
            if widx >= 4:
                aw.append(("pe", qmm_pe.get(widx - 4, 0)))
            m.emit("vector",
                   lambda e, i=widx, wh=which: e.tensor_scalar(
                       qp[i % 4][:], wf[i % 2][:], thrc[:, wh:wh + 1], None,
                       ALU.is_ge),
                   waits=aw)
            m.emit("vector",
                   lambda e, i=widx, wh=which: e.tensor_scalar(
                       scr1[:].bitcast(BF16), wf[i % 2][:],
                       nthrc[:, wh:wh + 1], None, ALU.is_le))
            m.emit("vector",
                   lambda e, i=widx: e.tensor_tensor(
                       qp[i % 4][:], qp[i % 4][:], scr1[:].bitcast(BF16),
                       ALU.subtract),
                   incs=(("dve", 1),))
            q_dve[widx] = m.c["dve"]

    emit_quant(0)
    emit_quant(1)

    rows4 = ln_rows(stats4_pe)
    bc4 = ln_bcast(rows4)
    ln3_dve = []
    for c in range(NCH):
        m.emit("vector",
               lambda e, c=c: e.tensor_tensor(s0b, xln3_ch(c), mbb[:],
                                              ALU.subtract))
        m.emit("vector",
               lambda e, c=c: e.tensor_tensor(xln3_ch(c), s0b, rbb[:],
                                              ALU.mult),
               incs=(("dve", 1),))
        ln3_dve.append(m.c["dve"])
    ln3_done = ln3_dve[-1]

    for ci in range(FCH):
        if 1 <= ci + 1 < FCH:
            emit_quant(ci + 1)

        if ci % 2 == 0:
            tg1, f1, f1w = pa, "fa", m.paw
            tg2, f2, f2w = pb, "fb", m.pbw
        else:
            tg1, f1, f1w = pk[:, 0:1024], "fka", m.pkaw
            tg2, f2, f2w = pk[:, 1024:2048], "fkb", m.pkbw
        for which, tg, fs, fw in ((0, tg1, f1, f1w), (1, tg2, f2, f2w)):
            w = [("dve", q_dve[2 * ci + which]), (fs, fw)]
            for cd in range(NCH):
                for half in range(2):
                    cw = list(w) if (cd == 0 and half == 0) else []
                    if ci == 0 and which == 0 and half == 0:
                        cw.append(("dve", ln3_dve[cd]))
                    m.mm(tg[:, half * 512:(half + 1) * 512],
                         qp[(2 * ci + which) % 4][:, cd * 128:(cd + 1) * 128],
                         xln3[:, cd * T + half * 512: cd * T + (half + 1) * 512],
                         start=(cd == 0), stop=(cd == NCH - 1),
                         waits=cw)
            m.emit("tensor", lambda e: e.nop(), incs=(("pe", 1),))
            qmm_pe[2 * ci + which] = m.c["pe"]
        aw = [("pe", qmm_pe[2 * ci + 1])]
        if ci >= 1:
            aw.append(("dve", hid_dve[ci - 1]))
        m.emit("scalar",
               lambda e, tg=tg1: e.activation(scr0[:], tg[:], AF.Sigmoid),
               waits=aw, incs=(("act", 1), (f1, 1)))
        if f1 == "fa":
            m.paw += 1
        else:
            m.pkaw += 1
        sig_act = m.c["act"]
        m.emit("vector",
               lambda e, ci=ci, tg=tg2: e.tensor_tensor(hid_ch(ci), scr0[:],
                                                        tg[:], ALU.mult),
               waits=(("act", sig_act),), incs=(("dve", 1), (f2, 1)))
        if f2 == "fb":
            m.pbw += 1
        else:
            m.pkbw += 1
        hid_dve.append(m.c["dve"])
    last_x2mm_pe = qmm_pe[2 * FCH - 1]

    ck[6] = m.snapshot()
    es_ffn.close()
    es_xln3.close()
    w3f = es_ffn.enter_context(nc.sbuf_tensor("w3f", [128, FF], F32, **R))
    q3p = [es_ffn.enter_context(nc.sbuf_tensor(f"q3p{i}", [128, FF], BF16, **R))
           for i in range(2)]
    q3tmp = es_ffn.enter_context(nc.sbuf_tensor("q3tmp", [128, FF], BF16, **R))

    o3_sp = []
    q3_dve = {}
    o3_pe = {}
    o3sq_pe = {}
    o3_act = []
    for co in range(NCH):
        w = [("pe", last_x2mm_pe)] if co == 0 else [("dve", q3_dve[co - 1])]
        m.dma("sync", w3f[:], w3t_d[co, :, :], "w3", waits=w)
        aw = [("w3", 16 * (co + 1))]
        if co >= 2:
            aw.append(("pe", o3_pe[co - 2]))
        m.emit("vector",
               lambda e, co=co: e.tensor_scalar(
                   q3p[co % 2][:], w3f[:], thrc[:, 2:3], None, ALU.is_ge),
               waits=aw)
        m.emit("vector",
               lambda e: e.tensor_scalar(q3tmp[:], w3f[:], nthrc[:, 2:3],
                                         None, ALU.is_le))
        m.emit("vector",
               lambda e, co=co: e.tensor_tensor(
                   q3p[co % 2][:], q3p[co % 2][:], q3tmp[:], ALU.subtract),
               incs=(("dve", 1),))
        q3_dve[co] = m.c["dve"]

        if co % 2 == 0:
            tgt, fs, fw = pa, "fa", m.paw
        else:
            tgt, fs, fw = pb, "fb", m.pbw
        w0 = [("dve", q3_dve[co]), (fs, fw)]
        for cj in range(FCH):
            cw = list(w0) if cj == 0 else []
            if co == 0:
                cw.append(("dve", hid_dve[cj]))
            for half in range(2):
                m.mm(tgt[:, half * 512:(half + 1) * 512],
                     q3p[co % 2][:, cj * 128:(cj + 1) * 128],
                     hidden[:, cj * T + half * 512: cj * T + (half + 1) * 512],
                     start=(cj == 0), stop=(cj == FCH - 1),
                     waits=cw if half == 0 else ())
        m.emit("tensor", lambda e: e.nop(), incs=(("pe", 1),))
        o3_pe[co] = m.c["pe"]
        aw = [("pe", o3_pe[co])]
        if co >= 1:
            aw.append(("sp", o3_sp[co - 1]))
        m.emit("scalar",
               lambda e, co=co, tg=tgt: e.activation(arb2[co % 2][:], tg[:],
                                                     AF.Identity,
                                                     bias=zcol[:]),
               waits=aw, incs=((fs, 1),))
        if co % 2 == 0:
            m.paw += 1
        else:
            m.pbw += 1
        m.emit("scalar",
               lambda e, co=co: e.activation(sqb[co % 2][:], arb2[co % 2][:],
                                             AF.Square),
               waits=[("pe", o3sq_pe[co - 2])] if co >= 2 else (),
               incs=(("act", 1),))
        o3_act.append(m.c["act"])
        m.dma("scalar", o3_d[co * 128:(co + 1) * 128, :], arb2[co % 2][:],
              "sp", waits=(("act", o3_act[co]),))
        o3_sp.append(m.c["sp"])
        stw = [("act", o3_act[co])]
        for half in range(2):
            m.mm(pk[:, half * 512:(half + 1) * 512], colones[:],
                 arb2[co % 2][:, half * 512:(half + 1) * 512],
                 start=(co == 0), stop=(co == NCH - 1),
                 waits=stw if half == 0 else ())
            m.mm(pk[:, 1024 + half * 512:1024 + (half + 1) * 512],
                 colones[:], sqb[co % 2][:, half * 512:(half + 1) * 512],
                 start=(co == 0), stop=(co == NCH - 1))
        m.emit("tensor", lambda e: e.nop(), incs=(("pe", 1),))
        o3sq_pe[co] = m.c["pe"]
    stats5_pe = m.c["pe"]

    o3r = [es.enter_context(nc.sbuf_tensor(f"o3r{i}", [128, T], BF16, **L))
           for i in range(4)]
    for c in range(2):
        m.dma("scalar", x2r[c][:], x2_d[c * 128:(c + 1) * 128, :],
              f"rlx{c}", waits=(("sp", x2_sp[-1]), ("act", cast_acts[-1])))
    for c in range(4):
        m.dma("scalar", o3r[c][:], o3_d[c * 128:(c + 1) * 128, :],
              f"rla{c % 2}", waits=(("sp", o3_sp[c]),))

    rows5 = ln_rows(stats5_pe)
    bc5 = ln_bcast(rows5)

    ck[7] = m.snapshot()
    es_ffn.close()
    outb = [es_ffn.enter_context(nc.sbuf_tensor(f"outb{i}", [128, T], F32,
                                                **R))
            for i in range(4)]
    x2rl = [x2r[0], x2r[1]] +            [es_ffn.enter_context(nc.sbuf_tensor(f"x2r{i}", [128, T], F32,
                                                **R)) for i in (2, 3)]
    for c in (2, 3):
        m.dma("scalar", x2rl[c][:], x2_d[c * 128:(c + 1) * 128, :],
              f"rlx{c % 2}", waits=(("pe", stats5_pe),))
    fin_dve = []
    for c in range(NCH):
        if c >= 4:
            m.dma("scalar", x2rl[c % 4][:], x2_d[c * 128:(c + 1) * 128, :],
                  f"rlx{c % 2}", waits=(("dve", fin_dve[c - 4]),))
            m.dma("scalar", o3r[c % 4][:], o3_d[c * 128:(c + 1) * 128, :],
                  f"rla{c % 2}", waits=(("dve", fin_dve[c - 4]),))
        w = [(f"rlx{c % 2}", m.c[f"rlx{c % 2}"]),
             (f"rla{c % 2}", m.c[f"rla{c % 2}"])]
        if c >= 4:
            w.append(("st", 16 * (c - 3)))
        m.emit("vector",
               lambda e, c=c: e.tensor_tensor(s0b, o3r[c % 4][:], mbb[:],
                                              ALU.subtract), waits=w)
        m.emit("vector",
               lambda e, c=c: e.tensor_tensor(s0b, s0b, rbb[:], ALU.mult))
        m.emit("vector",
               lambda e, c=c: e.tensor_tensor(outb[c % 4][:], x2rl[c % 4][:],
                                              s0b, ALU.add),
               incs=(("dve", 1),))
        fin_dve.append(m.c["dve"])
        m.dma("sync", out_t[c * 128:(c + 1) * 128, :], outb[c % 4][:], "st",
              waits=(("dve", fin_dve[c]),))
    fw = [("st", 16 * NCH)]
    if debug:
        fw.append(("dbg", m.c["dbg"]))
    m.emit("gpsimd", lambda e: e.nop(), waits=fw)
    if phases < 99:
        m.truncate(ck[phases])
        m.c["st"] = 0
        for c in range(NCH):
            m.dma("sync", out_t[c * 128:(c + 1) * 128, :], mb_f[:], "st")
        def _final(e):
            e.wait_ge(m.sems["st"], 16 * NCH)
            return e.nop()
        m.ops["sync"].append(lambda e: _final(e))
        m.meta["sync"].append((((("st", 16 * NCH)),), (), "final"))
    build_program.last_mk = m

    with nc.Block() as block:
        @block.sync
        def _(e):
            for f in m.ops["sync"]:
                f(e)

        @block.scalar
        def _(e):
            for f in m.ops["scalar"]:
                f(e)

        @block.vector
        def _(e):
            for f in m.ops["vector"]:
                f(e)

        @block.tensor
        def _(e):
            for f in m.ops["tensor"]:
                f(e)

        @block.gpsimd
        def _(e):
            for f in m.ops["gpsimd"]:
                f(e)

    es_ffn.close()
    es_xln3.close()
    es_a2.close()
    m.es.close()
    es_a.close()
    return nc


_CACHE = {}


def _stage(inputs):
    x = np.ascontiguousarray(np.asarray(inputs["L0_attn_x"], np.float32))
    Wqk = np.asarray(inputs["L0_attn_Wqk"], np.float32)
    Wv = np.asarray(inputs["L0_attn_Wv"], np.float32)
    Wu = np.asarray(inputs["L0_attn_Wu"], np.float32)
    Wo = np.asarray(inputs["L0_attn_Wo"], np.float32)
    w1 = np.asarray(inputs["L0_ffn_w1"], np.float32)
    w2 = np.asarray(inputs["L0_ffn_w2"], np.float32)
    w3 = np.asarray(inputs["L0_ffn_w3"], np.float32)

    def attn_w(W):
        return np.ascontiguousarray(
            W.reshape(NCH, 128, NCH, 128).transpose(2, 1, 0, 3)
        ).reshape(NCH, 128, D)

    def ffn_w12(W):
        return np.ascontiguousarray(
            W.reshape(FCH, 128, NCH, 128).transpose(0, 3, 2, 1)
        ).reshape(FCH, 128, D)

    def ffn_w3(W):
        return np.ascontiguousarray(
            W.reshape(NCH, 128, FCH, 128).transpose(0, 3, 2, 1)
        ).reshape(NCH, 128, FF)

    def col(v):
        return np.ascontiguousarray(
            np.asarray(v, np.float32).reshape(NCH, 128).T)

    shared = {
        "wqk": attn_w(Wqk), "wv": attn_w(Wv), "wu": attn_w(Wu),
        "wo": attn_w(Wo),
        "bqk": col(inputs["L0_attn_bqk"]), "bv": col(inputs["L0_attn_bv"]),
        "bu": col(inputs["L0_attn_bu"]), "bo": col(inputs["L0_attn_bo"]),
        "lng": col(inputs["L0_attn_ln_g"]), "lnb": col(inputs["L0_attn_ln_b"]),
        "w1t": ffn_w12(w1), "w2t": ffn_w12(w2), "w3t": ffn_w3(w3),
    }
    flats = (shared["w1t"].reshape(-1), shared["w2t"].reshape(-1),
             shared["w3t"].reshape(-1))
    in_maps = []
    for core in range(8):
        b, half = divmod(core, 2)
        d = dict(shared)
        d["x_t"] = np.ascontiguousarray(x[b, half * T:(half + 1) * T, :].T)
        for j, name in enumerate(("gs1", "gs2", "gs3")):
            d[name] = np.ascontiguousarray(
                flats[j][core * L8:(core + 1) * L8].reshape(128, GW))
        in_maps.append(d)
    return in_maps


def kernel(**inputs):
    in_maps = _stage(inputs)
    if "nc" not in _CACHE:
        _CACHE["nc"] = build_program()
    nc = _CACHE["nc"]
    res = run_bass_kernel_spmd(nc, in_maps, core_ids=list(range(8)))
    out = np.empty((B, SEQ, D), np.float32)
    for core in range(8):
        b, half = divmod(core, 2)
        out[b, half * T:(half + 1) * T, :] = res.results[core]["out_t"].T
    return out
